# version 01
# baseline (speedup 1.0000x reference)
import numpy as np
import ml_dtypes
import concourse.bass as bass
import concourse.mybir as mybir
from concourse.bass_utils import run_bass_kernel_spmd
from concourse.library_config import mlp as _mlplib
from contextlib import ExitStack

F32 = mybir.dt.float32
BF16 = mybir.dt.bfloat16
I16 = mybir.dt.int16
U8 = mybir.dt.uint8
AF = mybir.ActivationFunctionType
ALU = mybir.AluOpType
AX = mybir.AxisListType
_DETECT_RACES = True

NCORE = 8
B, S, E = 8, 2048, 1024
H, DH = 16, 64
NE, FF, CAP = 8, 4096, 2560
T = B * S
TOK = S
NT = TOK // 128
NP = T // 128
TRASH = CAP
TBL_ROWS = CAP + 256
CW = CAP // 16
EPS = 1e-5
SCH = 512
NSC = CAP // SCH
W1PF = 256
NW1P = FF // W1PF
NKT = [4, 8, 12, 16]
CUMKT = [4, 12, 24, 40]

_TPERM = [4 * (j % 8) + j // 8 for j in range(32)]
TWALK = []
_seen = [False] * 32
for _s0 in range(32):
    _c = _s0
    while not _seen[_c]:
        _seen[_c] = True
        TWALK.append(_c)
        _c = _TPERM[_c]


def _host_pack(inputs):
    x = np.asarray(inputs["L0_attn_x"])
    wqkv = np.asarray(inputs["L0_attn_w_qkv"])
    wout = np.asarray(inputs["L0_attn_w_out"])
    gw = np.asarray(inputs["L0_ffn_gate_w"])
    gb = np.asarray(inputs["L0_ffn_gate_b"])
    w1 = np.asarray(inputs["L0_ffn_w1"])
    b1 = np.asarray(inputs["L0_ffn_b1"])
    w2 = np.asarray(inputs["L0_ffn_w2"])
    b2 = np.asarray(inputs["L0_ffn_b2"])

    wq = np.stack([np.concatenate([wqkv[:, 2 * p, 0:64], wqkv[:, 2 * p + 1, 0:64]], 1)
                   for p in range(8)]).astype(np.float32)
    wk = np.stack([np.concatenate([wqkv[:, 2 * p, 64:128], wqkv[:, 2 * p + 1, 64:128]], 1)
                   for p in range(8)]).astype(np.float32)
    wv = np.stack([np.concatenate([wqkv[:, 8 * g + h, 128:192] for h in range(8)], 1)
                   for g in range(2)]).astype(np.float32)
    wo = np.ascontiguousarray(wout.reshape(H * DH, E)).astype(np.float32)

    gwT = np.ascontiguousarray(gw.T.reshape(8, 128, 8).transpose(1, 0, 2)).astype(np.float32)
    gb_dev = np.tile(np.asarray(gb)[None, :], (128, 1)).astype(np.float32)
    b1col = np.ascontiguousarray(b1.reshape(NE, 32, 128).transpose(0, 2, 1)).astype(np.float32)
    b2rep = np.tile(np.asarray(b2)[:, None, :], (1, 128, 1)).astype(np.float32)

    ident_bf = np.eye(128).astype(ml_dtypes.bfloat16)
    ident_f32 = np.eye(128).astype(np.float32)
    trils = np.triu(np.ones((128, 128)), 1).astype(ml_dtypes.bfloat16)
    iota8 = np.tile(np.arange(8, dtype=np.float32)[None, :], (128, 1))
    rowiota = (np.arange(16, dtype=np.float32)[:, None] * CW
               + np.arange(CW, dtype=np.float32)[None, :]).astype(np.float32)
    zeros16 = np.zeros((16, CW), np.float32)
    ones128 = np.ones((128, 1), dtype=ml_dtypes.bfloat16)
    th16 = np.tile((160.0 * np.arange(1, 17, dtype=np.float32))[None, :], (128, 1))
    gid128 = (np.arange(128, dtype=np.float32)[:, None]
              + 128.0 * np.arange(128, dtype=np.float32)[None, :])
    zcol = np.zeros((128, 128), np.float32)
    kk = np.arange(128)[:, None]
    qq = np.arange(512)[None, :]
    cmask = np.stack([(qq >= kk + 128 * o) for o in range(4)]).astype(ml_dtypes.bfloat16)

    per_core = []
    for ci in range(NCORE):
        esel = np.zeros((128, 16), np.float32)
        for m in range(16):
            esel[16 * ci + m, m] = 1.0
        per_core.append({
            "x_my": np.ascontiguousarray(x[ci]).astype(np.float32),
            "w1_my": np.ascontiguousarray(w1[ci]).astype(np.float32),
            "w2_my": np.ascontiguousarray(w2[ci]).astype(np.float32),
            "b1_my": b1col[ci], "b2_my": b2rep[ci],
            "gwT": gwT, "gb": gb_dev,
            "wq": wq, "wk": wk, "wv": wv, "wo": wo,
            "ident_bf": ident_bf, "ident_f32": ident_f32, "trils": trils,
            "iota8": iota8, "rowiota": rowiota, "zeros16": zeros16, "zcol": zcol,
            "cmask": np.ascontiguousarray(cmask),
            "gid128": gid128, "ones128": ones128, "th16": th16,
            "mye": np.full((128, 1), float(ci), np.float32),
            "myoh": np.eye(8, dtype=np.float32)[ci:ci + 1].copy(),
            "esel": esel,
        })
    return per_core


def build_program(upto=99, dumps=()):
    nc = bass.Bass("TRN2", target_bir_lowering=False, detect_race_conditions=_DETECT_RACES)
    X = {}

    def ein(name, shape, dt):
        X[name] = nc.dram_tensor(name, shape, dt, kind="ExternalInput").ap()

    for name, shape, dt in [
            ("x_my", [TOK, E], F32), ("w1_my", [E, FF], F32), ("w2_my", [FF, E], F32),
            ("b1_my", [128, 32], F32), ("b2_my", [128, E], F32),
            ("gwT", [128, 8, 8], F32), ("gb", [128, 8], F32),
            ("wq", [8, E, 128], F32), ("wk", [8, E, 128], F32),
            ("wv", [2, E, 512], F32), ("wo", [E, E], F32),
            ("ident_bf", [128, 128], BF16), ("ident_f32", [128, 128], F32),
            ("trils", [128, 128], BF16), ("iota8", [128, 8], F32),
            ("rowiota", [16, CW], F32), ("zeros16", [16, CW], F32),
            ("zcol", [128, 128], F32), ("cmask", [4, 128, 512], BF16),
            ("mye", [128, 1], F32), ("myoh", [1, 8], F32), ("esel", [128, 16], F32),
            ("gid128", [128, 128], F32), ("ones128", [128, 1], BF16),
            ("th16", [128, 16], F32)]:
        ein(name, shape, dt)

    y = nc.dram_tensor("y", [TOK, E], F32, kind="ExternalOutput").ap()

    w1b = nc.dram_tensor("w1b", [E, FF], BF16).ap()
    w2b = nc.dram_tensor("w2b", [FF, E], BF16).ap()
    ln_in = nc.dram_tensor("ln_in", [TOK, E], BF16).ap()
    ln_all = nc.dram_tensor("ln_all", [T, E], BF16).ap()
    lg_in = nc.dram_tensor("lg_in", [TOK, 8], F32).ap()
    lg_all = nc.dram_tensor("lg_all", [T, 8], F32).ap()
    eo_in = nc.dram_tensor("eo_in", [CAP, E], F32).ap()
    eo_all = nc.dram_tensor("eo_all", [NE * CAP, E], F32).ap()
    tbl = nc.dram_tensor("tbl", [TBL_ROWS, 64], F32).ap()
    dscat = nc.dram_tensor("dscat", [16, NP * 8], I16).ap()
    onorm_d = nc.dram_tensor("onorm_d", [8, 128, TOK], BF16).ap()
    down = nc.dram_tensor("down", [16, 128], I16).ap()

    es = ExitStack()
    sem, cnt = {}, {}

    def mksem(*names):
        for n in names:
            sem[n] = es.enter_context(nc.semaphore(n))
            cnt[n] = 0

    def inc(inst, name, by=1):
        cnt[name] += by
        return inst.then_inc(sem[name], by)

    def dinc(inst, name):
        return inc(inst, name, 16)

    REG = {}
    NREG = {}
    _uqc = [0]

    def U(nm):
        _uqc[0] += 1
        return f"{nm}_u{_uqc[0]}"

    with es:
        sb = nc.alloc_sbuf_tensor
        RES = sb("RES", [128, NT, E], F32)
        CN = {}
        for nm, shape, dt in [("ident_bf", [128, 128], BF16), ("ident_f32", [128, 128], F32),
                              ("trils", [128, 128], BF16), ("iota8", [128, 8], F32),
                              ("rowiota", [16, CW], F32), ("zeros16", [16, CW], F32),
                              ("zcol", [128, 128], F32), ("cmask", [128, 4, 512], BF16),
                              ("gid128", [128, 128], F32), ("ones128", [128, 1], BF16),
                              ("th16", [128, 16], F32),
                              ("mye", [128, 1], F32), ("myoh", [1, 8], F32),
                              ("esel", [128, 16], F32), ("gwT", [128, 8, 8], F32),
                              ("gb", [128, 8], F32), ("b1c", [128, 32], F32),
                              ("b2r", [128, E], F32)]:
            CN[nm] = sb("c_" + nm, shape, dt)
        TOTB = sb("TOTB", [128, 8], BF16)
        EXCL = sb("EXCL", [128, 8], F32)
        CNTS = sb("CNTS", [1, 8], F32)
        CNT1 = sb("CNT1", [16, 1], F32)
        CNTsc = sb("CNTsc", [1, 1], F32)
        coefT = sb("coefT", [128, 16], F32)
        ownsl = sb("ownsl", [16, 128], F32)
        idxown = sb("idxown", [128, 128], I16)
        idxg = sb("idxg", [128, CW], I16)
        idg16 = sb("idg16", [16, CW], I16)
        selg = sb("selg", [16, CW], F32)
        maskv = sb("maskv", [16, CW], U8)
        LGT = sb("LGT", [128, NT, 8], F32)
        SQT = [sb(f"SQT{i}", [128, E], F32) for i in range(2)]
        STAT = sb("STAT", [128, NT, 8], F32)
        LNT = [sb(f"LNT{i}", [128, E], F32) for i in range(2)]
        LNB = [sb(f"LNB{i}", [128, E], BF16) for i in range(2)]
        idxsc = sb("idxsc", [128, NP * 8], I16)
        REG.update(lgt=(LGT[:, :, :], [128, NT, 8], F32),
                   cnts=(CNTS[:, :], [1, 8], F32), coeft=(coefT[:, :], [128, 16], F32),
                   ownsl=(ownsl[:, :], [16, 128], F32),
                   idg16=(idg16[:, :], [16, CW], I16),
                   idxown=(idxown[:, :], [128, 128], I16),
                   res0=(RES[:, 0:2, :], [128, 2, E], F32))
        mksem("dx", "dw", "dwa", "dsc", "dg", "cc", "ts", "vs", "ss", "gs",
              "dat", "dout", "lnr0", "lnr1", "w1s0", "w1s1", "w2s0", "w2s1",
              "eots0", "eots1", "eots2", "eots3", "wqs0", "wqs1",
              "ons0", "ons1", "oin0", "oin1")

        x_tm = X["x_my"].rearrange("(a p) e -> p a e", p=128)
        ln_in_tm = ln_in.rearrange("(a p) e -> p a e", p=128)
        lg_in_tm = lg_in.rearrange("(a p) e -> p a e", p=128)
        y_tm = y.rearrange("(a p) e -> p a e", p=128)
        w1b_v = w1b.rearrange("(k p) f -> p k f", p=128)
        wq_v = X["wq"].rearrange("h (k p) f -> h p k f", p=128)
        wk_v = X["wk"].rearrange("h (k p) f -> h p k f", p=128)
        wv_v = X["wv"].rearrange("g (k p) f -> g p k f", p=128)
        wo_v = X["wo"].rearrange("(k p) f -> p k f", p=128)

        with ExitStack() as p0s:
            WCV = p0s.enter_context(nc.sbuf_tensor(U("WCV"), [128, FF], BF16))
            blk = p0s.enter_context(nc.Block())
            _ = blk

            @blk.sync
            def _(sy):
                dinc(sy.dma_start(out=RES[:, :, :], in_=x_tm), "dx")
                for nm in ("ident_bf", "ident_f32", "trils", "iota8", "rowiota",
                           "zeros16", "zcol", "gid128", "ones128", "th16", "mye", "myoh", "esel",
                           "gwT", "gb"):
                    dinc(sy.dma_start(out=CN[nm][...], in_=X[nm][...]), "dx")
                dinc(sy.dma_start(out=CN["cmask"][:, :, :],
                                  in_=X["cmask"].rearrange("o p q -> p o q")), "dx")
                dinc(sy.dma_start(out=CN["b1c"][:, :], in_=X["b1_my"][:, :]), "dx")
                dinc(sy.dma_start(out=CN["b2r"][:, :], in_=X["b2_my"][:, :]), "dx")

            @blk.gpsimd
            def _(g):
                g.load_library(_mlplib)
                for v in (896, 256, 1024, 512):
                    NREG[v] = g.alloc_register(f"nreg{v}")
                    g.reg_mov(NREG[v], v)
                g.wait_ge(sem["dx"], cnt["dx"])
                for zk in range(TBL_ROWS // 256):
                    dinc(g.dma_start(out=tbl[zk * 256:(zk + 1) * 256, :],
                                     in_=CN["zcol"][:, :]), "dw")
                for piece in range(8):
                    dinc(g.dma_start(out=WCV[:, :],
                                     in_=X["w1_my"][piece * 128:(piece + 1) * 128, :]), "dw")
                    g.wait_ge(sem["dw"], cnt["dw"])
                    dinc(g.dma_start(out=w1b[piece * 128:(piece + 1) * 128, :],
                                     in_=WCV[:, :]), "dw")
                    g.wait_ge(sem["dw"], cnt["dw"])
                for piece in range(32):
                    dinc(g.dma_start(out=WCV[:, 0:E],
                                     in_=X["w2_my"][piece * 128:(piece + 1) * 128, :]), "dw")
                    g.wait_ge(sem["dw"], cnt["dw"])
                    dinc(g.dma_start(out=w2b[piece * 128:(piece + 1) * 128, :],
                                     in_=WCV[:, 0:E]), "dw")
                    g.wait_ge(sem["dw"], cnt["dw"])
        wconv_done = cnt["dw"]
        dx_done = cnt["dx"]

        def fence():
            snap = {k: v for k, v in cnt.items() if v and k != "dout"}
            with nc.Block() as blk:
                for eng_name in ("sync", "gpsimd", "tensor", "scalar", "vector"):
                    def body(eng, _snap=snap):
                        for k, v in _snap.items():
                            eng.wait_ge(sem[k], v)
                    getattr(blk, eng_name)(body)

        def emit_ln_gate():
            t0, v0, s0, d0 = cnt["ts"], cnt["vs"], cnt["ss"], cnt["dsc"]
            lnr_base = [cnt["lnr0"] // 16, cnt["lnr1"] // 16]
            with ExitStack() as ph:
                TPS = [ph.enter_context(nc.psum_tensor(U(f"TPS{i}"), [128, 8, 128], F32))
                       for i in range(2)]
                LNFM = [ph.enter_context(nc.sbuf_tensor(U(f"LNFM{i}"),
                                                        [128, 8, 128], F32))
                        for i in range(2)]
                LGP = [ph.enter_context(nc.psum_tensor(U(f"LGP{i}"), [128, 8], F32))
                       for i in range(2)]
                with nc.Block() as blk:
                    @blk.scalar
                    def _(s_):
                        s_.wait_ge(sem["dx"], dx_done)
                        for i in range(NT):
                            if i >= 2:
                                s_.wait_ge(sem["vs"], v0 + 11 * (i - 2) + 2)
                            inc(s_.activation(SQT[i % 2][:, :], RES[:, i, :],
                                              AF.Square), "ss")
                            s_.wait_ge(sem["vs"], v0 + 11 * i + 6)
                            inc(s_.activation(STAT[:, i, 6:7], STAT[:, i, 5:6],
                                              AF.Sqrt), "ss")

                    @blk.vector
                    def _(v):
                        v.wait_ge(sem["dx"], dx_done)
                        for i in range(NT):
                            inc(v.reduce_sum(STAT[:, i, 0:1], RES[:, i, :],
                                             axis=AX.X), "vs")
                            v.wait_ge(sem["ss"], s0 + 2 * i + 1)
                            inc(v.reduce_sum(STAT[:, i, 1:2], SQT[i % 2][:, :],
                                             axis=AX.X), "vs")
                            v.wait_ge(sem["vs"], cnt["vs"])
                            inc(v.tensor_scalar_mul(STAT[:, i, 2:3], STAT[:, i, 0:1],
                                                    1.0 / E), "vs")
                            v.wait_ge(sem["vs"], cnt["vs"])
                            inc(v.tensor_mul(STAT[:, i, 3:4], STAT[:, i, 2:3],
                                             STAT[:, i, 2:3]), "vs")
                            v.wait_ge(sem["vs"], cnt["vs"])
                            inc(v.scalar_tensor_tensor(STAT[:, i, 4:5], STAT[:, i, 1:2],
                                                       1.0 / E, STAT[:, i, 3:4],
                                                       ALU.mult, ALU.subtract), "vs")
                            v.wait_ge(sem["vs"], cnt["vs"])
                            inc(v.tensor_scalar_add(STAT[:, i, 5:6], STAT[:, i, 4:5],
                                                    EPS), "vs")
                            v.wait_ge(sem["ss"], s0 + 2 * i + 2)
                            inc(v.reciprocal(STAT[:, i, 7:8], STAT[:, i, 6:7]), "vs")
                            v.wait_ge(sem["vs"], cnt["vs"])
                            inc(v.scalar_tensor_tensor(STAT[:, i, 3:4], STAT[:, i, 2:3],
                                                       -1.0, STAT[:, i, 7:8],
                                                       ALU.mult, ALU.mult), "vs")
                            v.wait_ge(sem["vs"], cnt["vs"])
                            if i >= 2:
                                v.wait_ge(sem[f"lnr{i % 2}"], 16 * ((i - 2) // 2 + 1 + lnr_base[i % 2]))
                                v.wait_ge(sem["ts"], t0 + 16 * (i - 2) + 8)
                            inc(v.tensor_scalar(LNT[i % 2][:, :], RES[:, i, :],
                                                STAT[:, i, 7:8], STAT[:, i, 3:4],
                                                ALU.mult, ALU.add), "vs")
                            v.wait_ge(sem["ts"], t0 + 16 * i + 8)
                            inc(v.tensor_copy(LNFM[i % 2][:, :, :],
                                              TPS[i % 2][:, :, :]), "vs")
                            v.wait_ge(sem["ts"], t0 + 16 * i + 16)
                            inc(v.tensor_add(LGT[:, i, :], LGP[i % 2][:, :],
                                             CN["gb"][:, :]), "vs")

                    @blk.gpsimd
                    def _(g):
                        for i in range(NT):
                            g.wait_ge(sem["vs"], v0 + 11 * i + 9)
                            dinc(g.dma_start(out=ln_in_tm[:, i, :],
                                             in_=LNT[i % 2][:, :]), f"lnr{i % 2}")

                    @blk.tensor
                    def _(t):
                        for i in range(NT):
                            t.wait_ge(sem["vs"], v0 + 11 * i + 9)
                            if i >= 2:
                                t.wait_ge(sem["vs"], v0 + 11 * (i - 2) + 10)
                            for k in range(8):
                                inc(t.transpose(TPS[i % 2][:, k, :],
                                                LNT[i % 2][:, k * 128:(k + 1) * 128],
                                                CN["ident_f32"][:, :]), "ts")
                            t.wait_ge(sem["vs"], v0 + 11 * i + 10)
                            if i >= 2:
                                t.wait_ge(sem["vs"], v0 + 11 * (i - 2) + 11)
                            for k in range(8):
                                inc(t.matmul(LGP[i % 2][:, :], LNFM[i % 2][:, k, :],
                                             CN["gwT"][:, k, :],
                                             start=(k == 0), stop=(k == 7)), "ts")
            with nc.Block() as blk:
                @blk.gpsimd
                def _(g):
                    g.wait_ge(sem["vs"], cnt["vs"])
                    dinc(g.dma_start(out=lg_in_tm[:, :, :], in_=LGT[:, :, :]), "dsc")

        def emit_ags():
            with nc.Block() as blk:
                @blk.gpsimd
                def _(g):
                    g.wait_ge(sem["dsc"], cnt["dsc"])
                    g.wait_ge(sem["lnr0"], cnt["lnr0"])
                    g.wait_ge(sem["lnr1"], cnt["lnr1"])
                    inc(g.collective_compute("AllGather", ALU.bypass,
                                             replica_groups=[list(range(NCORE))],
                                             ins=[lg_in[:, :].opt()],
                                             outs=[lg_all[:, :].opt()]), "cc")
                    inc(g.collective_compute("AllGather", ALU.bypass,
                                             replica_groups=[list(range(NCORE))],
                                             ins=[ln_in[:, :].opt()],
                                             outs=[ln_all[:, :].opt()]), "cc")

        def emit_routing_scatter():
            with ExitStack() as ph:
                LG = ph.enter_context(nc.sbuf_tensor(U("LG"), [128, NP, 8], F32))
                EXV = ph.enter_context(nc.sbuf_tensor(U("EXV"), [128, NP, 8], F32))
                OHV = ph.enter_context(nc.sbuf_tensor(U("OHV"), [128, NP, 8], F32))
                INCL = ph.enter_context(nc.sbuf_tensor(U("INCL"), [128, NP, 8], F32))
                SCR8 = ph.enter_context(nc.sbuf_tensor(U("SCR8"), [128, NP, 8], F32))
                RM = ph.enter_context(nc.sbuf_tensor(U("RM"), [128, NP], F32))
                RS = ph.enter_context(nc.sbuf_tensor(U("RS"), [128, NP], F32))
                TP = ph.enter_context(nc.sbuf_tensor(U("TP"), [128, NP], F32))
                ET = ph.enter_context(nc.sbuf_tensor(U("ET"), [128, NP], F32))
                POS = ph.enter_context(nc.sbuf_tensor(U("POS"), [128, NP], F32))
                KEEP = ph.enter_context(nc.sbuf_tensor(U("KEEP"), [128, NP], F32))
                COEF = ph.enter_context(nc.sbuf_tensor(U("COEF"), [128, NP], F32))
                SA = ph.enter_context(nc.sbuf_tensor(U("SA"), [128, NP], F32))
                SB_ = ph.enter_context(nc.sbuf_tensor(U("SB_"), [128, NP], F32))
                gidrep = ph.enter_context(nc.sbuf_tensor(U("gidrep"), [128, NP, 64], F32))
                SB2 = ph.enter_context(nc.sbuf_tensor(U("SB2"), [128, NP], F32))
                OW2 = ph.enter_context(nc.sbuf_tensor(U("OW2"), [16, 128], F32))
                TMP16 = ph.enter_context(nc.sbuf_tensor(U("TMP16"), [128, NP, 16], F32))
                REG.update(et=(ET[:, :], [128, NP], F32), pos=(POS[:, :], [128, NP], F32),
                           keep=(KEEP[:, :], [128, NP], F32),
                           coef=(COEF[:, :], [128, NP], F32),
                           tp=(TP[:, :], [128, NP], F32))
                iota_bc = CN["iota8"][:, :].unsqueeze(1).broadcast_to([128, NP, 8])
                with ExitStack() as php:
                    EXP_ = php.enter_context(nc.psum_tensor(U("EXCP"), [128, 8], F32))
                    CNP = php.enter_context(nc.psum_tensor(U("CNP"), [1, 8], F32))
                    OSP = php.enter_context(nc.psum_tensor(U("OSP"), [16, 128], F32))
                    CTP = php.enter_context(nc.psum_tensor(U("CTP"), [128, 16], F32))
                    with nc.Block() as blk:
                        @blk.sync
                        def _(sy):
                            sy.wait_ge(sem["cc"], cnt["cc"])
                            dinc(sy.dma_start(
                                out=LG[:, :, :],
                                in_=lg_all.rearrange("(p a) e -> p a e", p=128)), "dat")

                        @blk.scalar
                        def _(s):
                            s.wait_ge(sem["dat"], cnt["dat"])
                            inc(s.activation(EXV[:, :, :], LG[:, :, :], AF.Exp), "ss")

                        @blk.vector
                        def _(v):
                            v.wait_ge(sem["ss"], cnt["ss"])
                            inc(v.reduce_max(RM[:, :], EXV[:, :, :], axis=AX.X), "vs")
                            inc(v.reduce_sum(RS[:, :], EXV[:, :, :], axis=AX.X), "vs")
                            v.wait_ge(sem["vs"], cnt["vs"])
                            inc(v.reciprocal(TP[:, :], RS[:, :]), "vs")
                            v.wait_ge(sem["vs"], cnt["vs"])
                            inc(v.tensor_tensor(TP[:, :], TP[:, :], RM[:, :],
                                                ALU.mult), "vs")
                            rm_bc = RM[:, :].unsqueeze(2).broadcast_to([128, NP, 8])
                            v.wait_ge(sem["vs"], cnt["vs"])
                            inc(v.tensor_tensor(SCR8[:, :, :], EXV[:, :, :], rm_bc,
                                                ALU.is_lt), "vs")
                            v.wait_ge(sem["vs"], cnt["vs"])
                            inc(v.scalar_tensor_tensor(SCR8[:, :, :], SCR8[:, :, :], 9.0,
                                                       iota_bc, ALU.mult, ALU.add), "vs")
                            v.wait_ge(sem["vs"], cnt["vs"])
                            inc(v.tensor_reduce(ET[:, :], SCR8[:, :, :], axis=AX.X,
                                                op=ALU.min), "vs")
                            et_bc = ET[:, :].unsqueeze(2).broadcast_to([128, NP, 8])
                            v.wait_ge(sem["vs"], cnt["vs"])
                            inc(v.tensor_tensor(OHV[:, :, :], iota_bc, et_bc,
                                                ALU.is_equal), "vs")
                            v.wait_ge(sem["vs"], cnt["vs"])
                            for e in range(8):
                                inc(v.tensor_tensor_scan(INCL[:, :, e], OHV[:, :, e],
                                                         CN["zcol"][:, :], 0.0,
                                                         ALU.add, ALU.add), "vs")
                            v.wait_ge(sem["vs"], cnt["vs"])
                            inc(v.tensor_copy(TOTB[:, :], INCL[:, NP - 1, :]), "vs")

                        @blk.tensor
                        def _(t):
                            t.wait_ge(sem["vs"], cnt["vs"])
                            inc(t.matmul(EXP_[:, :], CN["trils"][:, :], TOTB[:, :],
                                         start=True, stop=True), "ts")
                            inc(t.matmul(CNP[:, :], CN["ones128"][:, :], TOTB[:, :],
                                         start=True, stop=True), "ts")

                        @blk.vector
                        def _(v):
                            v.wait_ge(sem["ts"], cnt["ts"])
                            v.wait_ge(sem["vs"], cnt["vs"])
                            inc(v.tensor_copy(EXCL[:, :], EXP_[:, :]), "vs")
                            v.wait_ge(sem["vs"], cnt["vs"])
                            inc(v.tensor_tensor(SCR8[:, :, :], OHV[:, :, :],
                                                INCL[:, :, :], ALU.mult), "vs")
                            v.wait_ge(sem["vs"], cnt["vs"])
                            inc(v.reduce_sum(POS[:, :], SCR8[:, :, :], axis=AX.X), "vs")
                            ex_bc = EXCL[:, :].unsqueeze(1).broadcast_to([128, NP, 8])
                            v.wait_ge(sem["vs"], cnt["vs"])
                            inc(v.tensor_tensor(SCR8[:, :, :], OHV[:, :, :], ex_bc,
                                                ALU.mult), "vs")
                            v.wait_ge(sem["vs"], cnt["vs"])
                            inc(v.reduce_sum(SB_[:, :], SCR8[:, :, :], axis=AX.X), "vs")
                            v.wait_ge(sem["vs"], cnt["vs"])
                            inc(v.scalar_tensor_tensor(POS[:, :], POS[:, :], -1.0,
                                                       SB_[:, :], ALU.add, ALU.add), "vs")
                            v.wait_ge(sem["vs"], cnt["vs"])
                            inc(v.tensor_single_scalar(KEEP[:, :], POS[:, :],
                                                       float(CAP), ALU.is_lt), "vs")
                            v.wait_ge(sem["vs"], cnt["vs"])
                            inc(v.scalar_tensor_tensor(COEF[:, :], TP[:, :], 0.5,
                                                       KEEP[:, :], ALU.mult,
                                                       ALU.mult), "vs")
                            v.wait_ge(sem["vs"], cnt["vs"])
                            inc(v.tensor_scalar(SA[:, :], ET[:, :], CN["mye"][:, :],
                                                None, ALU.is_equal, ALU.bypass), "vs")
                            v.wait_ge(sem["vs"], cnt["vs"])
                            inc(v.tensor_tensor(SA[:, :], SA[:, :], KEEP[:, :],
                                                ALU.mult), "vs")
                            v.wait_ge(sem["vs"], cnt["vs"])
                            inc(v.tensor_scalar_add(SB_[:, :], POS[:, :],
                                                    -float(TRASH)), "vs")
                            v.wait_ge(sem["vs"], cnt["vs"])
                            inc(v.tensor_tensor(SB_[:, :], SB_[:, :], SA[:, :],
                                                ALU.mult), "vs")
                            v.wait_ge(sem["vs"], cnt["vs"])
                            inc(v.tensor_scalar_add(SB_[:, :], SB_[:, :],
                                                    float(TRASH)), "vs")
                            v.wait_ge(sem["vs"], cnt["vs"])
                            inc(v.tensor_tensor(SA[:, :], POS[:, :], KEEP[:, :],
                                                ALU.mult), "vs")
                            v.wait_ge(sem["vs"], cnt["vs"])
                            inc(v.tensor_tensor(
                                TMP16[:, :, :],
                                SA[:, :].unsqueeze(2).broadcast_to([128, NP, 16]),
                                CN["th16"][:, :].unsqueeze(1)
                                .broadcast_to([128, NP, 16]), ALU.is_ge), "vs")
                            v.wait_ge(sem["vs"], cnt["vs"])
                            inc(v.reduce_sum(RS[:, :], TMP16[:, :, :], axis=AX.X), "vs")
                            v.wait_ge(sem["vs"], cnt["vs"])
                            inc(v.scalar_tensor_tensor(RM[:, :], RS[:, :], -float(CW),
                                                       SA[:, :], ALU.mult, ALU.add), "vs")
                            v.wait_ge(sem["vs"], cnt["vs"])
                            inc(v.scalar_tensor_tensor(SA[:, :], ET[:, :], float(CAP),
                                                       RS[:, :], ALU.mult, ALU.add), "vs")
                            v.wait_ge(sem["vs"], cnt["vs"])
                            inc(v.scalar_tensor_tensor(SA[:, :], RM[:, :], 16.0,
                                                       SA[:, :], ALU.mult, ALU.add), "vs")

                        @blk.tensor
                        def _(t):
                            t.wait_ge(sem["vs"], cnt["vs"])
                            inc(t.matmul(OSP[:, :], CN["esel"][:, :], SA[:, :],
                                         start=True, stop=True), "ts")
                            inc(t.matmul(CTP[:, :], COEF[:, :], CN["esel"][:, :],
                                         start=True, stop=True), "ts")

                        @blk.vector
                        def _(v):
                            v.wait_ge(sem["ts"], cnt["ts"])
                            inc(v.tensor_copy(ownsl[:, :], OSP[:, :]), "vs")
                            inc(v.tensor_copy(coefT[:, :], CTP[:, :]), "vs")
                            inc(v.tensor_copy(CNTS[:, :], CNP[:, :]), "vs")
                            v.wait_ge(sem["vs"], cnt["vs"])
                            inc(v.tensor_tensor(CNTS[:, :], CNTS[:, :],
                                                CN["myoh"][:, :], ALU.mult), "vs")
                            v.wait_ge(sem["vs"], cnt["vs"])
                            inc(v.reduce_sum(CNTsc[:, :], CNTS[:, :], axis=AX.X), "vs")
                            inc(v.tensor_copy(
                                gidrep[:, :, :],
                                CN["gid128"][:, :].unsqueeze(2)
                                .broadcast_to([128, NP, 64])), "vs")
                            v.wait_ge(sem["vs"], cnt["vs"])
                            inc(v.tensor_copy(
                                SB2[:, :].rearrange("p (q d) -> p q d", q=16),
                                SB_[:, :].rearrange("p (d q) -> p q d", q=16)), "vs")
                            inc(v.tensor_copy(
                                OW2[:, :].rearrange("p (q d) -> p q d", q=16),
                                ownsl[:, :].rearrange("p (d q) -> p q d", q=16)), "vs")

                        @blk.gpsimd
                        def _(g):
                            g.wait_ge(sem["vs"], cnt["vs"])
                            inc(g.partition_broadcast(CNT1[:, :], CNTsc[:, :]), "gs")
                            dinc(g.dma_start(
                                out=dscat.rearrange("q (p d) -> p q d", p=128),
                                in_=SB2[:, :].rearrange("p (q d) -> p q d", q=16)), "dsc")
                            dinc(g.dma_start(
                                out=down.rearrange("q (p d) -> p q d", p=16),
                                in_=OW2[:, :].rearrange("p (q d) -> p q d", q=16)), "dsc")
                            g.wait_ge(sem["dsc"], cnt["dsc"])
                            for grp in range(8):
                                dinc(g.dma_start(out=idxsc[16 * grp:16 * grp + 16, :],
                                                 in_=dscat[:, :]), "dsc")
                                dinc(g.dma_start(out=idxown[16 * grp:16 * grp + 16, :],
                                                 in_=down[:, :]), "dsc")
                            g.wait_ge(sem["dsc"], cnt["dsc"])
                            g.wait_ge(sem["vs"], cnt["vs"])
                            sc_chunks = [(ch * 896, 896) for ch in range(18)]
                            sc_chunks.append((18 * 896, T - 18 * 896))
                            for base, ncnt in sc_chunks:
                                dinc(g.dma_scatter_add(
                                    tbl[:, :],
                                    gidrep[:, base // 128:(base + ncnt) // 128, :],
                                    idxsc[:, base // 16:(base + ncnt) // 16],
                                    ncnt, NREG[ncnt], 64), "dg")
                                g.wait_ge(sem["dg"], cnt["dg"])

        def emit_table_read():
            with ExitStack() as ph:
                TRD = ph.enter_context(nc.sbuf_tensor(U("TRD"), [16, CW, 64], F32))
                with nc.Block() as blk:
                    @blk.gpsimd
                    def _(g):
                        dinc(g.dma_start(
                            out=TRD[:, :, :],
                            in_=tbl[0:CAP, :].rearrange("(q f) z -> q f z", q=16)), "dg")
                        g.wait_ge(sem["dg"], cnt["dg"])

                    @blk.vector
                    def _(v):
                        v.wait_ge(sem["dg"], cnt["dg"])
                        v.wait_ge(sem["gs"], cnt["gs"])
                        inc(v.tensor_scalar(maskv[:, :], CN["rowiota"][:, :],
                                            CNT1[:, :], None, ALU.is_lt,
                                            ALU.bypass), "vs")
                        v.wait_ge(sem["vs"], cnt["vs"])
                        inc(v.select(selg[:, :], maskv[:, :], TRD[:, :, 0],
                                     CN["zeros16"][:, :], add_drain=True), "vs")
                        v.wait_ge(sem["vs"], cnt["vs"])
                        inc(v.tensor_copy(idg16[:, :], selg[:, :]), "vs")

                    @blk.gpsimd
                    def _(g):
                        g.wait_ge(sem["vs"], cnt["vs"])
                        for grp in range(8):
                            dinc(g.dma_start(out=idxg[16 * grp:16 * grp + 16, :],
                                             in_=idg16[:, :]), "dsc")
                        g.wait_ge(sem["dsc"], cnt["dsc"])

        def emit_ffn():
            with ExitStack() as ph:
                BUF = ph.enter_context(nc.sbuf_tensor(U("BUF"), [128, CAP // 128, E], BF16))
                HFM = ph.enter_context(nc.sbuf_tensor(U("HFM"), [128, 32, SCH], BF16))
                W1P = [ph.enter_context(nc.sbuf_tensor(U(f"W1P{i}"), [128, 8, W1PF], BF16))
                       for i in range(2)]
                W2P = [ph.enter_context(nc.sbuf_tensor(U(f"W2P{i}"), [128, 512], BF16))
                       for i in range(2)]
                EOT = [ph.enter_context(nc.sbuf_tensor(U(f"EOT{i}"), [128, 512], F32))
                       for i in range(2)]
                REG.update(buf0=(BUF[:, 0:2, :], [128, 2, E], BF16))
                with nc.Block() as blk:
                    @blk.gpsimd
                    def _(g):
                        for base, ncnt in ((0, 1024), (1024, 1024), (2048, 512)):
                            dinc(g.dma_gather(
                                BUF[:, base // 128:(base + ncnt) // 128, :],
                                ln_all[:, :], idxg[:, base // 16:(base + ncnt) // 16],
                                ncnt, NREG[ncnt], E, transpose=False), "dg")
                            g.wait_ge(sem["dg"], cnt["dg"])

                ts0, ss0, vs0 = cnt["ts"], cnt["ss"], cnt["vs"]
                import os
                if os.environ.get("KDBG"):
                    print("FFN bases: ts0", ts0, "ss0", ss0, "vs0", vs0,
                          "w1s", [cnt["w1s0"], cnt["w1s1"]],
                          "w2s", [cnt["w2s0"], cnt["w2s1"]],
                          "dg", cnt["dg"], "dw", cnt["dw"])
                w1s_base = [cnt["w1s0"] // 16, cnt["w1s1"] // 16]
                w2s_base = [cnt["w2s0"] // 16, cnt["w2s1"] // 16]
                eot_base = [cnt[f"eots{i}"] // 16 for i in range(2)]

                def ts_h(sc_, pc):
                    return ts0 + sc_ * 544 + 32 + (pc + 1) * 16

                def ts_h_half(sc_, fc):
                    return ts0 + sc_ * 544 + 32 + fc * 8 + 8

                def ts_eo(sc_, dmh, fc):
                    return ts0 + sc_ * 544 + 288 + dmh * 128 + (fc + 1) * 4

                def ss_relu(sc_, fc):
                    return ss0 + sc_ * 32 + fc + 1

                def vs_ev(sc_, k):
                    return vs0 + sc_ * 40 + 32 + k + 1

                def w1_wait(t, gi):
                    t.wait_ge(sem[f"w1s{gi % 2}"],
                              16 * (gi // 2 + 1 + w1s_base[gi % 2]))

                def w2_wait(t, gj):
                    t.wait_ge(sem[f"w2s{gj % 2}"],
                              16 * (gj // 2 + 1 + w2s_base[gj % 2]))

                with ExitStack() as php:
                    HP = [php.enter_context(nc.psum_tensor(U(f"HP{i}"), [128, SCH], F32))
                          for i in range(2)]
                    EP = [php.enter_context(nc.psum_tensor(U(f"EPp{i}"), [128, 512], F32))
                          for i in range(4)]
                    TPB2 = [php.enter_context(nc.psum_tensor(U(f"TPB2{i}"), [128, 8, 128],
                                                             BF16))
                            for i in range(2)]
                    with nc.Block() as blk:
                        @blk.sync
                        def _(sy):
                            sy.wait_ge(sem["dw"], wconv_done)
                            for sc_ in range(NSC):
                                for pc in range(NW1P):
                                    gi = sc_ * NW1P + pc
                                    if gi >= 2:
                                        psc, ppc = divmod(gi - 2, NW1P)
                                        sy.wait_ge(sem["ts"], ts_h(psc, ppc))
                                    dinc(sy.dma_start(
                                        out=W1P[gi % 2][:, :, :],
                                        in_=w1b_v[:, :, pc * W1PF:(pc + 1) * W1PF]),
                                        f"w1s{gi % 2}")
                                for dmh in range(2):
                                    for fc in range(32):
                                        gj = sc_ * 64 + dmh * 32 + fc
                                        if gj >= 2:
                                            psc, r = divmod((gj - 2), 64)
                                            pdmh, pfc = divmod(r, 32)
                                            sy.wait_ge(sem["ts"],
                                                       ts_eo(psc, pdmh, pfc))
                                        dinc(sy.dma_start(
                                            out=W2P[gj % 2][:, :],
                                            in_=w2b[fc * 128:(fc + 1) * 128,
                                                    dmh * 512:(dmh + 1) * 512]),
                                            f"w2s{gj % 2}")
                                    for k in range(dmh * 4, dmh * 4 + 4):
                                        st = k % 4
                                        gk = sc_ * 8 + k
                                        sy.wait_ge(sem["vs"], vs_ev(sc_, k))
                                        dinc(sy.dma_start(
                                            out=eo_in[sc_ * SCH + st * 128:
                                                      sc_ * SCH + (st + 1) * 128,
                                                      dmh * 512:(dmh + 1) * 512],
                                            in_=EOT[gk % 2][:, :]), f"eots{gk % 2}")

                        @blk.tensor
                        def _(t):
                            t.wait_ge(sem["dg"], cnt["dg"])
                            for sc_ in range(NSC):
                                for i_, tj in enumerate(TWALK):
                                    st, dmc = divmod(tj, 8)
                                    if i_ >= 2:
                                        t.wait_ge(sem["vs"],
                                                  vs0 + sc_ * 40 + i_ - 1)
                                    inc(t.transpose(
                                        TPB2[i_ % 2][:, 0, :],
                                        BUF[:, 4 * sc_ + st,
                                            dmc * 128:(dmc + 1) * 128],
                                        CN["ident_bf"][:, :]), "ts")
                                t.wait_ge(sem["vs"], vs0 + sc_ * 40 + 32)
                                for pc in range(NW1P):
                                    gi = sc_ * NW1P + pc
                                    w1_wait(t, gi)
                                    for f1 in range(2):
                                        fc = pc * 2 + f1
                                        gfc = sc_ * 32 + fc
                                        if gfc >= 2:
                                            psc, pfc = divmod(gfc - 2, 32)
                                            t.wait_ge(sem["ss"], ss_relu(psc, pfc))
                                        for k in range(8):
                                            inc(t.matmul(
                                                HP[fc % 2][:, :],
                                                W1P[gi % 2][:, k,
                                                            f1 * 128:(f1 + 1) * 128],
                                                BUF[:, 4 * sc_ + k // 2,
                                                    (k % 2) * 512:
                                                    (k % 2) * 512 + 512],
                                                start=(k == 0), stop=(k == 7)), "ts")
                                for dmh in range(2):
                                    for fc in range(32):
                                        gj = sc_ * 64 + dmh * 32 + fc
                                        w2_wait(t, gj)
                                        t.wait_ge(sem["ss"], ss_relu(sc_, fc))
                                        if fc == 0:
                                            if dmh == 1:
                                                t.wait_ge(sem["vs"], vs_ev(sc_, 3))
                                            elif sc_ >= 1:
                                                t.wait_ge(sem["vs"], vs_ev(sc_ - 1, 7))
                                        for st in range(4):
                                            inc(t.matmul(
                                                EP[st][:, :],
                                                HFM[:, fc, st * 128:(st + 1) * 128],
                                                W2P[gj % 2][:, :],
                                                start=(fc == 0), stop=(fc == 31)), "ts")

                        @blk.scalar
                        def _(s):
                            for sc_ in range(NSC):
                                for fc in range(32):
                                    s.wait_ge(sem["ts"], ts_h_half(sc_, fc))
                                    if sc_ >= 1:
                                        s.wait_ge(sem["ts"], ts_eo(sc_ - 1, 1, 31))
                                    inc(s.activation(HFM[:, fc, :], HP[fc % 2][:, :],
                                                     AF.Relu,
                                                     bias=CN["b1c"][:, fc:fc + 1]), "ss")

                        @blk.vector
                        def _(v):
                            for sc_ in range(NSC):
                                for i_, tj in enumerate(TWALK):
                                    tgt = 4 * (tj % 8) + tj // 8
                                    v.wait_ge(sem["ts"],
                                              ts0 + sc_ * 544 + min(i_ + 2, 32))
                                    inc(v.tensor_copy(
                                        BUF[:, 4 * sc_ + tgt // 8,
                                            (tgt % 8) * 128:(tgt % 8) * 128 + 128],
                                        TPB2[i_ % 2][:, 0, :]), "vs")
                                for k in range(8):
                                    dmh, st = divmod(k, 4)
                                    v.wait_ge(sem["ts"], ts_eo(sc_, dmh, 31))
                                    gk = sc_ * 8 + k
                                    if gk >= 2:
                                        v.wait_ge(sem[f"eots{gk % 2}"],
                                                  16 * ((gk - 2) // 2 + 1
                                                        + eot_base[gk % 2]))
                                    inc(v.tensor_add(
                                        EOT[gk % 2][:, :], EP[st][:, :],
                                        CN["b2r"][:, dmh * 512:(dmh + 1) * 512]), "vs")
                with nc.Block() as blk:
                    @blk.gpsimd
                    def _(g):
                        for i4 in range(2):
                            g.wait_ge(sem[f"eots{i4}"], cnt[f"eots{i4}"])
                        inc(g.collective_compute("AllGather", ALU.bypass,
                                                 replica_groups=[list(range(NCORE))],
                                                 ins=[eo_in[:, :].opt()],
                                                 outs=[eo_all[:, :].opt()]), "cc")

        def emit_gather_back():
            with ExitStack() as ph:
                GATH = ph.enter_context(nc.sbuf_tensor(U("GATH"), [128, NT, E], F32))
                REG.update(gath0=(GATH[:, 0:2, :], [128, 2, E], F32))
                with nc.Block() as blk:
                    @blk.gpsimd
                    def _(g):
                        g.wait_ge(sem["cc"], cnt["cc"])
                        for ch in range(2):
                            dinc(g.dma_gather(GATH[:, 8 * ch:8 * ch + 8, :],
                                              eo_all[:, :],
                                              idxown[:, 64 * ch:64 * ch + 64],
                                              1024, NREG[1024], E, transpose=False), "dg")
                            g.wait_ge(sem["dg"], cnt["dg"])

                    @blk.vector
                    def _(v):
                        v.wait_ge(sem["dg"], cnt["dg"])
                        v.wait_ge(sem["vs"], cnt["vs"])
                        for i in range(NT):
                            inc(v.scalar_tensor_tensor(RES[:, i, :], GATH[:, i, :],
                                                       coefT[:, i:i + 1], RES[:, i, :],
                                                       ALU.mult, ALU.add), "vs")

        def moe_sublayer(zero_tbl=False):
            if zero_tbl:
                with nc.Block() as blk:
                    @blk.gpsimd
                    def _(g):
                        for zk in range(TBL_ROWS // 256):
                            dinc(g.dma_start(out=tbl[zk * 256:(zk + 1) * 256, :],
                                             in_=CN["zcol"][:, :]), "dw")
                        g.wait_ge(sem["dw"], cnt["dw"])
            emit_ln_gate()
            emit_ags()
            emit_routing_scatter()
            if upto >= 4:
                emit_table_read()
            if upto >= 5:
                emit_ffn()
                emit_gather_back()
            fence()

        def emit_attention():
            import os
            _au = int(os.environ.get("ATTN_UPTO", "9"))
            with ExitStack() as att:
                LN2FM = att.enter_context(nc.sbuf_tensor(U("LN2FM"), [128, 8, TOK], BF16))
                VT = att.enter_context(nc.sbuf_tensor(U("VT"), [128, NT, 1040], BF16))
                REG.update(ln2fm0=(LN2FM[:, :, 0:256], [128, 8, 256], BF16),
                           vt0=(VT[:, 0, :], [128, 1040], BF16))

                t0, v0, s0 = cnt["ts"], cnt["vs"], cnt["ss"]
                with ExitStack() as ph:
                    TPB = [ph.enter_context(nc.psum_tensor(U(f"TPB{i}"), [128, 8, 128],
                                                           BF16))
                           for i in range(2)]
                    with nc.Block() as blk:
                        @blk.scalar
                        def _(s_):
                            for i in range(NT):
                                if i >= 2:
                                    s_.wait_ge(sem["vs"], v0 + 1 + 10 * (i - 2) + 2)
                                inc(s_.activation(SQT[i % 2][:, :], RES[:, i, :],
                                                  AF.Square), "ss")
                                s_.wait_ge(sem["vs"], v0 + 1 + 10 * i + 6)
                                inc(s_.activation(STAT[:, i, 6:7], STAT[:, i, 5:6],
                                                  AF.Sqrt), "ss")

                        @blk.vector
                        def _(v):
                            inc(v.memset(VT[:, :, :]
                                         .rearrange("p a (h z) -> p a h z", z=65)
                                         [:, :, :, 64:65], 1.0), "vs")
                            for i in range(NT):
                                inc(v.reduce_sum(STAT[:, i, 0:1], RES[:, i, :],
                                                 axis=AX.X), "vs")
                                v.wait_ge(sem["ss"], s0 + 2 * i + 1)
                                inc(v.reduce_sum(STAT[:, i, 1:2], SQT[i % 2][:, :],
                                                 axis=AX.X), "vs")
                                v.wait_ge(sem["vs"], cnt["vs"])
                                inc(v.tensor_scalar_mul(STAT[:, i, 2:3],
                                                        STAT[:, i, 0:1], 1.0 / E), "vs")
                                v.wait_ge(sem["vs"], cnt["vs"])
                                inc(v.tensor_mul(STAT[:, i, 3:4], STAT[:, i, 2:3],
                                                 STAT[:, i, 2:3]), "vs")
                                v.wait_ge(sem["vs"], cnt["vs"])
                                inc(v.scalar_tensor_tensor(STAT[:, i, 4:5],
                                                           STAT[:, i, 1:2], 1.0 / E,
                                                           STAT[:, i, 3:4],
                                                           ALU.mult, ALU.subtract), "vs")
                                v.wait_ge(sem["vs"], cnt["vs"])
                                inc(v.tensor_scalar_add(STAT[:, i, 5:6],
                                                        STAT[:, i, 4:5], EPS), "vs")
                                v.wait_ge(sem["ss"], s0 + 2 * i + 2)
                                inc(v.reciprocal(STAT[:, i, 7:8], STAT[:, i, 6:7]), "vs")
                                v.wait_ge(sem["vs"], cnt["vs"])
                                inc(v.scalar_tensor_tensor(STAT[:, i, 3:4],
                                                           STAT[:, i, 2:3], -1.0,
                                                           STAT[:, i, 7:8],
                                                           ALU.mult, ALU.mult), "vs")
                                v.wait_ge(sem["vs"], cnt["vs"])
                                if i >= 2:
                                    v.wait_ge(sem["ts"], t0 + 8 * (i - 1))
                                inc(v.tensor_scalar(LNB[i % 2][:, :], RES[:, i, :],
                                                    STAT[:, i, 7:8], STAT[:, i, 3:4],
                                                    ALU.mult, ALU.add), "vs")
                                v.wait_ge(sem["ts"], t0 + 8 * (i + 1))
                                inc(v.tensor_copy(LN2FM[:, :, i * 128:(i + 1) * 128],
                                                  TPB[i % 2][:, :, :]), "vs")

                        @blk.tensor
                        def _(t):
                            for i in range(NT):
                                t.wait_ge(sem["vs"], v0 + 1 + 10 * i + 9)
                                if i >= 2:
                                    t.wait_ge(sem["vs"], v0 + 1 + 10 * (i - 2) + 10)
                                for k in range(8):
                                    inc(t.transpose(TPB[i % 2][:, k, :],
                                                    LNB[i % 2][:, k * 128:(k + 1) * 128],
                                                    CN["ident_bf"][:, :]), "ts")
                fence()
                if _au < 2:
                    return

                t0v, v0v, dw0v = cnt["ts"], cnt["vs"], cnt["dwa"]
                with ExitStack() as ph:
                    WVP = ph.enter_context(nc.sbuf_tensor(U("WVP"), [128, 8, 512], BF16))
                    VP = [ph.enter_context(nc.psum_tensor(U(f"VP{i}"), [128, 512], F32))
                          for i in range(2)]
                    with nc.Block() as blk:
                        @blk.gpsimd
                        def _(g):
                            for grp in range(2):
                                if grp >= 1:
                                    g.wait_ge(sem["ts"], t0v + NT * 8)
                                dinc(g.dma_start(out=WVP[:, :, :],
                                                 in_=wv_v[grp, :, :, :]), "dwa")

                        @blk.tensor
                        def _(t):
                            for grp in range(2):
                                t.wait_ge(sem["dwa"], dw0v + 16 * (grp + 1))
                                for i in range(NT):
                                    gi = grp * NT + i
                                    if gi >= 2:
                                        t.wait_ge(sem["vs"], v0v + gi - 1)
                                    for k in range(8):
                                        inc(t.matmul(VP[gi % 2][:, :],
                                                     LN2FM[:, k, i * 128:(i + 1) * 128],
                                                     WVP[:, k, :],
                                                     start=(k == 0), stop=(k == 7)),
                                            "ts")

                        @blk.vector
                        def _(v):
                            for grp in range(2):
                                for i in range(NT):
                                    gi = grp * NT + i
                                    v.wait_ge(sem["ts"], t0v + (gi + 1) * 8)
                                    out = VT[:, i, :].rearrange("p (h z) -> p h z", z=65)
                                    inc(v.tensor_copy(
                                        out[:, 8 * grp:8 * grp + 8, 0:64],
                                        VP[gi % 2][:, :]
                                        .rearrange("p (h d) -> p h d", d=64)), "vs")
                fence()
                if _au < 3:
                    return

                with ExitStack() as ph:
                    QF = ph.enter_context(nc.sbuf_tensor(U("QF"), [128, TOK], BF16))
                    KF = ph.enter_context(nc.sbuf_tensor(U("KF"), [128, TOK], BF16))
                    PT = [ph.enter_context(nc.sbuf_tensor(U(f"PT{i}"), [128, 512], BF16))
                          for i in range(2)]
                    RCP = ph.enter_context(nc.sbuf_tensor(U("RCP"), [128, 512], F32))
                    RCPB = ph.enter_context(nc.sbuf_tensor(U("RCPB"), [128, 512], F32))
                    RA = ph.enter_context(nc.sbuf_tensor(U("RA"), [1, 512], F32))
                    RB = ph.enter_context(nc.sbuf_tensor(U("RB"), [1, 512], F32))
                    WQP = [ph.enter_context(nc.sbuf_tensor(U(f"WQP{i}"), [128, 8, 128],
                                                           BF16)) for i in range(2)]
                    WKP = [ph.enter_context(nc.sbuf_tensor(U(f"WKP{i}"), [128, 8, 128],
                                                           BF16)) for i in range(2)]
                    ONB = [ph.enter_context(nc.sbuf_tensor(U(f"ONB{i}"), [128, 512],
                                                           BF16)) for i in range(2)]
                    QP = [ph.enter_context(nc.psum_tensor(U(f"QPp{i}"), [128, 512], F32))
                          for i in range(2)]
                    SP = [ph.enter_context(nc.psum_tensor(U(f"SPp{i}"), [128, 512], F32))
                          for i in range(2)]
                    OP1 = ph.enter_context(nc.psum_tensor(U("OP1"), [128, 512], F32))
                    OP2 = ph.enter_context(nc.psum_tensor(U("OP2"), [128, 512], F32))
                    LP = ph.enter_context(nc.psum_tensor(U("LP"), [128, 512], F32))
                    ta0, sa0, va0, ga0 = (cnt["ts"], cnt["ss"], cnt["vs"],
                                          cnt["gs"])
                    wqs_base = [cnt["wqs0"] // 16, cnt["wqs1"] // 16]
                    ons_base = [cnt["ons0"] // 16, cnt["ons1"] // 16]
                    TPP, SPP, VPP, GPP = 264, 84, 52, 8

                    def t_pair(pr):
                        return ta0 + pr * TPP

                    def s_pair(pr):
                        return sa0 + pr * SPP

                    def v_pair(pr):
                        return va0 + pr * VPP

                    def g_pair(pr):
                        return ga0 + pr * GPP

                    npairs = 8 if _au >= 4 else min(_au - 2, 8)
                    for pr in range(npairs):
                        with nc.Block() as blk:
                            @blk.gpsimd
                            def _(g, pr=pr):
                                if pr >= 2:
                                    g.wait_ge(sem["ts"], t_pair(pr - 1))
                                dinc(g.dma_start(out=WQP[pr % 2][:, :, :],
                                                 in_=wq_v[pr, :, :, :]), f"wqs{pr % 2}")
                                dinc(g.dma_start(out=WKP[pr % 2][:, :, :],
                                                 in_=wk_v[pr, :, :, :]), f"wqs{pr % 2}")

                            @blk.tensor
                            def _(t, pr=pr):
                                t.wait_ge(sem[f"wqs{pr % 2}"],
                                          16 * (2 * (pr // 2 + 1) + wqs_base[pr % 2]))
                                for tc in range(4):
                                    if tc >= 2:
                                        t.wait_ge(sem["ss"], s_pair(pr) + (tc - 2) + 1)
                                    for k in range(8):
                                        inc(t.matmul(
                                            QP[tc % 2][:, :], WQP[pr % 2][:, k, :],
                                            LN2FM[:, k, tc * 512:(tc + 1) * 512],
                                            start=(k == 0), stop=(k == 7)), "ts")
                                for tc in range(4):
                                    if tc < 2:
                                        t.wait_ge(sem["ss"], s_pair(pr) + tc + 3)
                                    else:
                                        t.wait_ge(sem["vs"], v_pair(pr) + (tc - 2) + 1)
                                    for k in range(8):
                                        inc(t.matmul(
                                            QP[tc % 2][:, :], WKP[pr % 2][:, k, :],
                                            LN2FM[:, k, tc * 512:(tc + 1) * 512],
                                            start=(k == 0), stop=(k == 7)), "ts")

                            @blk.scalar
                            def _(s, pr=pr):
                                for tc in range(4):
                                    s.wait_ge(sem["ts"], t_pair(pr) + (tc + 1) * 8)
                                    inc(s.activation(QF[:, tc * 512:(tc + 1) * 512],
                                                     QP[tc % 2][:, :], AF.Copy,
                                                     scale=0.125), "ss")

                            @blk.vector
                            def _(v, pr=pr):
                                for tc in range(4):
                                    v.wait_ge(sem["ts"], t_pair(pr) + 32 + (tc + 1) * 8)
                                    inc(v.tensor_copy(KF[:, tc * 512:(tc + 1) * 512],
                                                      QP[tc % 2][:, :]), "vs")

                        h1, h2 = 2 * pr, 2 * pr + 1
                        with nc.Block() as blk:
                            @blk.tensor
                            def _(t, pr=pr, h1=h1, h2=h2):
                                tb = t_pair(pr) + 64
                                sB = s_pair(pr) + 4
                                vB = v_pair(pr) + 4
                                t.wait_ge(sem["ss"], sB)
                                t.wait_ge(sem["vs"], vB)
                                m = 0
                                for qc in range(4):
                                    d = 0
                                    for kt in range(NKT[qc]):
                                        if m >= 1:
                                            t.wait_ge(sem["ss"], sB + 2 * m)
                                        inc(t.matmul(
                                            SP[0][:, :],
                                            KF[0:64, kt * 128:(kt + 1) * 128],
                                            QF[0:64, qc * 512:(qc + 1) * 512],
                                            start=True, stop=True), "ts")
                                        inc(t.matmul(
                                            SP[1][:, :],
                                            KF[64:128, kt * 128:(kt + 1) * 128],
                                            QF[64:128, qc * 512:(qc + 1) * 512],
                                            start=True, stop=True,
                                            tile_position=(64, 0)), "ts")
                                        t.wait_ge(sem["ss"], sB + 2 * (m + 1))
                                        if kt >= 4 * qc:
                                            d += 1
                                            t.wait_ge(sem["vs"], vB + qc * 12 + 2 * d)
                                        if kt == 0:
                                            t.wait_ge(sem["vs"], vB + qc * 12)
                                        inc(t.matmul(
                                            OP1[0:65, :],
                                            VT[:, kt, h1 * 65:h1 * 65 + 65],
                                            PT[0][:, :], start=(kt == 0),
                                            stop=(kt == NKT[qc] - 1)), "ts")
                                        inc(t.matmul(
                                            OP2[64:128, :],
                                            VT[:, kt, h2 * 65:h2 * 65 + 64],
                                            PT[1][:, :], start=(kt == 0),
                                            stop=(kt == NKT[qc] - 1),
                                            tile_position=(0, 64)), "ts")
                                        inc(t.matmul(
                                            LP[0:1, :],
                                            VT[:, kt, h1 * 65 + 64:h1 * 65 + 65],
                                            PT[1][:, :], start=(kt == 0),
                                            stop=(kt == NKT[qc] - 1)), "ts")
                                        m += 1

                            @blk.scalar
                            def _(s, pr=pr):
                                tb = t_pair(pr) + 64
                                m = 0
                                for qc in range(4):
                                    for kt in range(NKT[qc]):
                                        s.wait_ge(sem["ts"], tb + 5 * m + 2)
                                        inc(s.activation(PT[0][:, :], SP[0][:, :],
                                                         AF.Exp), "ss")
                                        inc(s.activation(PT[1][:, :], SP[1][:, :],
                                                         AF.Exp), "ss")
                                        m += 1

                            @blk.vector
                            def _(v, pr=pr):
                                tb = t_pair(pr) + 64
                                sB = s_pair(pr) + 4
                                gB = g_pair(pr)
                                m = 0
                                for qc in range(4):
                                    for kt in range(NKT[qc]):
                                        if kt >= 4 * qc:
                                            v.wait_ge(sem["ss"], sB + 2 * (m + 1))
                                            om = kt - 4 * qc
                                            inc(v.tensor_tensor(PT[0][:, :],
                                                                PT[0][:, :],
                                                                CN["cmask"][:, om, :],
                                                                ALU.mult), "vs")
                                            inc(v.tensor_tensor(PT[1][:, :],
                                                                PT[1][:, :],
                                                                CN["cmask"][:, om, :],
                                                                ALU.mult), "vs")
                                        m += 1
                                    v.wait_ge(sem["ts"], tb + 5 * CUMKT[qc])
                                    inc(v.reciprocal(RA[0:1, :], OP1[64:65, :]), "vs")
                                    inc(v.reciprocal(RB[0:1, :], LP[0:1, :]), "vs")
                                    v.wait_ge(sem["gs"], gB + 2 * (qc + 1))
                                    n_on = pr * 4 + qc
                                    j_on = n_on % 2
                                    if n_on >= 2:
                                        v.wait_ge(sem[f"ons{j_on}"],
                                                  16 * ((n_on - 2) // 2 + 1
                                                        + ons_base[j_on]))
                                    inc(v.tensor_tensor(
                                        ONB[j_on][0:64, :],
                                        OP1[0:64, :], RCP[0:64, :], ALU.mult), "vs")
                                    inc(v.tensor_tensor(
                                        ONB[j_on][64:128, :],
                                        OP2[64:128, :], RCPB[64:128, :], ALU.mult), "vs")

                            @blk.gpsimd
                            def _(g, pr=pr):
                                vB = v_pair(pr) + 4
                                for qc in range(4):
                                    g.wait_ge(sem["vs"], vB + qc * 12 + 10)
                                    inc(g.partition_broadcast(RCP[0:64, :],
                                                              RA[0:1, :]), "gs")
                                    inc(g.partition_broadcast(RCPB[:, :],
                                                              RB[0:1, :]), "gs")
                                    n_on = pr * 4 + qc
                                    g.wait_ge(sem["vs"], vB + qc * 12 + 12)
                                    dinc(g.dma_start(
                                        out=onorm_d[pr, :,
                                                    qc * 512:(qc + 1) * 512],
                                        in_=ONB[n_on % 2][:, :]),
                                        f"ons{n_on % 2}")
                fence()
                if _au < 4:
                    return

                tp0, vp0, dwp0 = cnt["ts"], cnt["vs"], cnt["dwa"]
                with ExitStack() as ph:
                    WOP = ph.enter_context(nc.sbuf_tensor(U("WOP"), [128, 8, E], BF16))
                    OIN = [ph.enter_context(nc.sbuf_tensor(U(f"OIN{i}"), [128, 8, 128],
                                                           BF16)) for i in range(2)]
                    APP = [ph.enter_context(nc.psum_tensor(U(f"APP{i}"), [128, 512], F32))
                           for i in range(2)]
                    oin_base = [cnt["oin0"] // 16, cnt["oin1"] // 16]
                    with nc.Block() as blk:
                        @blk.gpsimd
                        def _(g):
                            dinc(g.dma_start(out=WOP[:, :, :], in_=wo_v[:, :, :]), "dwa")
                            for qt in range(NT):
                                if qt >= 2:
                                    g.wait_ge(sem["ts"], tp0 + 16 * (qt - 1))
                                dinc(g.dma_start(
                                    out=OIN[qt % 2][:, :, :],
                                    in_=onorm_d[:, :, qt * 128:(qt + 1) * 128]
                                    .rearrange("c p t -> p c t")), f"oin{qt % 2}")

                        @blk.tensor
                        def _(t):
                            t.wait_ge(sem["dwa"], dwp0 + 16)
                            n = 0
                            for qt in range(NT):
                                t.wait_ge(sem[f"oin{qt % 2}"],
                                          16 * (qt // 2 + 1 + oin_base[qt % 2]))
                                for eh in range(2):
                                    if n >= 2:
                                        t.wait_ge(sem["vs"], vp0 + n - 1)
                                    for c in range(8):
                                        inc(t.matmul(
                                            APP[n % 2][:, :],
                                            OIN[qt % 2][:, c, :],
                                            WOP[:, c, eh * 512:(eh + 1) * 512],
                                            start=(c == 0), stop=(c == 7)), "ts")
                                    n += 1

                        @blk.vector
                        def _(v):
                            n = 0
                            for qt in range(NT):
                                for eh in range(2):
                                    v.wait_ge(sem["ts"], tp0 + (n + 1) * 8)
                                    inc(v.tensor_add(
                                        RES[:, qt, eh * 512:(eh + 1) * 512],
                                        APP[n % 2][:, :],
                                        RES[:, qt, eh * 512:(eh + 1) * 512]), "vs")
                                    n += 1
                fence()

        if upto >= 3:
            moe_sublayer()
        elif upto >= 2:
            emit_ln_gate()
            emit_ags()
            fence()
        if upto >= 6:
            emit_attention()
        if upto >= 7:
            moe_sublayer(zero_tbl=True)

        with nc.Block() as blk:
            @blk.sync
            def _(sy):
                for k in ("ts", "vs", "ss", "gs", "dsc", "dg", "dat", "cc"):
                    if cnt[k]:
                        sy.wait_ge(sem[k], cnt[k])
                sy.wait_ge(sem["dx"], dx_done)
                dinc(sy.dma_start(out=y_tm, in_=RES[:, :, :]), "dout")
                sy.wait_ge(sem["dout"], cnt["dout"])

        dump_outs = []
        for nm in dumps:
            ap, shape, dt = REG[nm]
            od = nc.dram_tensor("dump_" + nm, shape, dt, kind="ExternalOutput").ap()
            dump_outs.append((od, ap))
        if dump_outs:
            with nc.Block() as blk:
                @blk.sync
                def _(sy):
                    for od, ap in dump_outs:
                        dinc(sy.dma_start(out=od[...], in_=ap), "dout")
                    sy.wait_ge(sem["dout"], cnt["dout"])

    mybir.codegen_inst_isa_subclasses(nc)
    return nc


def kernel(**inputs):
    per_core = _host_pack(inputs)
    nc = build_program(upto=7)
    res = run_bass_kernel_spmd(nc, per_core, core_ids=list(range(NCORE)))
    out = np.stack([np.asarray(res.results[i]["y"]).reshape(S, E) for i in range(NCORE)])
    return out.astype(np.float32)


# version 02
# speedup vs baseline: 1.0560x; 1.0560x over previous
import numpy as np
import ml_dtypes
import concourse.bass as bass
import concourse.mybir as mybir
from concourse.bass_utils import run_bass_kernel_spmd
from concourse.library_config import mlp as _mlplib
from contextlib import ExitStack

F32 = mybir.dt.float32
BF16 = mybir.dt.bfloat16
I16 = mybir.dt.int16
U8 = mybir.dt.uint8
AF = mybir.ActivationFunctionType
ALU = mybir.AluOpType
AX = mybir.AxisListType
_DETECT_RACES = True

NCORE = 8
B, S, E = 8, 2048, 1024
H, DH = 16, 64
NE, FF, CAP = 8, 4096, 2560
T = B * S
TOK = S
NT = TOK // 128
NP = T // 128
TRASH = CAP
TBL_ROWS = CAP + 256
CW = CAP // 16
EPS = 1e-5
SCH = 512
NSC = CAP // SCH
W1PF = 256
NW1P = FF // W1PF
NKT = [4, 8, 12, 16]
CUMKT = [4, 12, 24, 40]

_TPERM = [4 * (j % 8) + j // 8 for j in range(32)]
TWALK = []
_seen = [False] * 32
for _s0 in range(32):
    _c = _s0
    while not _seen[_c]:
        _seen[_c] = True
        TWALK.append(_c)
        _c = _TPERM[_c]


def _host_pack(inputs):
    x = np.asarray(inputs["L0_attn_x"])
    wqkv = np.asarray(inputs["L0_attn_w_qkv"])
    wout = np.asarray(inputs["L0_attn_w_out"])
    gw = np.asarray(inputs["L0_ffn_gate_w"])
    gb = np.asarray(inputs["L0_ffn_gate_b"])
    w1 = np.asarray(inputs["L0_ffn_w1"])
    b1 = np.asarray(inputs["L0_ffn_b1"])
    w2 = np.asarray(inputs["L0_ffn_w2"])
    b2 = np.asarray(inputs["L0_ffn_b2"])

    wq = np.stack([np.concatenate([wqkv[:, 2 * p, 0:64], wqkv[:, 2 * p + 1, 0:64]], 1)
                   for p in range(8)]).astype(np.float32)
    wk = np.stack([np.concatenate([wqkv[:, 2 * p, 64:128], wqkv[:, 2 * p + 1, 64:128]], 1)
                   for p in range(8)]).astype(np.float32)
    wv = np.stack([np.concatenate([wqkv[:, 8 * g + h, 128:192] for h in range(8)], 1)
                   for g in range(2)]).astype(np.float32)
    wo = np.ascontiguousarray(wout.reshape(H * DH, E)).astype(np.float32)

    gwT = np.ascontiguousarray(gw.T.reshape(8, 128, 8).transpose(1, 0, 2)).astype(np.float32)
    gb_dev = np.tile(np.asarray(gb)[None, :], (128, 1)).astype(np.float32)
    b1col = np.ascontiguousarray(b1.reshape(NE, 32, 128).transpose(0, 2, 1)).astype(np.float32)
    b2rep = np.tile(np.asarray(b2)[:, None, :], (1, 128, 1)).astype(np.float32)

    ident_bf = np.eye(128).astype(ml_dtypes.bfloat16)
    ident_f32 = np.eye(128).astype(np.float32)
    trils = np.triu(np.ones((128, 128)), 1).astype(ml_dtypes.bfloat16)
    iota8 = np.tile(np.arange(8, dtype=np.float32)[None, :], (128, 1))
    rowiota = (np.arange(16, dtype=np.float32)[:, None] * CW
               + np.arange(CW, dtype=np.float32)[None, :]).astype(np.float32)
    zeros16 = np.zeros((16, CW), np.float32)
    ones128 = np.ones((128, 1), dtype=ml_dtypes.bfloat16)
    th16 = np.tile((160.0 * np.arange(1, 17, dtype=np.float32))[None, :], (128, 1))
    gid128 = (np.arange(128, dtype=np.float32)[:, None]
              + 128.0 * np.arange(128, dtype=np.float32)[None, :])
    zcol = np.zeros((128, 128), np.float32)
    kk = np.arange(128)[:, None]
    qq = np.arange(512)[None, :]
    cmask = np.stack([(qq >= kk + 128 * o) for o in range(4)]).astype(ml_dtypes.bfloat16)

    per_core = []
    for ci in range(NCORE):
        esel = np.zeros((128, 16), np.float32)
        for m in range(16):
            esel[16 * ci + m, m] = 1.0
        per_core.append({
            "x_my": np.ascontiguousarray(x[ci]).astype(np.float32),
            "w1_my": np.ascontiguousarray(w1[ci]).astype(np.float32),
            "w2_my": np.ascontiguousarray(w2[ci]).astype(np.float32),
            "b1_my": b1col[ci], "b2_my": b2rep[ci],
            "gwT": gwT, "gb": gb_dev,
            "wq": wq, "wk": wk, "wv": wv, "wo": wo,
            "ident_bf": ident_bf, "ident_f32": ident_f32, "trils": trils,
            "iota8": iota8, "rowiota": rowiota, "zeros16": zeros16, "zcol": zcol,
            "cmask": np.ascontiguousarray(cmask),
            "gid128": gid128, "ones128": ones128, "th16": th16,
            "mye": np.full((128, 1), float(ci), np.float32),
            "myoh": np.eye(8, dtype=np.float32)[ci:ci + 1].copy(),
            "esel": esel,
        })
    return per_core


def build_program(upto=99, dumps=()):
    nc = bass.Bass("TRN2", target_bir_lowering=False, detect_race_conditions=_DETECT_RACES)
    X = {}

    def ein(name, shape, dt):
        X[name] = nc.dram_tensor(name, shape, dt, kind="ExternalInput").ap()

    for name, shape, dt in [
            ("x_my", [TOK, E], F32), ("w1_my", [E, FF], F32), ("w2_my", [FF, E], F32),
            ("b1_my", [128, 32], F32), ("b2_my", [128, E], F32),
            ("gwT", [128, 8, 8], F32), ("gb", [128, 8], F32),
            ("wq", [8, E, 128], F32), ("wk", [8, E, 128], F32),
            ("wv", [2, E, 512], F32), ("wo", [E, E], F32),
            ("ident_bf", [128, 128], BF16), ("ident_f32", [128, 128], F32),
            ("trils", [128, 128], BF16), ("iota8", [128, 8], F32),
            ("rowiota", [16, CW], F32), ("zeros16", [16, CW], F32),
            ("zcol", [128, 128], F32), ("cmask", [4, 128, 512], BF16),
            ("mye", [128, 1], F32), ("myoh", [1, 8], F32), ("esel", [128, 16], F32),
            ("gid128", [128, 128], F32), ("ones128", [128, 1], BF16),
            ("th16", [128, 16], F32)]:
        ein(name, shape, dt)

    y = nc.dram_tensor("y", [TOK, E], F32, kind="ExternalOutput").ap()

    w1b = nc.dram_tensor("w1b", [E, FF], BF16).ap()
    w2b = nc.dram_tensor("w2b", [FF, E], BF16).ap()
    ln_in = nc.dram_tensor("ln_in", [TOK, E], BF16).ap()
    ln_all = nc.dram_tensor("ln_all", [T, E], BF16).ap()
    lg_in = nc.dram_tensor("lg_in", [TOK, 8], F32).ap()
    lg_all = nc.dram_tensor("lg_all", [T, 8], F32).ap()
    eo_in = nc.dram_tensor("eo_in", [CAP, E], F32).ap()
    eo_all = nc.dram_tensor("eo_all", [NE * CAP, E], F32).ap()
    tbl = nc.dram_tensor("tbl", [TBL_ROWS, 64], F32).ap()
    dscat = nc.dram_tensor("dscat", [16, NP * 8], I16).ap()
    onorm_d = nc.dram_tensor("onorm_d", [8, 128, TOK], BF16).ap()
    down = nc.dram_tensor("down", [16, 128], I16).ap()

    es = ExitStack()
    sem, cnt = {}, {}

    def mksem(*names):
        for n in names:
            sem[n] = es.enter_context(nc.semaphore(n))
            cnt[n] = 0

    def inc(inst, name, by=1):
        cnt[name] += by
        return inst.then_inc(sem[name], by)

    def dinc(inst, name):
        return inc(inst, name, 16)

    REG = {}
    NREG = {}
    _uqc = [0]

    def U(nm):
        _uqc[0] += 1
        return f"{nm}_u{_uqc[0]}"

    with es:
        sb = nc.alloc_sbuf_tensor
        RES = sb("RES", [128, NT, E], F32)
        CN = {}
        for nm, shape, dt in [("ident_bf", [128, 128], BF16), ("ident_f32", [128, 128], F32),
                              ("trils", [128, 128], BF16), ("iota8", [128, 8], F32),
                              ("rowiota", [16, CW], F32), ("zeros16", [16, CW], F32),
                              ("zcol", [128, 128], F32), ("cmask", [128, 4, 512], BF16),
                              ("gid128", [128, 128], F32), ("ones128", [128, 1], BF16),
                              ("th16", [128, 16], F32),
                              ("mye", [128, 1], F32), ("myoh", [1, 8], F32),
                              ("esel", [128, 16], F32), ("gwT", [128, 8, 8], F32),
                              ("gb", [128, 8], F32), ("b1c", [128, 32], F32),
                              ("b2r", [128, E], F32)]:
            CN[nm] = sb("c_" + nm, shape, dt)
        TOTB = sb("TOTB", [128, 8], BF16)
        EXCL = sb("EXCL", [128, 8], F32)
        CNTS = sb("CNTS", [1, 8], F32)
        CNT1 = sb("CNT1", [16, 1], F32)
        CNTsc = sb("CNTsc", [1, 1], F32)
        coefT = sb("coefT", [128, 16], F32)
        ownsl = sb("ownsl", [16, 128], F32)
        idxown = sb("idxown", [128, 128], I16)
        idxg = sb("idxg", [128, CW], I16)
        idg16 = sb("idg16", [16, CW], I16)
        selg = sb("selg", [16, CW], F32)
        maskv = sb("maskv", [16, CW], U8)
        LGT = sb("LGT", [128, NT, 8], F32)
        SQT = [sb(f"SQT{i}", [128, E], F32) for i in range(2)]
        STAT = sb("STAT", [128, NT, 8], F32)
        LNT = [sb(f"LNT{i}", [128, E], F32) for i in range(2)]
        LNB = [sb(f"LNB{i}", [128, E], BF16) for i in range(2)]
        idxsc = sb("idxsc", [128, NP * 8], I16)
        REG.update(lgt=(LGT[:, :, :], [128, NT, 8], F32),
                   cnts=(CNTS[:, :], [1, 8], F32), coeft=(coefT[:, :], [128, 16], F32),
                   ownsl=(ownsl[:, :], [16, 128], F32),
                   idg16=(idg16[:, :], [16, CW], I16),
                   idxown=(idxown[:, :], [128, 128], I16),
                   res0=(RES[:, 0:2, :], [128, 2, E], F32))
        mksem("dx", "dw", "dwa", "dsc", "dg", "cc", "ts", "vs", "ss", "gs",
              "dat", "dout", "lnr0", "lnr1", "w1s0", "w1s1", "w2s0", "w2s1",
              "eots0", "eots1", "eots2", "eots3", "wqs0", "wqs1",
              "ons0", "ons1", "oin0", "oin1")

        x_tm = X["x_my"].rearrange("(a p) e -> p a e", p=128)
        ln_in_tm = ln_in.rearrange("(a p) e -> p a e", p=128)
        lg_in_tm = lg_in.rearrange("(a p) e -> p a e", p=128)
        y_tm = y.rearrange("(a p) e -> p a e", p=128)
        w1b_v = w1b.rearrange("(k p) f -> p k f", p=128)
        wq_v = X["wq"].rearrange("h (k p) f -> h p k f", p=128)
        wk_v = X["wk"].rearrange("h (k p) f -> h p k f", p=128)
        wv_v = X["wv"].rearrange("g (k p) f -> g p k f", p=128)
        wo_v = X["wo"].rearrange("(k p) f -> p k f", p=128)

        with ExitStack() as p0s:
            WCV = p0s.enter_context(nc.sbuf_tensor(U("WCV"), [128, FF], BF16))
            blk = p0s.enter_context(nc.Block())
            _ = blk

            @blk.sync
            def _(sy):
                dinc(sy.dma_start(out=RES[:, :, :], in_=x_tm), "dx")
                for nm in ("ident_bf", "ident_f32", "trils", "iota8", "rowiota",
                           "zeros16", "zcol", "gid128", "ones128", "th16", "mye", "myoh", "esel",
                           "gwT", "gb"):
                    dinc(sy.dma_start(out=CN[nm][...], in_=X[nm][...]), "dx")
                dinc(sy.dma_start(out=CN["cmask"][:, :, :],
                                  in_=X["cmask"].rearrange("o p q -> p o q")), "dx")
                dinc(sy.dma_start(out=CN["b1c"][:, :], in_=X["b1_my"][:, :]), "dx")
                dinc(sy.dma_start(out=CN["b2r"][:, :], in_=X["b2_my"][:, :]), "dx")

            @blk.gpsimd
            def _(g):
                g.load_library(_mlplib)
                for v in (896, 256, 1024, 512):
                    NREG[v] = g.alloc_register(f"nreg{v}")
                    g.reg_mov(NREG[v], v)
                g.wait_ge(sem["dx"], cnt["dx"])
                for zk in range(TBL_ROWS // 256):
                    dinc(g.dma_start(out=tbl[zk * 256:(zk + 1) * 256, :],
                                     in_=CN["zcol"][:, :]), "dw")
                for piece in range(8):
                    dinc(g.dma_start(out=WCV[:, :],
                                     in_=X["w1_my"][piece * 128:(piece + 1) * 128, :]), "dw")
                    g.wait_ge(sem["dw"], cnt["dw"])
                    dinc(g.dma_start(out=w1b[piece * 128:(piece + 1) * 128, :],
                                     in_=WCV[:, :]), "dw")
                    g.wait_ge(sem["dw"], cnt["dw"])
                for piece in range(32):
                    dinc(g.dma_start(out=WCV[:, 0:E],
                                     in_=X["w2_my"][piece * 128:(piece + 1) * 128, :]), "dw")
                    g.wait_ge(sem["dw"], cnt["dw"])
                    dinc(g.dma_start(out=w2b[piece * 128:(piece + 1) * 128, :],
                                     in_=WCV[:, 0:E]), "dw")
                    g.wait_ge(sem["dw"], cnt["dw"])
        wconv_done = cnt["dw"]
        dx_done = cnt["dx"]

        def fence():
            snap = {k: v for k, v in cnt.items() if v and k != "dout"}
            with nc.Block() as blk:
                for eng_name in ("sync", "gpsimd", "tensor", "scalar", "vector"):
                    def body(eng, _snap=snap):
                        for k, v in _snap.items():
                            eng.wait_ge(sem[k], v)
                    getattr(blk, eng_name)(body)

        def emit_ln_gate():
            t0, v0, s0, d0 = cnt["ts"], cnt["vs"], cnt["ss"], cnt["dsc"]
            lnr_base = [cnt["lnr0"] // 16, cnt["lnr1"] // 16]
            with ExitStack() as ph:
                TPS = [ph.enter_context(nc.psum_tensor(U(f"TPS{i}"), [128, 8, 128], F32))
                       for i in range(2)]
                LNFM = [ph.enter_context(nc.sbuf_tensor(U(f"LNFM{i}"),
                                                        [128, 8, 128], F32))
                        for i in range(2)]
                LGP = [ph.enter_context(nc.psum_tensor(U(f"LGP{i}"), [128, 8], F32))
                       for i in range(2)]
                with nc.Block() as blk:
                    @blk.scalar
                    def _(s_):
                        s_.wait_ge(sem["dx"], dx_done)
                        for i in range(NT):
                            if i >= 2:
                                s_.wait_ge(sem["vs"], v0 + 11 * (i - 2) + 2)
                            inc(s_.activation(SQT[i % 2][:, :], RES[:, i, :],
                                              AF.Square), "ss")
                            s_.wait_ge(sem["vs"], v0 + 11 * i + 6)
                            inc(s_.activation(STAT[:, i, 6:7], STAT[:, i, 5:6],
                                              AF.Sqrt), "ss")

                    @blk.vector
                    def _(v):
                        v.wait_ge(sem["dx"], dx_done)
                        for i in range(NT):
                            inc(v.reduce_sum(STAT[:, i, 0:1], RES[:, i, :],
                                             axis=AX.X), "vs")
                            v.wait_ge(sem["ss"], s0 + 2 * i + 1)
                            inc(v.reduce_sum(STAT[:, i, 1:2], SQT[i % 2][:, :],
                                             axis=AX.X), "vs")
                            v.wait_ge(sem["vs"], cnt["vs"])
                            inc(v.tensor_scalar_mul(STAT[:, i, 2:3], STAT[:, i, 0:1],
                                                    1.0 / E), "vs")
                            v.wait_ge(sem["vs"], cnt["vs"])
                            inc(v.tensor_mul(STAT[:, i, 3:4], STAT[:, i, 2:3],
                                             STAT[:, i, 2:3]), "vs")
                            v.wait_ge(sem["vs"], cnt["vs"])
                            inc(v.scalar_tensor_tensor(STAT[:, i, 4:5], STAT[:, i, 1:2],
                                                       1.0 / E, STAT[:, i, 3:4],
                                                       ALU.mult, ALU.subtract), "vs")
                            v.wait_ge(sem["vs"], cnt["vs"])
                            inc(v.tensor_scalar_add(STAT[:, i, 5:6], STAT[:, i, 4:5],
                                                    EPS), "vs")
                            v.wait_ge(sem["ss"], s0 + 2 * i + 2)
                            inc(v.reciprocal(STAT[:, i, 7:8], STAT[:, i, 6:7]), "vs")
                            v.wait_ge(sem["vs"], cnt["vs"])
                            inc(v.scalar_tensor_tensor(STAT[:, i, 3:4], STAT[:, i, 2:3],
                                                       -1.0, STAT[:, i, 7:8],
                                                       ALU.mult, ALU.mult), "vs")
                            v.wait_ge(sem["vs"], cnt["vs"])
                            if i >= 2:
                                v.wait_ge(sem[f"lnr{i % 2}"], 16 * ((i - 2) // 2 + 1 + lnr_base[i % 2]))
                                v.wait_ge(sem["ts"], t0 + 16 * (i - 2) + 8)
                            inc(v.tensor_scalar(LNT[i % 2][:, :], RES[:, i, :],
                                                STAT[:, i, 7:8], STAT[:, i, 3:4],
                                                ALU.mult, ALU.add), "vs")
                            v.wait_ge(sem["ts"], t0 + 16 * i + 8)
                            inc(v.tensor_copy(LNFM[i % 2][:, :, :],
                                              TPS[i % 2][:, :, :]), "vs")
                            v.wait_ge(sem["ts"], t0 + 16 * i + 16)
                            inc(v.tensor_add(LGT[:, i, :], LGP[i % 2][:, :],
                                             CN["gb"][:, :]), "vs")

                    @blk.gpsimd
                    def _(g):
                        for i in range(NT):
                            g.wait_ge(sem["vs"], v0 + 11 * i + 9)
                            dinc(g.dma_start(out=ln_in_tm[:, i, :],
                                             in_=LNT[i % 2][:, :]), f"lnr{i % 2}")

                    @blk.tensor
                    def _(t):
                        for i in range(NT):
                            t.wait_ge(sem["vs"], v0 + 11 * i + 9)
                            if i >= 2:
                                t.wait_ge(sem["vs"], v0 + 11 * (i - 2) + 10)
                            for k in range(8):
                                inc(t.transpose(TPS[i % 2][:, k, :],
                                                LNT[i % 2][:, k * 128:(k + 1) * 128],
                                                CN["ident_f32"][:, :]), "ts")
                            t.wait_ge(sem["vs"], v0 + 11 * i + 10)
                            if i >= 2:
                                t.wait_ge(sem["vs"], v0 + 11 * (i - 2) + 11)
                            for k in range(8):
                                inc(t.matmul(LGP[i % 2][:, :], LNFM[i % 2][:, k, :],
                                             CN["gwT"][:, k, :],
                                             start=(k == 0), stop=(k == 7)), "ts")
            with nc.Block() as blk:
                @blk.gpsimd
                def _(g):
                    g.wait_ge(sem["vs"], cnt["vs"])
                    dinc(g.dma_start(out=lg_in_tm[:, :, :], in_=LGT[:, :, :]), "dsc")

        def emit_ags():
            with nc.Block() as blk:
                @blk.gpsimd
                def _(g):
                    g.wait_ge(sem["dsc"], cnt["dsc"])
                    g.wait_ge(sem["lnr0"], cnt["lnr0"])
                    g.wait_ge(sem["lnr1"], cnt["lnr1"])
                    inc(g.collective_compute("AllGather", ALU.bypass,
                                             replica_groups=[list(range(NCORE))],
                                             ins=[lg_in[:, :].opt()],
                                             outs=[lg_all[:, :].opt()]), "cc")
                    inc(g.collective_compute("AllGather", ALU.bypass,
                                             replica_groups=[list(range(NCORE))],
                                             ins=[ln_in[:, :].opt()],
                                             outs=[ln_all[:, :].opt()]), "cc")

        def emit_routing_scatter():
            with ExitStack() as ph:
                LG = ph.enter_context(nc.sbuf_tensor(U("LG"), [128, NP, 8], F32))
                EXV = ph.enter_context(nc.sbuf_tensor(U("EXV"), [128, NP, 8], F32))
                OHV = ph.enter_context(nc.sbuf_tensor(U("OHV"), [128, NP, 8], F32))
                INCL = ph.enter_context(nc.sbuf_tensor(U("INCL"), [128, NP, 8], F32))
                SCR8 = ph.enter_context(nc.sbuf_tensor(U("SCR8"), [128, NP, 8], F32))
                RM = ph.enter_context(nc.sbuf_tensor(U("RM"), [128, NP], F32))
                RS = ph.enter_context(nc.sbuf_tensor(U("RS"), [128, NP], F32))
                TP = ph.enter_context(nc.sbuf_tensor(U("TP"), [128, NP], F32))
                ET = ph.enter_context(nc.sbuf_tensor(U("ET"), [128, NP], F32))
                POS = ph.enter_context(nc.sbuf_tensor(U("POS"), [128, NP], F32))
                KEEP = ph.enter_context(nc.sbuf_tensor(U("KEEP"), [128, NP], F32))
                COEF = ph.enter_context(nc.sbuf_tensor(U("COEF"), [128, NP], F32))
                SA = ph.enter_context(nc.sbuf_tensor(U("SA"), [128, NP], F32))
                SB_ = ph.enter_context(nc.sbuf_tensor(U("SB_"), [128, NP], F32))
                gidrep = ph.enter_context(nc.sbuf_tensor(U("gidrep"), [128, NP, 64], F32))
                SB2 = ph.enter_context(nc.sbuf_tensor(U("SB2"), [128, NP], F32))
                OW2 = ph.enter_context(nc.sbuf_tensor(U("OW2"), [16, 128], F32))
                TMP16 = ph.enter_context(nc.sbuf_tensor(U("TMP16"), [128, NP, 16], F32))
                REG.update(et=(ET[:, :], [128, NP], F32), pos=(POS[:, :], [128, NP], F32),
                           keep=(KEEP[:, :], [128, NP], F32),
                           coef=(COEF[:, :], [128, NP], F32),
                           tp=(TP[:, :], [128, NP], F32))
                iota_bc = CN["iota8"][:, :].unsqueeze(1).broadcast_to([128, NP, 8])
                with ExitStack() as php:
                    EXP_ = php.enter_context(nc.psum_tensor(U("EXCP"), [128, 8], F32))
                    CNP = php.enter_context(nc.psum_tensor(U("CNP"), [1, 8], F32))
                    OSP = php.enter_context(nc.psum_tensor(U("OSP"), [16, 128], F32))
                    CTP = php.enter_context(nc.psum_tensor(U("CTP"), [128, 16], F32))
                    with nc.Block() as blk:
                        @blk.sync
                        def _(sy):
                            sy.wait_ge(sem["cc"], cnt["cc"])
                            dinc(sy.dma_start(
                                out=LG[:, :, :],
                                in_=lg_all.rearrange("(p a) e -> p a e", p=128)), "dat")

                        @blk.scalar
                        def _(s):
                            s.wait_ge(sem["dat"], cnt["dat"])
                            inc(s.activation(EXV[:, :, :], LG[:, :, :], AF.Exp), "ss")

                        @blk.vector
                        def _(v):
                            v.wait_ge(sem["ss"], cnt["ss"])
                            inc(v.reduce_max(RM[:, :], EXV[:, :, :], axis=AX.X), "vs")
                            inc(v.reduce_sum(RS[:, :], EXV[:, :, :], axis=AX.X), "vs")
                            v.wait_ge(sem["vs"], cnt["vs"])
                            inc(v.reciprocal(TP[:, :], RS[:, :]), "vs")
                            v.wait_ge(sem["vs"], cnt["vs"])
                            inc(v.tensor_tensor(TP[:, :], TP[:, :], RM[:, :],
                                                ALU.mult), "vs")
                            rm_bc = RM[:, :].unsqueeze(2).broadcast_to([128, NP, 8])
                            v.wait_ge(sem["vs"], cnt["vs"])
                            inc(v.tensor_tensor(SCR8[:, :, :], EXV[:, :, :], rm_bc,
                                                ALU.is_lt), "vs")
                            v.wait_ge(sem["vs"], cnt["vs"])
                            inc(v.scalar_tensor_tensor(SCR8[:, :, :], SCR8[:, :, :], 9.0,
                                                       iota_bc, ALU.mult, ALU.add), "vs")
                            v.wait_ge(sem["vs"], cnt["vs"])
                            inc(v.tensor_reduce(ET[:, :], SCR8[:, :, :], axis=AX.X,
                                                op=ALU.min), "vs")
                            et_bc = ET[:, :].unsqueeze(2).broadcast_to([128, NP, 8])
                            v.wait_ge(sem["vs"], cnt["vs"])
                            inc(v.tensor_tensor(OHV[:, :, :], iota_bc, et_bc,
                                                ALU.is_equal), "vs")
                            v.wait_ge(sem["vs"], cnt["vs"])
                            for e in range(8):
                                inc(v.tensor_tensor_scan(INCL[:, :, e], OHV[:, :, e],
                                                         CN["zcol"][:, :], 0.0,
                                                         ALU.add, ALU.add), "vs")
                            v.wait_ge(sem["vs"], cnt["vs"])
                            inc(v.tensor_copy(TOTB[:, :], INCL[:, NP - 1, :]), "vs")

                        @blk.tensor
                        def _(t):
                            t.wait_ge(sem["vs"], cnt["vs"])
                            inc(t.matmul(EXP_[:, :], CN["trils"][:, :], TOTB[:, :],
                                         start=True, stop=True), "ts")
                            inc(t.matmul(CNP[:, :], CN["ones128"][:, :], TOTB[:, :],
                                         start=True, stop=True), "ts")

                        @blk.vector
                        def _(v):
                            v.wait_ge(sem["ts"], cnt["ts"])
                            v.wait_ge(sem["vs"], cnt["vs"])
                            inc(v.tensor_copy(EXCL[:, :], EXP_[:, :]), "vs")
                            v.wait_ge(sem["vs"], cnt["vs"])
                            inc(v.tensor_tensor(SCR8[:, :, :], OHV[:, :, :],
                                                INCL[:, :, :], ALU.mult), "vs")
                            v.wait_ge(sem["vs"], cnt["vs"])
                            inc(v.reduce_sum(POS[:, :], SCR8[:, :, :], axis=AX.X), "vs")
                            ex_bc = EXCL[:, :].unsqueeze(1).broadcast_to([128, NP, 8])
                            v.wait_ge(sem["vs"], cnt["vs"])
                            inc(v.tensor_tensor(SCR8[:, :, :], OHV[:, :, :], ex_bc,
                                                ALU.mult), "vs")
                            v.wait_ge(sem["vs"], cnt["vs"])
                            inc(v.reduce_sum(SB_[:, :], SCR8[:, :, :], axis=AX.X), "vs")
                            v.wait_ge(sem["vs"], cnt["vs"])
                            inc(v.scalar_tensor_tensor(POS[:, :], POS[:, :], -1.0,
                                                       SB_[:, :], ALU.add, ALU.add), "vs")
                            v.wait_ge(sem["vs"], cnt["vs"])
                            inc(v.tensor_single_scalar(KEEP[:, :], POS[:, :],
                                                       float(CAP), ALU.is_lt), "vs")
                            v.wait_ge(sem["vs"], cnt["vs"])
                            inc(v.scalar_tensor_tensor(COEF[:, :], TP[:, :], 0.5,
                                                       KEEP[:, :], ALU.mult,
                                                       ALU.mult), "vs")
                            v.wait_ge(sem["vs"], cnt["vs"])
                            inc(v.tensor_scalar(SA[:, :], ET[:, :], CN["mye"][:, :],
                                                None, ALU.is_equal, ALU.bypass), "vs")
                            v.wait_ge(sem["vs"], cnt["vs"])
                            inc(v.tensor_tensor(SA[:, :], SA[:, :], KEEP[:, :],
                                                ALU.mult), "vs")
                            v.wait_ge(sem["vs"], cnt["vs"])
                            inc(v.tensor_scalar_add(SB_[:, :], POS[:, :],
                                                    -float(TRASH)), "vs")
                            v.wait_ge(sem["vs"], cnt["vs"])
                            inc(v.tensor_tensor(SB_[:, :], SB_[:, :], SA[:, :],
                                                ALU.mult), "vs")
                            v.wait_ge(sem["vs"], cnt["vs"])
                            inc(v.tensor_scalar_add(SB_[:, :], SB_[:, :],
                                                    float(TRASH)), "vs")
                            v.wait_ge(sem["vs"], cnt["vs"])
                            inc(v.tensor_tensor(SA[:, :], POS[:, :], KEEP[:, :],
                                                ALU.mult), "vs")
                            v.wait_ge(sem["vs"], cnt["vs"])
                            inc(v.tensor_tensor(
                                TMP16[:, :, :],
                                SA[:, :].unsqueeze(2).broadcast_to([128, NP, 16]),
                                CN["th16"][:, :].unsqueeze(1)
                                .broadcast_to([128, NP, 16]), ALU.is_ge), "vs")
                            v.wait_ge(sem["vs"], cnt["vs"])
                            inc(v.reduce_sum(RS[:, :], TMP16[:, :, :], axis=AX.X), "vs")
                            v.wait_ge(sem["vs"], cnt["vs"])
                            inc(v.scalar_tensor_tensor(RM[:, :], RS[:, :], -float(CW),
                                                       SA[:, :], ALU.mult, ALU.add), "vs")
                            v.wait_ge(sem["vs"], cnt["vs"])
                            inc(v.scalar_tensor_tensor(SA[:, :], ET[:, :], float(CAP),
                                                       RS[:, :], ALU.mult, ALU.add), "vs")
                            v.wait_ge(sem["vs"], cnt["vs"])
                            inc(v.scalar_tensor_tensor(SA[:, :], RM[:, :], 16.0,
                                                       SA[:, :], ALU.mult, ALU.add), "vs")

                        @blk.tensor
                        def _(t):
                            t.wait_ge(sem["vs"], cnt["vs"])
                            inc(t.matmul(OSP[:, :], CN["esel"][:, :], SA[:, :],
                                         start=True, stop=True), "ts")
                            inc(t.matmul(CTP[:, :], COEF[:, :], CN["esel"][:, :],
                                         start=True, stop=True), "ts")

                        @blk.vector
                        def _(v):
                            v.wait_ge(sem["ts"], cnt["ts"])
                            inc(v.tensor_copy(ownsl[:, :], OSP[:, :]), "vs")
                            inc(v.tensor_copy(coefT[:, :], CTP[:, :]), "vs")
                            inc(v.tensor_copy(CNTS[:, :], CNP[:, :]), "vs")
                            v.wait_ge(sem["vs"], cnt["vs"])
                            inc(v.tensor_tensor(CNTS[:, :], CNTS[:, :],
                                                CN["myoh"][:, :], ALU.mult), "vs")
                            v.wait_ge(sem["vs"], cnt["vs"])
                            inc(v.reduce_sum(CNTsc[:, :], CNTS[:, :], axis=AX.X), "vs")
                            inc(v.tensor_copy(
                                gidrep[:, :, :],
                                CN["gid128"][:, :].unsqueeze(2)
                                .broadcast_to([128, NP, 64])), "vs")
                            v.wait_ge(sem["vs"], cnt["vs"])
                            inc(v.tensor_copy(
                                SB2[:, :].rearrange("p (q d) -> p q d", q=16),
                                SB_[:, :].rearrange("p (d q) -> p q d", q=16)), "vs")
                            inc(v.tensor_copy(
                                OW2[:, :].rearrange("p (q d) -> p q d", q=16),
                                ownsl[:, :].rearrange("p (d q) -> p q d", q=16)), "vs")

                        @blk.gpsimd
                        def _(g):
                            g.wait_ge(sem["vs"], cnt["vs"])
                            inc(g.partition_broadcast(CNT1[:, :], CNTsc[:, :]), "gs")
                            dinc(g.dma_start(
                                out=dscat.rearrange("q (p d) -> p q d", p=128),
                                in_=SB2[:, :].rearrange("p (q d) -> p q d", q=16)), "dsc")
                            dinc(g.dma_start(
                                out=down.rearrange("q (p d) -> p q d", p=16),
                                in_=OW2[:, :].rearrange("p (q d) -> p q d", q=16)), "dsc")
                            g.wait_ge(sem["dsc"], cnt["dsc"])
                            for grp in range(8):
                                dinc(g.dma_start(out=idxsc[16 * grp:16 * grp + 16, :],
                                                 in_=dscat[:, :]), "dsc")
                                dinc(g.dma_start(out=idxown[16 * grp:16 * grp + 16, :],
                                                 in_=down[:, :]), "dsc")
                            g.wait_ge(sem["dsc"], cnt["dsc"])
                            g.wait_ge(sem["vs"], cnt["vs"])
                            sc_chunks = [(ch * 896, 896) for ch in range(18)]
                            sc_chunks.append((18 * 896, T - 18 * 896))
                            for base, ncnt in sc_chunks:
                                dinc(g.dma_scatter_add(
                                    tbl[:, :],
                                    gidrep[:, base // 128:(base + ncnt) // 128, :],
                                    idxsc[:, base // 16:(base + ncnt) // 16],
                                    ncnt, NREG[ncnt], 64), "dg")
                                g.wait_ge(sem["dg"], cnt["dg"])

        def emit_table_read():
            with ExitStack() as ph:
                TRD = ph.enter_context(nc.sbuf_tensor(U("TRD"), [16, CW, 64], F32))
                with nc.Block() as blk:
                    @blk.gpsimd
                    def _(g):
                        dinc(g.dma_start(
                            out=TRD[:, :, :],
                            in_=tbl[0:CAP, :].rearrange("(q f) z -> q f z", q=16)), "dg")
                        g.wait_ge(sem["dg"], cnt["dg"])

                    @blk.vector
                    def _(v):
                        v.wait_ge(sem["dg"], cnt["dg"])
                        v.wait_ge(sem["gs"], cnt["gs"])
                        inc(v.tensor_scalar(maskv[:, :], CN["rowiota"][:, :],
                                            CNT1[:, :], None, ALU.is_lt,
                                            ALU.bypass), "vs")
                        v.wait_ge(sem["vs"], cnt["vs"])
                        inc(v.select(selg[:, :], maskv[:, :], TRD[:, :, 0],
                                     CN["zeros16"][:, :], add_drain=True), "vs")
                        v.wait_ge(sem["vs"], cnt["vs"])
                        inc(v.tensor_copy(idg16[:, :], selg[:, :]), "vs")

                    @blk.gpsimd
                    def _(g):
                        g.wait_ge(sem["vs"], cnt["vs"])
                        for grp in range(8):
                            dinc(g.dma_start(out=idxg[16 * grp:16 * grp + 16, :],
                                             in_=idg16[:, :]), "dsc")
                        g.wait_ge(sem["dsc"], cnt["dsc"])

        def emit_ffn():
            with ExitStack() as ph:
                BUF = ph.enter_context(nc.sbuf_tensor(U("BUF"), [128, CAP // 128, E], BF16))
                HFM = ph.enter_context(nc.sbuf_tensor(U("HFM"), [128, 32, SCH], BF16))
                W1P = [ph.enter_context(nc.sbuf_tensor(U(f"W1P{i}"), [128, 8, 512], BF16))
                       for i in range(2)]
                W2P = [ph.enter_context(nc.sbuf_tensor(U(f"W2P{i}"), [128, 4, 512], BF16))
                       for i in range(2)]
                EOT = [ph.enter_context(nc.sbuf_tensor(U(f"EOT{i}"), [128, 512], F32))
                       for i in range(2)]
                REG.update(buf0=(BUF[:, 0:2, :], [128, 2, E], BF16))

                with nc.Block() as blk:
                    @blk.gpsimd
                    def _(g):
                        for base, ncnt in ((0, 1024), (1024, 1024), (2048, 512)):
                            dinc(g.dma_gather(
                                BUF[:, base // 128:(base + ncnt) // 128, :],
                                ln_all[:, :], idxg[:, base // 16:(base + ncnt) // 16],
                                ncnt, NREG[ncnt], E, transpose=False), "dg")
                            g.wait_ge(sem["dg"], cnt["dg"])

                ts0, ss0, vs0 = cnt["ts"], cnt["ss"], cnt["vs"]
                import os
                if os.environ.get("KDBG"):
                    print("FFN bases: ts0", ts0, "ss0", ss0, "vs0", vs0,
                          "w1s", [cnt["w1s0"], cnt["w1s1"]],
                          "w2s", [cnt["w2s0"], cnt["w2s1"]],
                          "dg", cnt["dg"], "dw", cnt["dw"])
                w1s_base = [cnt["w1s0"] // 16, cnt["w1s1"] // 16]
                w2s_base = [cnt["w2s0"] // 16, cnt["w2s1"] // 16]
                eot_base = [cnt[f"eots{i}"] // 16 for i in range(2)]

                def ts_h(sc_, pc):
                    return ts0 + sc_ * 544 + 32 + (pc + 1) * 16

                def ts_h_half(sc_, fc):
                    return ts0 + sc_ * 544 + 32 + fc * 8 + 8

                def ts_eo(sc_, dmh, fc):
                    return ts0 + sc_ * 544 + 288 + dmh * 128 + (fc + 1) * 4

                def ss_relu(sc_, fc):
                    return ss0 + sc_ * 32 + fc + 1

                def vs_ev(sc_, k):
                    return vs0 + sc_ * 40 + 32 + k + 1

                with ExitStack() as php:
                    HP = [php.enter_context(nc.psum_tensor(U(f"HP{i}"), [128, SCH], F32))
                          for i in range(2)]
                    EP = [php.enter_context(nc.psum_tensor(U(f"EPp{i}"), [128, 512], F32))
                          for i in range(4)]
                    TPB2 = [php.enter_context(nc.psum_tensor(U(f"TPB2{i}"), [128, 8, 128],
                                                             BF16))
                            for i in range(2)]
                    with nc.Block() as blk:
                        @blk.sync
                        def _(sy):
                            sy.wait_ge(sem["dw"], wconv_done)
                            for sc_ in range(NSC):
                                for b in range(8):
                                    gb = sc_ * 8 + b
                                    if gb >= 2:
                                        psc, pb = divmod(gb - 2, 8)
                                        sy.wait_ge(sem["ts"], ts_h(psc, 2 * pb + 1))
                                    dinc(sy.dma_start(
                                        out=W1P[gb % 2][:, :, :],
                                        in_=w1b_v[:, :, b * 512:(b + 1) * 512]),
                                        f"w1s{gb % 2}")
                                for dmh in range(2):
                                    for bk in range(8):
                                        gj = sc_ * 16 + dmh * 8 + bk
                                        if gj >= 2:
                                            psc, r = divmod(gj - 2, 16)
                                            pdmh, pbk = divmod(r, 8)
                                            sy.wait_ge(sem["ts"],
                                                       ts_eo(psc, pdmh,
                                                             pbk * 4 + 3))
                                        dinc(sy.dma_start(
                                            out=W2P[gj % 2][:, :, :],
                                            in_=w2b[bk * 512:(bk + 1) * 512,
                                                    dmh * 512:(dmh + 1) * 512]
                                            .rearrange("(f p) e -> p f e", p=128)),
                                            f"w2s{gj % 2}")

                        @blk.tensor
                        def _(t):
                            t.wait_ge(sem["dg"], cnt["dg"])
                            for sc_ in range(NSC):
                                for i_, tj in enumerate(TWALK):
                                    st, dmc = divmod(tj, 8)
                                    if i_ >= 2:
                                        t.wait_ge(sem["vs"],
                                                  vs0 + sc_ * 40 + i_ - 1)
                                    inc(t.transpose(
                                        TPB2[i_ % 2][:, 0, :],
                                        BUF[:, 4 * sc_ + st,
                                            dmc * 128:(dmc + 1) * 128],
                                        CN["ident_bf"][:, :]), "ts")
                                t.wait_ge(sem["vs"], vs0 + sc_ * 40 + 32)
                                for pc in range(NW1P):
                                    gb = sc_ * 8 + pc // 2
                                    t.wait_ge(sem[f"w1s{gb % 2}"],
                                              16 * (gb // 2 + 1 + w1s_base[gb % 2]))
                                    for f1 in range(2):
                                        fc = pc * 2 + f1
                                        gfc = sc_ * 32 + fc
                                        if gfc >= 2:
                                            psc, pfc = divmod(gfc - 2, 32)
                                            t.wait_ge(sem["ss"], ss_relu(psc, pfc))
                                        off = (pc % 2) * 256 + f1 * 128
                                        for k in range(8):
                                            inc(t.matmul(
                                                HP[fc % 2][:, :],
                                                W1P[gb % 2][:, k, off:off + 128],
                                                BUF[:, 4 * sc_ + k // 2,
                                                    (k % 2) * 512:
                                                    (k % 2) * 512 + 512],
                                                start=(k == 0), stop=(k == 7)), "ts")
                                for dmh in range(2):
                                    for fc in range(32):
                                        gj = sc_ * 16 + dmh * 8 + fc // 4
                                        if fc % 4 == 0:
                                            t.wait_ge(sem[f"w2s{gj % 2}"],
                                                      16 * (gj // 2 + 1
                                                            + w2s_base[gj % 2]))
                                        t.wait_ge(sem["ss"], ss_relu(sc_, fc))
                                        if fc == 0:
                                            if dmh == 1:
                                                t.wait_ge(sem["vs"], vs_ev(sc_, 3))
                                            elif sc_ >= 1:
                                                t.wait_ge(sem["vs"], vs_ev(sc_ - 1, 7))
                                        for st in range(4):
                                            inc(t.matmul(
                                                EP[st][:, :],
                                                HFM[:, fc, st * 128:(st + 1) * 128],
                                                W2P[gj % 2][:, fc % 4, :],
                                                start=(fc == 0), stop=(fc == 31)), "ts")

                        @blk.scalar
                        def _(s):
                            for sc_ in range(NSC):
                                for fc in range(32):
                                    s.wait_ge(sem["ts"], ts_h_half(sc_, fc))
                                    if sc_ >= 1:
                                        s.wait_ge(sem["ts"], ts_eo(sc_ - 1, 1, 31))
                                    inc(s.activation(HFM[:, fc, :], HP[fc % 2][:, :],
                                                     AF.Relu,
                                                     bias=CN["b1c"][:, fc:fc + 1]), "ss")

                        @blk.vector
                        def _(v):
                            for sc_ in range(NSC):
                                for i_, tj in enumerate(TWALK):
                                    tgt = 4 * (tj % 8) + tj // 8
                                    v.wait_ge(sem["ts"],
                                              ts0 + sc_ * 544 + min(i_ + 2, 32))
                                    inc(v.tensor_copy(
                                        BUF[:, 4 * sc_ + tgt // 8,
                                            (tgt % 8) * 128:(tgt % 8) * 128 + 128],
                                        TPB2[i_ % 2][:, 0, :]), "vs")
                                for k in range(8):
                                    dmh, st = divmod(k, 4)
                                    v.wait_ge(sem["ts"], ts_eo(sc_, dmh, 31))
                                    gk = sc_ * 8 + k
                                    if gk >= 2:
                                        v.wait_ge(sem[f"eots{gk % 2}"],
                                                  16 * ((gk - 2) // 2 + 1
                                                        + eot_base[gk % 2]))
                                    inc(v.tensor_add(
                                        EOT[gk % 2][:, :], EP[st][:, :],
                                        CN["b2r"][:, dmh * 512:(dmh + 1) * 512]), "vs")

                        @blk.gpsimd
                        def _(g):
                            for sc_ in range(NSC):
                                for k in range(8):
                                    dmh, st = divmod(k, 4)
                                    gk = sc_ * 8 + k
                                    g.wait_ge(sem["vs"], vs_ev(sc_, k))
                                    dinc(g.dma_start(
                                        out=eo_in[sc_ * SCH + st * 128:
                                                  sc_ * SCH + (st + 1) * 128,
                                                  dmh * 512:(dmh + 1) * 512],
                                        in_=EOT[gk % 2][:, :]), f"eots{gk % 2}")
                with nc.Block() as blk:
                    @blk.gpsimd
                    def _(g):
                        for i4 in range(2):
                            g.wait_ge(sem[f"eots{i4}"], cnt[f"eots{i4}"])
                        inc(g.collective_compute("AllGather", ALU.bypass,
                                                 replica_groups=[list(range(NCORE))],
                                                 ins=[eo_in[:, :].opt()],
                                                 outs=[eo_all[:, :].opt()]), "cc")

        def emit_gather_back():
            with ExitStack() as ph:
                GATH = ph.enter_context(nc.sbuf_tensor(U("GATH"), [128, NT, E], F32))
                REG.update(gath0=(GATH[:, 0:2, :], [128, 2, E], F32))
                with nc.Block() as blk:
                    @blk.gpsimd
                    def _(g):
                        g.wait_ge(sem["cc"], cnt["cc"])
                        for ch in range(2):
                            dinc(g.dma_gather(GATH[:, 8 * ch:8 * ch + 8, :],
                                              eo_all[:, :],
                                              idxown[:, 64 * ch:64 * ch + 64],
                                              1024, NREG[1024], E, transpose=False), "dg")
                            g.wait_ge(sem["dg"], cnt["dg"])

                    @blk.vector
                    def _(v):
                        v.wait_ge(sem["dg"], cnt["dg"])
                        v.wait_ge(sem["vs"], cnt["vs"])
                        for i in range(NT):
                            inc(v.scalar_tensor_tensor(RES[:, i, :], GATH[:, i, :],
                                                       coefT[:, i:i + 1], RES[:, i, :],
                                                       ALU.mult, ALU.add), "vs")

        def moe_sublayer(zero_tbl=False):
            if zero_tbl:
                with nc.Block() as blk:
                    @blk.gpsimd
                    def _(g):
                        for zk in range(TBL_ROWS // 256):
                            dinc(g.dma_start(out=tbl[zk * 256:(zk + 1) * 256, :],
                                             in_=CN["zcol"][:, :]), "dw")
                        g.wait_ge(sem["dw"], cnt["dw"])
            emit_ln_gate()
            emit_ags()
            emit_routing_scatter()
            if upto >= 4:
                emit_table_read()
            if upto >= 5:
                emit_ffn()
                emit_gather_back()
            fence()

        def emit_attention():
            import os
            _au = int(os.environ.get("ATTN_UPTO", "9"))
            with ExitStack() as att:
                LN2FM = att.enter_context(nc.sbuf_tensor(U("LN2FM"), [128, 8, TOK], BF16))
                VT = att.enter_context(nc.sbuf_tensor(U("VT"), [128, NT, 1040], BF16))
                REG.update(ln2fm0=(LN2FM[:, :, 0:256], [128, 8, 256], BF16),
                           vt0=(VT[:, 0, :], [128, 1040], BF16))

                t0, v0, s0 = cnt["ts"], cnt["vs"], cnt["ss"]
                with ExitStack() as ph:
                    TPB = [ph.enter_context(nc.psum_tensor(U(f"TPB{i}"), [128, 8, 128],
                                                           BF16))
                           for i in range(2)]
                    with nc.Block() as blk:
                        @blk.scalar
                        def _(s_):
                            for i in range(NT):
                                if i >= 2:
                                    s_.wait_ge(sem["vs"], v0 + 1 + 10 * (i - 2) + 2)
                                inc(s_.activation(SQT[i % 2][:, :], RES[:, i, :],
                                                  AF.Square), "ss")
                                s_.wait_ge(sem["vs"], v0 + 1 + 10 * i + 6)
                                inc(s_.activation(STAT[:, i, 6:7], STAT[:, i, 5:6],
                                                  AF.Sqrt), "ss")

                        @blk.vector
                        def _(v):
                            inc(v.memset(VT[:, :, :]
                                         .rearrange("p a (h z) -> p a h z", z=65)
                                         [:, :, :, 64:65], 1.0), "vs")
                            for i in range(NT):
                                inc(v.reduce_sum(STAT[:, i, 0:1], RES[:, i, :],
                                                 axis=AX.X), "vs")
                                v.wait_ge(sem["ss"], s0 + 2 * i + 1)
                                inc(v.reduce_sum(STAT[:, i, 1:2], SQT[i % 2][:, :],
                                                 axis=AX.X), "vs")
                                v.wait_ge(sem["vs"], cnt["vs"])
                                inc(v.tensor_scalar_mul(STAT[:, i, 2:3],
                                                        STAT[:, i, 0:1], 1.0 / E), "vs")
                                v.wait_ge(sem["vs"], cnt["vs"])
                                inc(v.tensor_mul(STAT[:, i, 3:4], STAT[:, i, 2:3],
                                                 STAT[:, i, 2:3]), "vs")
                                v.wait_ge(sem["vs"], cnt["vs"])
                                inc(v.scalar_tensor_tensor(STAT[:, i, 4:5],
                                                           STAT[:, i, 1:2], 1.0 / E,
                                                           STAT[:, i, 3:4],
                                                           ALU.mult, ALU.subtract), "vs")
                                v.wait_ge(sem["vs"], cnt["vs"])
                                inc(v.tensor_scalar_add(STAT[:, i, 5:6],
                                                        STAT[:, i, 4:5], EPS), "vs")
                                v.wait_ge(sem["ss"], s0 + 2 * i + 2)
                                inc(v.reciprocal(STAT[:, i, 7:8], STAT[:, i, 6:7]), "vs")
                                v.wait_ge(sem["vs"], cnt["vs"])
                                inc(v.scalar_tensor_tensor(STAT[:, i, 3:4],
                                                           STAT[:, i, 2:3], -1.0,
                                                           STAT[:, i, 7:8],
                                                           ALU.mult, ALU.mult), "vs")
                                v.wait_ge(sem["vs"], cnt["vs"])
                                if i >= 2:
                                    v.wait_ge(sem["ts"], t0 + 8 * (i - 1))
                                inc(v.tensor_scalar(LNB[i % 2][:, :], RES[:, i, :],
                                                    STAT[:, i, 7:8], STAT[:, i, 3:4],
                                                    ALU.mult, ALU.add), "vs")
                                v.wait_ge(sem["ts"], t0 + 8 * (i + 1))
                                inc(v.tensor_copy(LN2FM[:, :, i * 128:(i + 1) * 128],
                                                  TPB[i % 2][:, :, :]), "vs")

                        @blk.tensor
                        def _(t):
                            for i in range(NT):
                                t.wait_ge(sem["vs"], v0 + 1 + 10 * i + 9)
                                if i >= 2:
                                    t.wait_ge(sem["vs"], v0 + 1 + 10 * (i - 2) + 10)
                                for k in range(8):
                                    inc(t.transpose(TPB[i % 2][:, k, :],
                                                    LNB[i % 2][:, k * 128:(k + 1) * 128],
                                                    CN["ident_bf"][:, :]), "ts")
                fence()
                if _au < 2:
                    return

                t0v, v0v, dw0v = cnt["ts"], cnt["vs"], cnt["dwa"]
                with ExitStack() as ph:
                    WVP = ph.enter_context(nc.sbuf_tensor(U("WVP"), [128, 8, 512], BF16))
                    VP = [ph.enter_context(nc.psum_tensor(U(f"VP{i}"), [128, 512], F32))
                          for i in range(2)]
                    with nc.Block() as blk:
                        @blk.gpsimd
                        def _(g):
                            for grp in range(2):
                                if grp >= 1:
                                    g.wait_ge(sem["ts"], t0v + NT * 8)
                                dinc(g.dma_start(out=WVP[:, :, :],
                                                 in_=wv_v[grp, :, :, :]), "dwa")

                        @blk.tensor
                        def _(t):
                            for grp in range(2):
                                t.wait_ge(sem["dwa"], dw0v + 16 * (grp + 1))
                                for i in range(NT):
                                    gi = grp * NT + i
                                    if gi >= 2:
                                        t.wait_ge(sem["vs"], v0v + gi - 1)
                                    for k in range(8):
                                        inc(t.matmul(VP[gi % 2][:, :],
                                                     LN2FM[:, k, i * 128:(i + 1) * 128],
                                                     WVP[:, k, :],
                                                     start=(k == 0), stop=(k == 7)),
                                            "ts")

                        @blk.vector
                        def _(v):
                            for grp in range(2):
                                for i in range(NT):
                                    gi = grp * NT + i
                                    v.wait_ge(sem["ts"], t0v + (gi + 1) * 8)
                                    out = VT[:, i, :].rearrange("p (h z) -> p h z", z=65)
                                    inc(v.tensor_copy(
                                        out[:, 8 * grp:8 * grp + 8, 0:64],
                                        VP[gi % 2][:, :]
                                        .rearrange("p (h d) -> p h d", d=64)), "vs")
                fence()
                if _au < 3:
                    return

                with ExitStack() as ph:
                    QF = ph.enter_context(nc.sbuf_tensor(U("QF"), [128, TOK], BF16))
                    KF = ph.enter_context(nc.sbuf_tensor(U("KF"), [128, TOK], BF16))
                    PT = [ph.enter_context(nc.sbuf_tensor(U(f"PT{i}"), [128, 512], BF16))
                          for i in range(2)]
                    RCP = ph.enter_context(nc.sbuf_tensor(U("RCP"), [128, 512], F32))
                    RCPB = ph.enter_context(nc.sbuf_tensor(U("RCPB"), [128, 512], F32))
                    RA = ph.enter_context(nc.sbuf_tensor(U("RA"), [1, 512], F32))
                    RB = ph.enter_context(nc.sbuf_tensor(U("RB"), [1, 512], F32))
                    WQP = [ph.enter_context(nc.sbuf_tensor(U(f"WQP{i}"), [128, 8, 128],
                                                           BF16)) for i in range(2)]
                    WKP = [ph.enter_context(nc.sbuf_tensor(U(f"WKP{i}"), [128, 8, 128],
                                                           BF16)) for i in range(2)]
                    ONB = [ph.enter_context(nc.sbuf_tensor(U(f"ONB{i}"), [128, 512],
                                                           BF16)) for i in range(2)]
                    QP = [ph.enter_context(nc.psum_tensor(U(f"QPp{i}"), [128, 512], F32))
                          for i in range(2)]
                    SP = [ph.enter_context(nc.psum_tensor(U(f"SPp{i}"), [128, 512], F32))
                          for i in range(2)]
                    OP1 = ph.enter_context(nc.psum_tensor(U("OP1"), [128, 512], F32))
                    OP2 = ph.enter_context(nc.psum_tensor(U("OP2"), [128, 512], F32))
                    LP = ph.enter_context(nc.psum_tensor(U("LP"), [128, 512], F32))
                    ta0, sa0, va0, ga0 = (cnt["ts"], cnt["ss"], cnt["vs"],
                                          cnt["gs"])
                    wqs_base = [cnt["wqs0"] // 16, cnt["wqs1"] // 16]
                    ons_base = [cnt["ons0"] // 16, cnt["ons1"] // 16]
                    TPP, SPP, VPP, GPP = 264, 84, 52, 8

                    def t_pair(pr):
                        return ta0 + pr * TPP

                    def s_pair(pr):
                        return sa0 + pr * SPP

                    def v_pair(pr):
                        return va0 + pr * VPP

                    def g_pair(pr):
                        return ga0 + pr * GPP

                    npairs = 8 if _au >= 4 else min(_au - 2, 8)
                    for pr in range(npairs):
                        with nc.Block() as blk:
                            @blk.gpsimd
                            def _(g, pr=pr):
                                if pr >= 2:
                                    g.wait_ge(sem["ts"], t_pair(pr - 1))
                                dinc(g.dma_start(out=WQP[pr % 2][:, :, :],
                                                 in_=wq_v[pr, :, :, :]), f"wqs{pr % 2}")
                                dinc(g.dma_start(out=WKP[pr % 2][:, :, :],
                                                 in_=wk_v[pr, :, :, :]), f"wqs{pr % 2}")

                            @blk.tensor
                            def _(t, pr=pr):
                                t.wait_ge(sem[f"wqs{pr % 2}"],
                                          16 * (2 * (pr // 2 + 1) + wqs_base[pr % 2]))
                                for tc in range(4):
                                    if tc >= 2:
                                        t.wait_ge(sem["ss"], s_pair(pr) + (tc - 2) + 1)
                                    for k in range(8):
                                        inc(t.matmul(
                                            QP[tc % 2][:, :], WQP[pr % 2][:, k, :],
                                            LN2FM[:, k, tc * 512:(tc + 1) * 512],
                                            start=(k == 0), stop=(k == 7)), "ts")
                                for tc in range(4):
                                    if tc < 2:
                                        t.wait_ge(sem["ss"], s_pair(pr) + tc + 3)
                                    else:
                                        t.wait_ge(sem["vs"], v_pair(pr) + (tc - 2) + 1)
                                    for k in range(8):
                                        inc(t.matmul(
                                            QP[tc % 2][:, :], WKP[pr % 2][:, k, :],
                                            LN2FM[:, k, tc * 512:(tc + 1) * 512],
                                            start=(k == 0), stop=(k == 7)), "ts")

                            @blk.scalar
                            def _(s, pr=pr):
                                for tc in range(4):
                                    s.wait_ge(sem["ts"], t_pair(pr) + (tc + 1) * 8)
                                    inc(s.activation(QF[:, tc * 512:(tc + 1) * 512],
                                                     QP[tc % 2][:, :], AF.Copy,
                                                     scale=0.125), "ss")

                            @blk.vector
                            def _(v, pr=pr):
                                for tc in range(4):
                                    v.wait_ge(sem["ts"], t_pair(pr) + 32 + (tc + 1) * 8)
                                    inc(v.tensor_copy(KF[:, tc * 512:(tc + 1) * 512],
                                                      QP[tc % 2][:, :]), "vs")

                        h1, h2 = 2 * pr, 2 * pr + 1
                        with nc.Block() as blk:
                            @blk.tensor
                            def _(t, pr=pr, h1=h1, h2=h2):
                                tb = t_pair(pr) + 64
                                sB = s_pair(pr) + 4
                                vB = v_pair(pr) + 4
                                t.wait_ge(sem["ss"], sB)
                                t.wait_ge(sem["vs"], vB)
                                m = 0
                                for qc in range(4):
                                    d = 0
                                    for kt in range(NKT[qc]):
                                        if m >= 1:
                                            t.wait_ge(sem["ss"], sB + 2 * m)
                                        inc(t.matmul(
                                            SP[0][:, :],
                                            KF[0:64, kt * 128:(kt + 1) * 128],
                                            QF[0:64, qc * 512:(qc + 1) * 512],
                                            start=True, stop=True), "ts")
                                        inc(t.matmul(
                                            SP[1][:, :],
                                            KF[64:128, kt * 128:(kt + 1) * 128],
                                            QF[64:128, qc * 512:(qc + 1) * 512],
                                            start=True, stop=True,
                                            tile_position=(64, 0)), "ts")
                                        t.wait_ge(sem["ss"], sB + 2 * (m + 1))
                                        if kt >= 4 * qc:
                                            d += 1
                                            t.wait_ge(sem["vs"], vB + qc * 12 + 2 * d)
                                        if kt == 0:
                                            t.wait_ge(sem["vs"], vB + qc * 12)
                                        inc(t.matmul(
                                            OP1[0:65, :],
                                            VT[:, kt, h1 * 65:h1 * 65 + 65],
                                            PT[0][:, :], start=(kt == 0),
                                            stop=(kt == NKT[qc] - 1)), "ts")
                                        inc(t.matmul(
                                            OP2[64:128, :],
                                            VT[:, kt, h2 * 65:h2 * 65 + 64],
                                            PT[1][:, :], start=(kt == 0),
                                            stop=(kt == NKT[qc] - 1),
                                            tile_position=(0, 64)), "ts")
                                        inc(t.matmul(
                                            LP[0:1, :],
                                            VT[:, kt, h1 * 65 + 64:h1 * 65 + 65],
                                            PT[1][:, :], start=(kt == 0),
                                            stop=(kt == NKT[qc] - 1)), "ts")
                                        m += 1

                            @blk.scalar
                            def _(s, pr=pr):
                                tb = t_pair(pr) + 64
                                m = 0
                                for qc in range(4):
                                    for kt in range(NKT[qc]):
                                        s.wait_ge(sem["ts"], tb + 5 * m + 2)
                                        inc(s.activation(PT[0][:, :], SP[0][:, :],
                                                         AF.Exp), "ss")
                                        inc(s.activation(PT[1][:, :], SP[1][:, :],
                                                         AF.Exp), "ss")
                                        m += 1

                            @blk.vector
                            def _(v, pr=pr):
                                tb = t_pair(pr) + 64
                                sB = s_pair(pr) + 4
                                gB = g_pair(pr)
                                m = 0
                                for qc in range(4):
                                    for kt in range(NKT[qc]):
                                        if kt >= 4 * qc:
                                            v.wait_ge(sem["ss"], sB + 2 * (m + 1))
                                            om = kt - 4 * qc
                                            inc(v.tensor_tensor(PT[0][:, :],
                                                                PT[0][:, :],
                                                                CN["cmask"][:, om, :],
                                                                ALU.mult), "vs")
                                            inc(v.tensor_tensor(PT[1][:, :],
                                                                PT[1][:, :],
                                                                CN["cmask"][:, om, :],
                                                                ALU.mult), "vs")
                                        m += 1
                                    v.wait_ge(sem["ts"], tb + 5 * CUMKT[qc])
                                    inc(v.reciprocal(RA[0:1, :], OP1[64:65, :]), "vs")
                                    inc(v.reciprocal(RB[0:1, :], LP[0:1, :]), "vs")
                                    v.wait_ge(sem["gs"], gB + 2 * (qc + 1))
                                    n_on = pr * 4 + qc
                                    j_on = n_on % 2
                                    if n_on >= 2:
                                        v.wait_ge(sem[f"ons{j_on}"],
                                                  16 * ((n_on - 2) // 2 + 1
                                                        + ons_base[j_on]))
                                    inc(v.tensor_tensor(
                                        ONB[j_on][0:64, :],
                                        OP1[0:64, :], RCP[0:64, :], ALU.mult), "vs")
                                    inc(v.tensor_tensor(
                                        ONB[j_on][64:128, :],
                                        OP2[64:128, :], RCPB[64:128, :], ALU.mult), "vs")

                            @blk.gpsimd
                            def _(g, pr=pr):
                                vB = v_pair(pr) + 4
                                for qc in range(4):
                                    g.wait_ge(sem["vs"], vB + qc * 12 + 10)
                                    inc(g.partition_broadcast(RCP[0:64, :],
                                                              RA[0:1, :]), "gs")
                                    inc(g.partition_broadcast(RCPB[:, :],
                                                              RB[0:1, :]), "gs")
                                    n_on = pr * 4 + qc
                                    g.wait_ge(sem["vs"], vB + qc * 12 + 12)
                                    dinc(g.dma_start(
                                        out=onorm_d[pr, :,
                                                    qc * 512:(qc + 1) * 512],
                                        in_=ONB[n_on % 2][:, :]),
                                        f"ons{n_on % 2}")
                fence()
                if _au < 4:
                    return

                tp0, vp0, dwp0 = cnt["ts"], cnt["vs"], cnt["dwa"]
                with ExitStack() as ph:
                    WOP = ph.enter_context(nc.sbuf_tensor(U("WOP"), [128, 8, E], BF16))
                    OIN = [ph.enter_context(nc.sbuf_tensor(U(f"OIN{i}"), [128, 8, 128],
                                                           BF16)) for i in range(2)]
                    APP = [ph.enter_context(nc.psum_tensor(U(f"APP{i}"), [128, 512], F32))
                           for i in range(2)]
                    oin_base = [cnt["oin0"] // 16, cnt["oin1"] // 16]
                    with nc.Block() as blk:
                        @blk.gpsimd
                        def _(g):
                            dinc(g.dma_start(out=WOP[:, :, :], in_=wo_v[:, :, :]), "dwa")
                            for qt in range(NT):
                                if qt >= 2:
                                    g.wait_ge(sem["ts"], tp0 + 16 * (qt - 1))
                                dinc(g.dma_start(
                                    out=OIN[qt % 2][:, :, :],
                                    in_=onorm_d[:, :, qt * 128:(qt + 1) * 128]
                                    .rearrange("c p t -> p c t")), f"oin{qt % 2}")

                        @blk.tensor
                        def _(t):
                            t.wait_ge(sem["dwa"], dwp0 + 16)
                            n = 0
                            for qt in range(NT):
                                t.wait_ge(sem[f"oin{qt % 2}"],
                                          16 * (qt // 2 + 1 + oin_base[qt % 2]))
                                for eh in range(2):
                                    if n >= 2:
                                        t.wait_ge(sem["vs"], vp0 + n - 1)
                                    for c in range(8):
                                        inc(t.matmul(
                                            APP[n % 2][:, :],
                                            OIN[qt % 2][:, c, :],
                                            WOP[:, c, eh * 512:(eh + 1) * 512],
                                            start=(c == 0), stop=(c == 7)), "ts")
                                    n += 1

                        @blk.vector
                        def _(v):
                            n = 0
                            for qt in range(NT):
                                for eh in range(2):
                                    v.wait_ge(sem["ts"], tp0 + (n + 1) * 8)
                                    inc(v.tensor_add(
                                        RES[:, qt, eh * 512:(eh + 1) * 512],
                                        APP[n % 2][:, :],
                                        RES[:, qt, eh * 512:(eh + 1) * 512]), "vs")
                                    n += 1
                fence()

        if upto >= 3:
            moe_sublayer()
        elif upto >= 2:
            emit_ln_gate()
            emit_ags()
            fence()
        if upto >= 6:
            emit_attention()
        if upto >= 7:
            moe_sublayer(zero_tbl=True)

        with nc.Block() as blk:
            @blk.sync
            def _(sy):
                for k in ("ts", "vs", "ss", "gs", "dsc", "dg", "dat", "cc"):
                    if cnt[k]:
                        sy.wait_ge(sem[k], cnt[k])
                sy.wait_ge(sem["dx"], dx_done)
                dinc(sy.dma_start(out=y_tm, in_=RES[:, :, :]), "dout")
                sy.wait_ge(sem["dout"], cnt["dout"])

        dump_outs = []
        for nm in dumps:
            ap, shape, dt = REG[nm]
            od = nc.dram_tensor("dump_" + nm, shape, dt, kind="ExternalOutput").ap()
            dump_outs.append((od, ap))
        if dump_outs:
            with nc.Block() as blk:
                @blk.sync
                def _(sy):
                    for od, ap in dump_outs:
                        dinc(sy.dma_start(out=od[...], in_=ap), "dout")
                    sy.wait_ge(sem["dout"], cnt["dout"])

    mybir.codegen_inst_isa_subclasses(nc)
    return nc


def kernel(**inputs):
    per_core = _host_pack(inputs)
    nc = build_program(upto=7)
    res = run_bass_kernel_spmd(nc, per_core, core_ids=list(range(NCORE)))
    out = np.stack([np.asarray(res.results[i]["y"]).reshape(S, E) for i in range(NCORE)])
    return out.astype(np.float32)


# version 03 (final)
# speedup vs baseline: 1.0633x; 1.0069x over previous
import numpy as np
import ml_dtypes
import concourse.bass as bass
import concourse.mybir as mybir
from concourse.bass_utils import run_bass_kernel_spmd
from concourse.library_config import mlp as _mlplib
from contextlib import ExitStack

F32 = mybir.dt.float32
BF16 = mybir.dt.bfloat16
I16 = mybir.dt.int16
U8 = mybir.dt.uint8
AF = mybir.ActivationFunctionType
ALU = mybir.AluOpType
AX = mybir.AxisListType
_DETECT_RACES = True

NCORE = 8
B, S, E = 8, 2048, 1024
H, DH = 16, 64
NE, FF, CAP = 8, 4096, 2560
T = B * S
TOK = S
NT = TOK // 128
NP = T // 128
TRASH = CAP
TBL_ROWS = CAP + 256
CW = CAP // 16
EPS = 1e-5
SCH = 512
NSC = CAP // SCH
W1PF = 256
NW1P = FF // W1PF
NKT = [4, 8, 12, 16]
CUMKT = [4, 12, 24, 40]

_TPERM = [4 * (j % 8) + j // 8 for j in range(32)]
TWALK = []
_seen = [False] * 32
for _s0 in range(32):
    _c = _s0
    while not _seen[_c]:
        _seen[_c] = True
        TWALK.append(_c)
        _c = _TPERM[_c]


def _host_pack(inputs):
    x = np.asarray(inputs["L0_attn_x"])
    wqkv = np.asarray(inputs["L0_attn_w_qkv"])
    wout = np.asarray(inputs["L0_attn_w_out"])
    gw = np.asarray(inputs["L0_ffn_gate_w"])
    gb = np.asarray(inputs["L0_ffn_gate_b"])
    w1 = np.asarray(inputs["L0_ffn_w1"])
    b1 = np.asarray(inputs["L0_ffn_b1"])
    w2 = np.asarray(inputs["L0_ffn_w2"])
    b2 = np.asarray(inputs["L0_ffn_b2"])

    wq = np.stack([np.concatenate([wqkv[:, 2 * p, 0:64], wqkv[:, 2 * p + 1, 0:64]], 1)
                   for p in range(8)]).astype(np.float32)
    wk = np.stack([np.concatenate([wqkv[:, 2 * p, 64:128], wqkv[:, 2 * p + 1, 64:128]], 1)
                   for p in range(8)]).astype(np.float32)
    wv = np.stack([np.concatenate([wqkv[:, 8 * g + h, 128:192] for h in range(8)], 1)
                   for g in range(2)]).astype(np.float32)
    wo = np.ascontiguousarray(wout.reshape(H * DH, E)).astype(np.float32)
    wq = np.ascontiguousarray(wq.reshape(8, 8, 128, 128).transpose(0, 2, 1, 3))
    wk = np.ascontiguousarray(wk.reshape(8, 8, 128, 128).transpose(0, 2, 1, 3))
    wv = np.ascontiguousarray(wv.reshape(2, 8, 128, 512).transpose(0, 2, 1, 3))
    wo = np.ascontiguousarray(wo.reshape(8, 128, 1024).transpose(1, 0, 2))

    gwT = np.ascontiguousarray(gw.T.reshape(8, 128, 8).transpose(1, 0, 2)).astype(np.float32)
    gb_dev = np.tile(np.asarray(gb)[None, :], (128, 1)).astype(np.float32)
    b1col = np.ascontiguousarray(b1.reshape(NE, 32, 128).transpose(0, 2, 1)).astype(np.float32)
    b2rep = np.tile(np.asarray(b2)[:, None, :], (1, 128, 1)).astype(np.float32)

    ident_bf = np.eye(128).astype(ml_dtypes.bfloat16)
    ident_f32 = np.eye(128).astype(np.float32)
    trils = np.triu(np.ones((128, 128)), 1).astype(ml_dtypes.bfloat16)
    iota8 = np.tile(np.arange(8, dtype=np.float32)[None, :], (128, 1))
    rowiota = (np.arange(16, dtype=np.float32)[:, None] * CW
               + np.arange(CW, dtype=np.float32)[None, :]).astype(np.float32)
    zeros16 = np.zeros((16, CW), np.float32)
    ones128 = np.ones((128, 1), dtype=ml_dtypes.bfloat16)
    th16 = np.tile((160.0 * np.arange(1, 17, dtype=np.float32))[None, :], (128, 1))
    gid128 = (np.arange(128, dtype=np.float32)[:, None]
              + 128.0 * np.arange(128, dtype=np.float32)[None, :])
    zcol = np.zeros((128, 128), np.float32)
    kk = np.arange(128)[:, None]
    qq = np.arange(512)[None, :]
    cmask = np.stack([(qq >= kk + 128 * o) for o in range(4)]).astype(ml_dtypes.bfloat16)

    per_core = []
    for ci in range(NCORE):
        esel = np.zeros((128, 16), np.float32)
        for m in range(16):
            esel[16 * ci + m, m] = 1.0
        per_core.append({
            "x_my": np.ascontiguousarray(x[ci]).astype(np.float32),
            "w1_my": np.ascontiguousarray(w1[ci]).astype(np.float32),
            "w2_my": np.ascontiguousarray(w2[ci]).astype(np.float32),
            "b1_my": b1col[ci], "b2_my": b2rep[ci],
            "gwT": gwT, "gb": gb_dev,
            "wq": wq, "wk": wk, "wv": wv, "wo": wo,
            "ident_bf": ident_bf, "ident_f32": ident_f32, "trils": trils,
            "iota8": iota8, "rowiota": rowiota, "zeros16": zeros16, "zcol": zcol,
            "cmask": np.ascontiguousarray(cmask),
            "gid128": gid128, "ones128": ones128, "th16": th16,
            "mye": np.full((128, 1), float(ci), np.float32),
            "myoh": np.eye(8, dtype=np.float32)[ci:ci + 1].copy(),
            "esel": esel,
        })
    return per_core


def build_program(upto=99, dumps=()):
    nc = bass.Bass("TRN2", target_bir_lowering=False, detect_race_conditions=_DETECT_RACES)
    X = {}

    def ein(name, shape, dt):
        X[name] = nc.dram_tensor(name, shape, dt, kind="ExternalInput").ap()

    for name, shape, dt in [
            ("x_my", [TOK, E], F32), ("w1_my", [E, FF], F32), ("w2_my", [FF, E], F32),
            ("b1_my", [128, 32], F32), ("b2_my", [128, E], F32),
            ("gwT", [128, 8, 8], F32), ("gb", [128, 8], F32),
            ("wq", [8, 128, 8, 128], F32), ("wk", [8, 128, 8, 128], F32),
            ("wv", [2, 128, 8, 512], F32), ("wo", [128, 8, E], F32),
            ("ident_bf", [128, 128], BF16), ("ident_f32", [128, 128], F32),
            ("trils", [128, 128], BF16), ("iota8", [128, 8], F32),
            ("rowiota", [16, CW], F32), ("zeros16", [16, CW], F32),
            ("zcol", [128, 128], F32), ("cmask", [4, 128, 512], BF16),
            ("mye", [128, 1], F32), ("myoh", [1, 8], F32), ("esel", [128, 16], F32),
            ("gid128", [128, 128], F32), ("ones128", [128, 1], BF16),
            ("th16", [128, 16], F32)]:
        ein(name, shape, dt)

    y = nc.dram_tensor("y", [TOK, E], F32, kind="ExternalOutput").ap()

    w1pk = nc.dram_tensor("w1pk", [8, 128, 8, 512], BF16).ap()
    w2pk = nc.dram_tensor("w2pk", [2, 8, 128, 4, 512], BF16).ap()
    ln_in = nc.dram_tensor("ln_in", [TOK, E], BF16).ap()
    ln_all = nc.dram_tensor("ln_all", [T, E], BF16).ap()
    lg_in = nc.dram_tensor("lg_in", [TOK, 8], F32).ap()
    lg_all = nc.dram_tensor("lg_all", [T, 8], F32).ap()
    eo_in = nc.dram_tensor("eo_in", [CAP, E], F32).ap()
    eo_all = nc.dram_tensor("eo_all", [NE * CAP, E], F32).ap()
    tbl = nc.dram_tensor("tbl", [TBL_ROWS, 64], F32).ap()
    dscat = nc.dram_tensor("dscat", [16, NP * 8], I16).ap()
    onorm_d = nc.dram_tensor("onorm_d", [8, 128, TOK], BF16).ap()
    down = nc.dram_tensor("down", [16, 128], I16).ap()

    es = ExitStack()
    sem, cnt = {}, {}

    def mksem(*names):
        for n in names:
            sem[n] = es.enter_context(nc.semaphore(n))
            cnt[n] = 0

    def inc(inst, name, by=1):
        cnt[name] += by
        return inst.then_inc(sem[name], by)

    def dinc(inst, name):
        return inc(inst, name, 16)

    REG = {}
    NREG = {}
    _uqc = [0]

    def U(nm):
        _uqc[0] += 1
        return f"{nm}_u{_uqc[0]}"

    with es:
        sb = nc.alloc_sbuf_tensor
        RES = sb("RES", [128, NT, E], F32)
        CN = {}
        for nm, shape, dt in [("ident_bf", [128, 128], BF16), ("ident_f32", [128, 128], F32),
                              ("trils", [128, 128], BF16), ("iota8", [128, 8], F32),
                              ("rowiota", [16, CW], F32), ("zeros16", [16, CW], F32),
                              ("zcol", [128, 128], F32), ("cmask", [128, 4, 512], BF16),
                              ("gid128", [128, 128], F32), ("ones128", [128, 1], BF16),
                              ("th16", [128, 16], F32),
                              ("mye", [128, 1], F32), ("myoh", [1, 8], F32),
                              ("esel", [128, 16], F32), ("gwT", [128, 8, 8], F32),
                              ("gb", [128, 8], F32), ("b1c", [128, 32], F32),
                              ("b2r", [128, E], F32)]:
            CN[nm] = sb("c_" + nm, shape, dt)
        TOTB = sb("TOTB", [128, 8], BF16)
        EXCL = sb("EXCL", [128, 8], F32)
        CNTS = sb("CNTS", [1, 8], F32)
        CNT1 = sb("CNT1", [16, 1], F32)
        CNTsc = sb("CNTsc", [1, 1], F32)
        coefT = sb("coefT", [128, 16], F32)
        ownsl = sb("ownsl", [16, 128], F32)
        idxown = sb("idxown", [128, 128], I16)
        idxg = sb("idxg", [128, CW], I16)
        idg16 = sb("idg16", [16, CW], I16)
        selg = sb("selg", [16, CW], F32)
        maskv = sb("maskv", [16, CW], U8)
        LGT = sb("LGT", [128, NT, 8], F32)
        SQT = [sb(f"SQT{i}", [128, E], F32) for i in range(2)]
        STAT = sb("STAT", [128, NT, 8], F32)
        LNT = [sb(f"LNT{i}", [128, E], F32) for i in range(2)]
        LNB = [sb(f"LNB{i}", [128, E], BF16) for i in range(2)]
        idxsc = sb("idxsc", [128, NP * 8], I16)
        REG.update(lgt=(LGT[:, :, :], [128, NT, 8], F32),
                   cnts=(CNTS[:, :], [1, 8], F32), coeft=(coefT[:, :], [128, 16], F32),
                   ownsl=(ownsl[:, :], [16, 128], F32),
                   idg16=(idg16[:, :], [16, CW], I16),
                   idxown=(idxown[:, :], [128, 128], I16),
                   res0=(RES[:, 0:2, :], [128, 2, E], F32))
        mksem("dx", "dw", "dwa", "dsc", "dg", "cc", "ts", "vs", "ss", "gs",
              "dat", "dout", "lnr0", "lnr1", "w1s0", "w1s1", "w2s0", "w2s1",
              "eots0", "eots1", "eots2", "eots3", "wqs0", "wqs1",
              "ons0", "ons1", "oin0", "oin1")

        x_tm = X["x_my"].rearrange("(a p) e -> p a e", p=128)
        ln_in_tm = ln_in.rearrange("(a p) e -> p a e", p=128)
        lg_in_tm = lg_in.rearrange("(a p) e -> p a e", p=128)
        y_tm = y.rearrange("(a p) e -> p a e", p=128)
        wq_v = X["wq"]
        wk_v = X["wk"]
        wv_v = X["wv"]
        wo_v = X["wo"]

        with ExitStack() as p0s:
            WCV = p0s.enter_context(nc.sbuf_tensor(U("WCV"), [128, FF], BF16))
            blk = p0s.enter_context(nc.Block())
            _ = blk

            @blk.sync
            def _(sy):
                dinc(sy.dma_start(out=RES[:, :, :], in_=x_tm), "dx")
                for nm in ("ident_bf", "ident_f32", "trils", "iota8", "rowiota",
                           "zeros16", "zcol", "gid128", "ones128", "th16", "mye", "myoh", "esel",
                           "gwT", "gb"):
                    dinc(sy.dma_start(out=CN[nm][...], in_=X[nm][...]), "dx")
                dinc(sy.dma_start(out=CN["cmask"][:, :, :],
                                  in_=X["cmask"].rearrange("o p q -> p o q")), "dx")
                dinc(sy.dma_start(out=CN["b1c"][:, :], in_=X["b1_my"][:, :]), "dx")
                dinc(sy.dma_start(out=CN["b2r"][:, :], in_=X["b2_my"][:, :]), "dx")

            @blk.gpsimd
            def _(g):
                g.load_library(_mlplib)
                for v in (896, 256, 1024, 512):
                    NREG[v] = g.alloc_register(f"nreg{v}")
                    g.reg_mov(NREG[v], v)
                g.wait_ge(sem["dx"], cnt["dx"])
                for zk in range(TBL_ROWS // 256):
                    dinc(g.dma_start(out=tbl[zk * 256:(zk + 1) * 256, :],
                                     in_=CN["zcol"][:, :]), "dw")
                for piece in range(8):
                    dinc(g.dma_start(out=WCV[:, :],
                                     in_=X["w1_my"][piece * 128:(piece + 1) * 128, :]), "dw")
                    g.wait_ge(sem["dw"], cnt["dw"])
                    dinc(g.dma_start(
                        out=w1pk[:, :, piece, :].rearrange("b p f -> p b f"),
                        in_=WCV[:, :].rearrange("p (b f) -> p b f", b=8)), "dw")
                    g.wait_ge(sem["dw"], cnt["dw"])
                for piece in range(32):
                    dinc(g.dma_start(out=WCV[:, 0:E],
                                     in_=X["w2_my"][piece * 128:(piece + 1) * 128, :]), "dw")
                    g.wait_ge(sem["dw"], cnt["dw"])
                    dinc(g.dma_start(
                        out=w2pk[:, piece // 4, :, piece % 4, :]
                        .rearrange("d p f -> p d f"),
                        in_=WCV[:, 0:E].rearrange("p (d f) -> p d f", d=2)), "dw")
                    g.wait_ge(sem["dw"], cnt["dw"])
        wconv_done = cnt["dw"]
        dx_done = cnt["dx"]

        def fence():
            snap = {k: v for k, v in cnt.items() if v and k != "dout"}
            with nc.Block() as blk:
                for eng_name in ("sync", "gpsimd", "tensor", "scalar", "vector"):
                    def body(eng, _snap=snap):
                        for k, v in _snap.items():
                            eng.wait_ge(sem[k], v)
                    getattr(blk, eng_name)(body)

        def emit_ln_gate():
            t0, v0, s0, d0 = cnt["ts"], cnt["vs"], cnt["ss"], cnt["dsc"]
            lnr_base = [cnt["lnr0"] // 16, cnt["lnr1"] // 16]
            with ExitStack() as ph:
                TPS = [ph.enter_context(nc.psum_tensor(U(f"TPS{i}"), [128, 8, 128], F32))
                       for i in range(2)]
                LNFM = [ph.enter_context(nc.sbuf_tensor(U(f"LNFM{i}"),
                                                        [128, 8, 128], F32))
                        for i in range(2)]
                LGP = [ph.enter_context(nc.psum_tensor(U(f"LGP{i}"), [128, 8], F32))
                       for i in range(2)]
                with nc.Block() as blk:
                    @blk.scalar
                    def _(s_):
                        s_.wait_ge(sem["dx"], dx_done)
                        for i in range(NT):
                            if i >= 2:
                                s_.wait_ge(sem["vs"], v0 + 11 * (i - 2) + 2)
                            inc(s_.activation(SQT[i % 2][:, :], RES[:, i, :],
                                              AF.Square), "ss")
                            s_.wait_ge(sem["vs"], v0 + 11 * i + 6)
                            inc(s_.activation(STAT[:, i, 6:7], STAT[:, i, 5:6],
                                              AF.Sqrt), "ss")

                    @blk.vector
                    def _(v):
                        v.wait_ge(sem["dx"], dx_done)
                        for i in range(NT):
                            inc(v.reduce_sum(STAT[:, i, 0:1], RES[:, i, :],
                                             axis=AX.X), "vs")
                            v.wait_ge(sem["ss"], s0 + 2 * i + 1)
                            inc(v.reduce_sum(STAT[:, i, 1:2], SQT[i % 2][:, :],
                                             axis=AX.X), "vs")
                            v.wait_ge(sem["vs"], cnt["vs"])
                            inc(v.tensor_scalar_mul(STAT[:, i, 2:3], STAT[:, i, 0:1],
                                                    1.0 / E), "vs")
                            v.wait_ge(sem["vs"], cnt["vs"])
                            inc(v.tensor_mul(STAT[:, i, 3:4], STAT[:, i, 2:3],
                                             STAT[:, i, 2:3]), "vs")
                            v.wait_ge(sem["vs"], cnt["vs"])
                            inc(v.scalar_tensor_tensor(STAT[:, i, 4:5], STAT[:, i, 1:2],
                                                       1.0 / E, STAT[:, i, 3:4],
                                                       ALU.mult, ALU.subtract), "vs")
                            v.wait_ge(sem["vs"], cnt["vs"])
                            inc(v.tensor_scalar_add(STAT[:, i, 5:6], STAT[:, i, 4:5],
                                                    EPS), "vs")
                            v.wait_ge(sem["ss"], s0 + 2 * i + 2)
                            inc(v.reciprocal(STAT[:, i, 7:8], STAT[:, i, 6:7]), "vs")
                            v.wait_ge(sem["vs"], cnt["vs"])
                            inc(v.scalar_tensor_tensor(STAT[:, i, 3:4], STAT[:, i, 2:3],
                                                       -1.0, STAT[:, i, 7:8],
                                                       ALU.mult, ALU.mult), "vs")
                            v.wait_ge(sem["vs"], cnt["vs"])
                            if i >= 2:
                                v.wait_ge(sem[f"lnr{i % 2}"], 16 * ((i - 2) // 2 + 1 + lnr_base[i % 2]))
                                v.wait_ge(sem["ts"], t0 + 16 * (i - 2) + 8)
                            inc(v.tensor_scalar(LNT[i % 2][:, :], RES[:, i, :],
                                                STAT[:, i, 7:8], STAT[:, i, 3:4],
                                                ALU.mult, ALU.add), "vs")
                            v.wait_ge(sem["ts"], t0 + 16 * i + 8)
                            inc(v.tensor_copy(LNFM[i % 2][:, :, :],
                                              TPS[i % 2][:, :, :]), "vs")
                            v.wait_ge(sem["ts"], t0 + 16 * i + 16)
                            inc(v.tensor_add(LGT[:, i, :], LGP[i % 2][:, :],
                                             CN["gb"][:, :]), "vs")

                    @blk.gpsimd
                    def _(g):
                        for i in range(NT):
                            g.wait_ge(sem["vs"], v0 + 11 * i + 9)
                            dinc(g.dma_start(out=ln_in_tm[:, i, :],
                                             in_=LNT[i % 2][:, :]), f"lnr{i % 2}")

                    @blk.tensor
                    def _(t):
                        for i in range(NT):
                            t.wait_ge(sem["vs"], v0 + 11 * i + 9)
                            if i >= 2:
                                t.wait_ge(sem["vs"], v0 + 11 * (i - 2) + 10)
                            for k in range(8):
                                inc(t.transpose(TPS[i % 2][:, k, :],
                                                LNT[i % 2][:, k * 128:(k + 1) * 128],
                                                CN["ident_f32"][:, :]), "ts")
                            t.wait_ge(sem["vs"], v0 + 11 * i + 10)
                            if i >= 2:
                                t.wait_ge(sem["vs"], v0 + 11 * (i - 2) + 11)
                            for k in range(8):
                                inc(t.matmul(LGP[i % 2][:, :], LNFM[i % 2][:, k, :],
                                             CN["gwT"][:, k, :],
                                             start=(k == 0), stop=(k == 7)), "ts")
            with nc.Block() as blk:
                @blk.gpsimd
                def _(g):
                    g.wait_ge(sem["vs"], cnt["vs"])
                    dinc(g.dma_start(out=lg_in_tm[:, :, :], in_=LGT[:, :, :]), "dsc")

        def emit_ags():
            with nc.Block() as blk:
                @blk.gpsimd
                def _(g):
                    g.wait_ge(sem["dsc"], cnt["dsc"])
                    g.wait_ge(sem["lnr0"], cnt["lnr0"])
                    g.wait_ge(sem["lnr1"], cnt["lnr1"])
                    inc(g.collective_compute("AllGather", ALU.bypass,
                                             replica_groups=[list(range(NCORE))],
                                             ins=[lg_in[:, :].opt()],
                                             outs=[lg_all[:, :].opt()]), "cc")
                    inc(g.collective_compute("AllGather", ALU.bypass,
                                             replica_groups=[list(range(NCORE))],
                                             ins=[ln_in[:, :].opt()],
                                             outs=[ln_all[:, :].opt()]), "cc")

        def emit_routing_scatter():
            with ExitStack() as ph:
                LG = ph.enter_context(nc.sbuf_tensor(U("LG"), [128, NP, 8], F32))
                EXV = ph.enter_context(nc.sbuf_tensor(U("EXV"), [128, NP, 8], F32))
                OHV = ph.enter_context(nc.sbuf_tensor(U("OHV"), [128, NP, 8], F32))
                INCL = ph.enter_context(nc.sbuf_tensor(U("INCL"), [128, NP, 8], F32))
                SCR8 = ph.enter_context(nc.sbuf_tensor(U("SCR8"), [128, NP, 8], F32))
                RM = ph.enter_context(nc.sbuf_tensor(U("RM"), [128, NP], F32))
                RS = ph.enter_context(nc.sbuf_tensor(U("RS"), [128, NP], F32))
                TP = ph.enter_context(nc.sbuf_tensor(U("TP"), [128, NP], F32))
                ET = ph.enter_context(nc.sbuf_tensor(U("ET"), [128, NP], F32))
                POS = ph.enter_context(nc.sbuf_tensor(U("POS"), [128, NP], F32))
                KEEP = ph.enter_context(nc.sbuf_tensor(U("KEEP"), [128, NP], F32))
                COEF = ph.enter_context(nc.sbuf_tensor(U("COEF"), [128, NP], F32))
                SA = ph.enter_context(nc.sbuf_tensor(U("SA"), [128, NP], F32))
                SB_ = ph.enter_context(nc.sbuf_tensor(U("SB_"), [128, NP], F32))
                gidrep = ph.enter_context(nc.sbuf_tensor(U("gidrep"), [128, NP, 64], F32))
                SB2 = ph.enter_context(nc.sbuf_tensor(U("SB2"), [128, NP], F32))
                OW2 = ph.enter_context(nc.sbuf_tensor(U("OW2"), [16, 128], F32))
                TMP16 = ph.enter_context(nc.sbuf_tensor(U("TMP16"), [128, NP, 16], F32))
                REG.update(et=(ET[:, :], [128, NP], F32), pos=(POS[:, :], [128, NP], F32),
                           keep=(KEEP[:, :], [128, NP], F32),
                           coef=(COEF[:, :], [128, NP], F32),
                           tp=(TP[:, :], [128, NP], F32))
                iota_bc = CN["iota8"][:, :].unsqueeze(1).broadcast_to([128, NP, 8])
                with ExitStack() as php:
                    EXP_ = php.enter_context(nc.psum_tensor(U("EXCP"), [128, 8], F32))
                    CNP = php.enter_context(nc.psum_tensor(U("CNP"), [1, 8], F32))
                    OSP = php.enter_context(nc.psum_tensor(U("OSP"), [16, 128], F32))
                    CTP = php.enter_context(nc.psum_tensor(U("CTP"), [128, 16], F32))
                    with nc.Block() as blk:
                        @blk.sync
                        def _(sy):
                            sy.wait_ge(sem["cc"], cnt["cc"])
                            dinc(sy.dma_start(
                                out=LG[:, :, :],
                                in_=lg_all.rearrange("(p a) e -> p a e", p=128)), "dat")

                        @blk.scalar
                        def _(s):
                            s.wait_ge(sem["dat"], cnt["dat"])
                            inc(s.activation(EXV[:, :, :], LG[:, :, :], AF.Exp), "ss")

                        @blk.vector
                        def _(v):
                            v.wait_ge(sem["ss"], cnt["ss"])
                            inc(v.reduce_max(RM[:, :], EXV[:, :, :], axis=AX.X), "vs")
                            inc(v.reduce_sum(RS[:, :], EXV[:, :, :], axis=AX.X), "vs")
                            v.wait_ge(sem["vs"], cnt["vs"])
                            inc(v.reciprocal(TP[:, :], RS[:, :]), "vs")
                            v.wait_ge(sem["vs"], cnt["vs"])
                            inc(v.tensor_tensor(TP[:, :], TP[:, :], RM[:, :],
                                                ALU.mult), "vs")
                            rm_bc = RM[:, :].unsqueeze(2).broadcast_to([128, NP, 8])
                            v.wait_ge(sem["vs"], cnt["vs"])
                            inc(v.tensor_tensor(SCR8[:, :, :], EXV[:, :, :], rm_bc,
                                                ALU.is_lt), "vs")
                            v.wait_ge(sem["vs"], cnt["vs"])
                            inc(v.scalar_tensor_tensor(SCR8[:, :, :], SCR8[:, :, :], 9.0,
                                                       iota_bc, ALU.mult, ALU.add), "vs")
                            v.wait_ge(sem["vs"], cnt["vs"])
                            inc(v.tensor_reduce(ET[:, :], SCR8[:, :, :], axis=AX.X,
                                                op=ALU.min), "vs")
                            et_bc = ET[:, :].unsqueeze(2).broadcast_to([128, NP, 8])
                            v.wait_ge(sem["vs"], cnt["vs"])
                            inc(v.tensor_tensor(OHV[:, :, :], iota_bc, et_bc,
                                                ALU.is_equal), "vs")
                            v.wait_ge(sem["vs"], cnt["vs"])
                            for e in range(8):
                                inc(v.tensor_tensor_scan(INCL[:, :, e], OHV[:, :, e],
                                                         CN["zcol"][:, :], 0.0,
                                                         ALU.add, ALU.add), "vs")
                            v.wait_ge(sem["vs"], cnt["vs"])
                            inc(v.tensor_copy(TOTB[:, :], INCL[:, NP - 1, :]), "vs")

                        @blk.tensor
                        def _(t):
                            t.wait_ge(sem["vs"], cnt["vs"])
                            inc(t.matmul(EXP_[:, :], CN["trils"][:, :], TOTB[:, :],
                                         start=True, stop=True), "ts")
                            inc(t.matmul(CNP[:, :], CN["ones128"][:, :], TOTB[:, :],
                                         start=True, stop=True), "ts")

                        @blk.vector
                        def _(v):
                            v.wait_ge(sem["ts"], cnt["ts"])
                            v.wait_ge(sem["vs"], cnt["vs"])
                            inc(v.tensor_copy(EXCL[:, :], EXP_[:, :]), "vs")
                            v.wait_ge(sem["vs"], cnt["vs"])
                            inc(v.tensor_tensor(SCR8[:, :, :], OHV[:, :, :],
                                                INCL[:, :, :], ALU.mult), "vs")
                            v.wait_ge(sem["vs"], cnt["vs"])
                            inc(v.reduce_sum(POS[:, :], SCR8[:, :, :], axis=AX.X), "vs")
                            ex_bc = EXCL[:, :].unsqueeze(1).broadcast_to([128, NP, 8])
                            v.wait_ge(sem["vs"], cnt["vs"])
                            inc(v.tensor_tensor(SCR8[:, :, :], OHV[:, :, :], ex_bc,
                                                ALU.mult), "vs")
                            v.wait_ge(sem["vs"], cnt["vs"])
                            inc(v.reduce_sum(SB_[:, :], SCR8[:, :, :], axis=AX.X), "vs")
                            v.wait_ge(sem["vs"], cnt["vs"])
                            inc(v.scalar_tensor_tensor(POS[:, :], POS[:, :], -1.0,
                                                       SB_[:, :], ALU.add, ALU.add), "vs")
                            v.wait_ge(sem["vs"], cnt["vs"])
                            inc(v.tensor_single_scalar(KEEP[:, :], POS[:, :],
                                                       float(CAP), ALU.is_lt), "vs")
                            v.wait_ge(sem["vs"], cnt["vs"])
                            inc(v.scalar_tensor_tensor(COEF[:, :], TP[:, :], 0.5,
                                                       KEEP[:, :], ALU.mult,
                                                       ALU.mult), "vs")
                            v.wait_ge(sem["vs"], cnt["vs"])
                            inc(v.tensor_scalar(SA[:, :], ET[:, :], CN["mye"][:, :],
                                                None, ALU.is_equal, ALU.bypass), "vs")
                            v.wait_ge(sem["vs"], cnt["vs"])
                            inc(v.tensor_tensor(SA[:, :], SA[:, :], KEEP[:, :],
                                                ALU.mult), "vs")
                            v.wait_ge(sem["vs"], cnt["vs"])
                            inc(v.tensor_scalar_add(SB_[:, :], POS[:, :],
                                                    -float(TRASH)), "vs")
                            v.wait_ge(sem["vs"], cnt["vs"])
                            inc(v.tensor_tensor(SB_[:, :], SB_[:, :], SA[:, :],
                                                ALU.mult), "vs")
                            v.wait_ge(sem["vs"], cnt["vs"])
                            inc(v.tensor_scalar_add(SB_[:, :], SB_[:, :],
                                                    float(TRASH)), "vs")
                            v.wait_ge(sem["vs"], cnt["vs"])
                            inc(v.tensor_tensor(SA[:, :], POS[:, :], KEEP[:, :],
                                                ALU.mult), "vs")
                            v.wait_ge(sem["vs"], cnt["vs"])
                            inc(v.tensor_tensor(
                                TMP16[:, :, :],
                                SA[:, :].unsqueeze(2).broadcast_to([128, NP, 16]),
                                CN["th16"][:, :].unsqueeze(1)
                                .broadcast_to([128, NP, 16]), ALU.is_ge), "vs")
                            v.wait_ge(sem["vs"], cnt["vs"])
                            inc(v.reduce_sum(RS[:, :], TMP16[:, :, :], axis=AX.X), "vs")
                            v.wait_ge(sem["vs"], cnt["vs"])
                            inc(v.scalar_tensor_tensor(RM[:, :], RS[:, :], -float(CW),
                                                       SA[:, :], ALU.mult, ALU.add), "vs")
                            v.wait_ge(sem["vs"], cnt["vs"])
                            inc(v.scalar_tensor_tensor(SA[:, :], ET[:, :], float(CAP),
                                                       RS[:, :], ALU.mult, ALU.add), "vs")
                            v.wait_ge(sem["vs"], cnt["vs"])
                            inc(v.scalar_tensor_tensor(SA[:, :], RM[:, :], 16.0,
                                                       SA[:, :], ALU.mult, ALU.add), "vs")

                        @blk.tensor
                        def _(t):
                            t.wait_ge(sem["vs"], cnt["vs"])
                            inc(t.matmul(OSP[:, :], CN["esel"][:, :], SA[:, :],
                                         start=True, stop=True), "ts")
                            inc(t.matmul(CTP[:, :], COEF[:, :], CN["esel"][:, :],
                                         start=True, stop=True), "ts")

                        @blk.vector
                        def _(v):
                            v.wait_ge(sem["ts"], cnt["ts"])
                            inc(v.tensor_copy(ownsl[:, :], OSP[:, :]), "vs")
                            inc(v.tensor_copy(coefT[:, :], CTP[:, :]), "vs")
                            inc(v.tensor_copy(CNTS[:, :], CNP[:, :]), "vs")
                            v.wait_ge(sem["vs"], cnt["vs"])
                            inc(v.tensor_tensor(CNTS[:, :], CNTS[:, :],
                                                CN["myoh"][:, :], ALU.mult), "vs")
                            v.wait_ge(sem["vs"], cnt["vs"])
                            inc(v.reduce_sum(CNTsc[:, :], CNTS[:, :], axis=AX.X), "vs")
                            inc(v.tensor_copy(
                                gidrep[:, :, :],
                                CN["gid128"][:, :].unsqueeze(2)
                                .broadcast_to([128, NP, 64])), "vs")
                            v.wait_ge(sem["vs"], cnt["vs"])
                            inc(v.tensor_copy(
                                SB2[:, :].rearrange("p (q d) -> p q d", q=16),
                                SB_[:, :].rearrange("p (d q) -> p q d", q=16)), "vs")
                            inc(v.tensor_copy(
                                OW2[:, :].rearrange("p (q d) -> p q d", q=16),
                                ownsl[:, :].rearrange("p (d q) -> p q d", q=16)), "vs")

                        @blk.gpsimd
                        def _(g):
                            g.wait_ge(sem["vs"], cnt["vs"])
                            inc(g.partition_broadcast(CNT1[:, :], CNTsc[:, :]), "gs")
                            dinc(g.dma_start(
                                out=dscat.rearrange("q (p d) -> p q d", p=128),
                                in_=SB2[:, :].rearrange("p (q d) -> p q d", q=16)), "dsc")
                            dinc(g.dma_start(
                                out=down.rearrange("q (p d) -> p q d", p=16),
                                in_=OW2[:, :].rearrange("p (q d) -> p q d", q=16)), "dsc")
                            g.wait_ge(sem["dsc"], cnt["dsc"])
                            for grp in range(8):
                                dinc(g.dma_start(out=idxsc[16 * grp:16 * grp + 16, :],
                                                 in_=dscat[:, :]), "dsc")
                                dinc(g.dma_start(out=idxown[16 * grp:16 * grp + 16, :],
                                                 in_=down[:, :]), "dsc")
                            g.wait_ge(sem["dsc"], cnt["dsc"])
                            g.wait_ge(sem["vs"], cnt["vs"])
                            sc_chunks = [(ch * 896, 896) for ch in range(18)]
                            sc_chunks.append((18 * 896, T - 18 * 896))
                            for base, ncnt in sc_chunks:
                                dinc(g.dma_scatter_add(
                                    tbl[:, :],
                                    gidrep[:, base // 128:(base + ncnt) // 128, :],
                                    idxsc[:, base // 16:(base + ncnt) // 16],
                                    ncnt, NREG[ncnt], 64), "dg")
                                g.wait_ge(sem["dg"], cnt["dg"])

        def emit_table_read():
            with ExitStack() as ph:
                TRD = ph.enter_context(nc.sbuf_tensor(U("TRD"), [16, CW, 64], F32))
                with nc.Block() as blk:
                    @blk.gpsimd
                    def _(g):
                        dinc(g.dma_start(
                            out=TRD[:, :, :],
                            in_=tbl[0:CAP, :].rearrange("(q f) z -> q f z", q=16)), "dg")
                        g.wait_ge(sem["dg"], cnt["dg"])

                    @blk.vector
                    def _(v):
                        v.wait_ge(sem["dg"], cnt["dg"])
                        v.wait_ge(sem["gs"], cnt["gs"])
                        inc(v.tensor_scalar(maskv[:, :], CN["rowiota"][:, :],
                                            CNT1[:, :], None, ALU.is_lt,
                                            ALU.bypass), "vs")
                        v.wait_ge(sem["vs"], cnt["vs"])
                        inc(v.select(selg[:, :], maskv[:, :], TRD[:, :, 0],
                                     CN["zeros16"][:, :], add_drain=True), "vs")
                        v.wait_ge(sem["vs"], cnt["vs"])
                        inc(v.tensor_copy(idg16[:, :], selg[:, :]), "vs")

                    @blk.gpsimd
                    def _(g):
                        g.wait_ge(sem["vs"], cnt["vs"])
                        for grp in range(8):
                            dinc(g.dma_start(out=idxg[16 * grp:16 * grp + 16, :],
                                             in_=idg16[:, :]), "dsc")
                        g.wait_ge(sem["dsc"], cnt["dsc"])

        def emit_ffn():
            with ExitStack() as ph:
                BUF = ph.enter_context(nc.sbuf_tensor(U("BUF"), [128, CAP // 128, E], BF16))
                HFM = ph.enter_context(nc.sbuf_tensor(U("HFM"), [128, 32, SCH], BF16))
                W1P = [ph.enter_context(nc.sbuf_tensor(U(f"W1P{i}"), [128, 8, 512], BF16))
                       for i in range(2)]
                W2P = [ph.enter_context(nc.sbuf_tensor(U(f"W2P{i}"), [128, 4, 512], BF16))
                       for i in range(2)]
                EOT = [ph.enter_context(nc.sbuf_tensor(U(f"EOT{i}"), [128, 512], F32))
                       for i in range(2)]
                REG.update(buf0=(BUF[:, 0:2, :], [128, 2, E], BF16))

                with nc.Block() as blk:
                    @blk.gpsimd
                    def _(g):
                        for base, ncnt in ((0, 1024), (1024, 1024), (2048, 512)):
                            dinc(g.dma_gather(
                                BUF[:, base // 128:(base + ncnt) // 128, :],
                                ln_all[:, :], idxg[:, base // 16:(base + ncnt) // 16],
                                ncnt, NREG[ncnt], E, transpose=False), "dg")
                            g.wait_ge(sem["dg"], cnt["dg"])

                ts0, ss0, vs0 = cnt["ts"], cnt["ss"], cnt["vs"]
                import os
                if os.environ.get("KDBG"):
                    print("FFN bases: ts0", ts0, "ss0", ss0, "vs0", vs0,
                          "w1s", [cnt["w1s0"], cnt["w1s1"]],
                          "w2s", [cnt["w2s0"], cnt["w2s1"]],
                          "dg", cnt["dg"], "dw", cnt["dw"])
                w1s_base = [cnt["w1s0"] // 16, cnt["w1s1"] // 16]
                w2s_base = [cnt["w2s0"] // 16, cnt["w2s1"] // 16]
                eot_base = [cnt[f"eots{i}"] // 16 for i in range(2)]

                def ts_h(sc_, pc):
                    return ts0 + sc_ * 544 + 32 + (pc + 1) * 16

                def ts_h_half(sc_, fc):
                    return ts0 + sc_ * 544 + 32 + fc * 8 + 8

                def ts_eo(sc_, dmh, fc):
                    return ts0 + sc_ * 544 + 288 + dmh * 128 + (fc + 1) * 4

                def ss_relu(sc_, fc):
                    return ss0 + sc_ * 32 + fc + 1

                def vs_ev(sc_, k):
                    return vs0 + sc_ * 40 + 32 + k + 1

                with ExitStack() as php:
                    HP = [php.enter_context(nc.psum_tensor(U(f"HP{i}"), [128, SCH], F32))
                          for i in range(2)]
                    EP = [php.enter_context(nc.psum_tensor(U(f"EPp{i}"), [128, 512], F32))
                          for i in range(4)]
                    TPB2 = [php.enter_context(nc.psum_tensor(U(f"TPB2{i}"), [128, 8, 128],
                                                             BF16))
                            for i in range(2)]
                    with nc.Block() as blk:
                        @blk.sync
                        def _(sy):
                            sy.wait_ge(sem["dw"], wconv_done)
                            for sc_ in range(NSC):
                                for b in range(8):
                                    gb = sc_ * 8 + b
                                    if gb >= 2:
                                        psc, pb = divmod(gb - 2, 8)
                                        sy.wait_ge(sem["ts"], ts_h(psc, 2 * pb + 1))
                                    dinc(sy.dma_start(
                                        out=W1P[gb % 2][:, :, :],
                                        in_=w1pk[b, :, :, :]),
                                        f"w1s{gb % 2}")
                                for dmh in range(2):
                                    for bk in range(8):
                                        gj = sc_ * 16 + dmh * 8 + bk
                                        if gj >= 2:
                                            psc, r = divmod(gj - 2, 16)
                                            pdmh, pbk = divmod(r, 8)
                                            sy.wait_ge(sem["ts"],
                                                       ts_eo(psc, pdmh,
                                                             pbk * 4 + 3))
                                        dinc(sy.dma_start(
                                            out=W2P[gj % 2][:, :, :],
                                            in_=w2pk[dmh, bk, :, :, :]),
                                            f"w2s{gj % 2}")

                        @blk.tensor
                        def _(t):
                            t.wait_ge(sem["dg"], cnt["dg"])
                            for sc_ in range(NSC):
                                for i_, tj in enumerate(TWALK):
                                    st, dmc = divmod(tj, 8)
                                    if i_ >= 2:
                                        t.wait_ge(sem["vs"],
                                                  vs0 + sc_ * 40 + i_ - 1)
                                    inc(t.transpose(
                                        TPB2[i_ % 2][:, 0, :],
                                        BUF[:, 4 * sc_ + st,
                                            dmc * 128:(dmc + 1) * 128],
                                        CN["ident_bf"][:, :]), "ts")
                                t.wait_ge(sem["vs"], vs0 + sc_ * 40 + 32)
                                for pc in range(NW1P):
                                    gb = sc_ * 8 + pc // 2
                                    t.wait_ge(sem[f"w1s{gb % 2}"],
                                              16 * (gb // 2 + 1 + w1s_base[gb % 2]))
                                    for f1 in range(2):
                                        fc = pc * 2 + f1
                                        gfc = sc_ * 32 + fc
                                        if gfc >= 2:
                                            psc, pfc = divmod(gfc - 2, 32)
                                            t.wait_ge(sem["ss"], ss_relu(psc, pfc))
                                        off = (pc % 2) * 256 + f1 * 128
                                        for k in range(8):
                                            inc(t.matmul(
                                                HP[fc % 2][:, :],
                                                W1P[gb % 2][:, k, off:off + 128],
                                                BUF[:, 4 * sc_ + k // 2,
                                                    (k % 2) * 512:
                                                    (k % 2) * 512 + 512],
                                                start=(k == 0), stop=(k == 7)), "ts")
                                for dmh in range(2):
                                    for fc in range(32):
                                        gj = sc_ * 16 + dmh * 8 + fc // 4
                                        if fc % 4 == 0:
                                            t.wait_ge(sem[f"w2s{gj % 2}"],
                                                      16 * (gj // 2 + 1
                                                            + w2s_base[gj % 2]))
                                        t.wait_ge(sem["ss"], ss_relu(sc_, fc))
                                        if fc == 0:
                                            if dmh == 1:
                                                t.wait_ge(sem["vs"], vs_ev(sc_, 3))
                                            elif sc_ >= 1:
                                                t.wait_ge(sem["vs"], vs_ev(sc_ - 1, 7))
                                        for st in range(4):
                                            inc(t.matmul(
                                                EP[st][:, :],
                                                HFM[:, fc, st * 128:(st + 1) * 128],
                                                W2P[gj % 2][:, fc % 4, :],
                                                start=(fc == 0), stop=(fc == 31)), "ts")

                        @blk.scalar
                        def _(s):
                            for sc_ in range(NSC):
                                for fc in range(32):
                                    s.wait_ge(sem["ts"], ts_h_half(sc_, fc))
                                    if sc_ >= 1:
                                        s.wait_ge(sem["ts"], ts_eo(sc_ - 1, 1, 31))
                                    inc(s.activation(HFM[:, fc, :], HP[fc % 2][:, :],
                                                     AF.Relu,
                                                     bias=CN["b1c"][:, fc:fc + 1]), "ss")

                        @blk.vector
                        def _(v):
                            for sc_ in range(NSC):
                                for i_, tj in enumerate(TWALK):
                                    tgt = 4 * (tj % 8) + tj // 8
                                    v.wait_ge(sem["ts"],
                                              ts0 + sc_ * 544 + min(i_ + 2, 32))
                                    inc(v.tensor_copy(
                                        BUF[:, 4 * sc_ + tgt // 8,
                                            (tgt % 8) * 128:(tgt % 8) * 128 + 128],
                                        TPB2[i_ % 2][:, 0, :]), "vs")
                                for k in range(8):
                                    dmh, st = divmod(k, 4)
                                    v.wait_ge(sem["ts"], ts_eo(sc_, dmh, 31))
                                    gk = sc_ * 8 + k
                                    if gk >= 2:
                                        v.wait_ge(sem[f"eots{gk % 2}"],
                                                  16 * ((gk - 2) // 2 + 1
                                                        + eot_base[gk % 2]))
                                    inc(v.tensor_add(
                                        EOT[gk % 2][:, :], EP[st][:, :],
                                        CN["b2r"][:, dmh * 512:(dmh + 1) * 512]), "vs")

                        @blk.gpsimd
                        def _(g):
                            for sc_ in range(NSC):
                                for k in range(8):
                                    dmh, st = divmod(k, 4)
                                    gk = sc_ * 8 + k
                                    g.wait_ge(sem["vs"], vs_ev(sc_, k))
                                    dinc(g.dma_start(
                                        out=eo_in[sc_ * SCH + st * 128:
                                                  sc_ * SCH + (st + 1) * 128,
                                                  dmh * 512:(dmh + 1) * 512],
                                        in_=EOT[gk % 2][:, :]), f"eots{gk % 2}")
                with nc.Block() as blk:
                    @blk.gpsimd
                    def _(g):
                        for i4 in range(2):
                            g.wait_ge(sem[f"eots{i4}"], cnt[f"eots{i4}"])
                        inc(g.collective_compute("AllGather", ALU.bypass,
                                                 replica_groups=[list(range(NCORE))],
                                                 ins=[eo_in[:, :].opt()],
                                                 outs=[eo_all[:, :].opt()]), "cc")

        def emit_gather_back():
            with ExitStack() as ph:
                GATH = ph.enter_context(nc.sbuf_tensor(U("GATH"), [128, NT, E], F32))
                REG.update(gath0=(GATH[:, 0:2, :], [128, 2, E], F32))
                with nc.Block() as blk:
                    @blk.gpsimd
                    def _(g):
                        g.wait_ge(sem["cc"], cnt["cc"])
                        for ch in range(2):
                            dinc(g.dma_gather(GATH[:, 8 * ch:8 * ch + 8, :],
                                              eo_all[:, :],
                                              idxown[:, 64 * ch:64 * ch + 64],
                                              1024, NREG[1024], E, transpose=False), "dg")
                            g.wait_ge(sem["dg"], cnt["dg"])

                    @blk.vector
                    def _(v):
                        v.wait_ge(sem["dg"], cnt["dg"])
                        v.wait_ge(sem["vs"], cnt["vs"])
                        for i in range(NT):
                            inc(v.scalar_tensor_tensor(RES[:, i, :], GATH[:, i, :],
                                                       coefT[:, i:i + 1], RES[:, i, :],
                                                       ALU.mult, ALU.add), "vs")

        def moe_sublayer(zero_tbl=False):
            if zero_tbl:
                with nc.Block() as blk:
                    @blk.gpsimd
                    def _(g):
                        for zk in range(TBL_ROWS // 256):
                            dinc(g.dma_start(out=tbl[zk * 256:(zk + 1) * 256, :],
                                             in_=CN["zcol"][:, :]), "dw")
                        g.wait_ge(sem["dw"], cnt["dw"])
            emit_ln_gate()
            emit_ags()
            emit_routing_scatter()
            if upto >= 4:
                emit_table_read()
            if upto >= 5:
                emit_ffn()
                emit_gather_back()
            fence()

        def emit_attention():
            import os
            _au = int(os.environ.get("ATTN_UPTO", "9"))
            with ExitStack() as att:
                LN2FM = att.enter_context(nc.sbuf_tensor(U("LN2FM"), [128, 8, TOK], BF16))
                VT = att.enter_context(nc.sbuf_tensor(U("VT"), [128, NT, 1040], BF16))
                REG.update(ln2fm0=(LN2FM[:, :, 0:256], [128, 8, 256], BF16),
                           vt0=(VT[:, 0, :], [128, 1040], BF16))

                t0, v0, s0 = cnt["ts"], cnt["vs"], cnt["ss"]
                with ExitStack() as ph:
                    TPB = [ph.enter_context(nc.psum_tensor(U(f"TPB{i}"), [128, 8, 128],
                                                           BF16))
                           for i in range(2)]
                    with nc.Block() as blk:
                        @blk.scalar
                        def _(s_):
                            for i in range(NT):
                                if i >= 2:
                                    s_.wait_ge(sem["vs"], v0 + 1 + 10 * (i - 2) + 2)
                                inc(s_.activation(SQT[i % 2][:, :], RES[:, i, :],
                                                  AF.Square), "ss")
                                s_.wait_ge(sem["vs"], v0 + 1 + 10 * i + 6)
                                inc(s_.activation(STAT[:, i, 6:7], STAT[:, i, 5:6],
                                                  AF.Sqrt), "ss")

                        @blk.vector
                        def _(v):
                            inc(v.memset(VT[:, :, :]
                                         .rearrange("p a (h z) -> p a h z", z=65)
                                         [:, :, :, 64:65], 1.0), "vs")
                            for i in range(NT):
                                inc(v.reduce_sum(STAT[:, i, 0:1], RES[:, i, :],
                                                 axis=AX.X), "vs")
                                v.wait_ge(sem["ss"], s0 + 2 * i + 1)
                                inc(v.reduce_sum(STAT[:, i, 1:2], SQT[i % 2][:, :],
                                                 axis=AX.X), "vs")
                                v.wait_ge(sem["vs"], cnt["vs"])
                                inc(v.tensor_scalar_mul(STAT[:, i, 2:3],
                                                        STAT[:, i, 0:1], 1.0 / E), "vs")
                                v.wait_ge(sem["vs"], cnt["vs"])
                                inc(v.tensor_mul(STAT[:, i, 3:4], STAT[:, i, 2:3],
                                                 STAT[:, i, 2:3]), "vs")
                                v.wait_ge(sem["vs"], cnt["vs"])
                                inc(v.scalar_tensor_tensor(STAT[:, i, 4:5],
                                                           STAT[:, i, 1:2], 1.0 / E,
                                                           STAT[:, i, 3:4],
                                                           ALU.mult, ALU.subtract), "vs")
                                v.wait_ge(sem["vs"], cnt["vs"])
                                inc(v.tensor_scalar_add(STAT[:, i, 5:6],
                                                        STAT[:, i, 4:5], EPS), "vs")
                                v.wait_ge(sem["ss"], s0 + 2 * i + 2)
                                inc(v.reciprocal(STAT[:, i, 7:8], STAT[:, i, 6:7]), "vs")
                                v.wait_ge(sem["vs"], cnt["vs"])
                                inc(v.scalar_tensor_tensor(STAT[:, i, 3:4],
                                                           STAT[:, i, 2:3], -1.0,
                                                           STAT[:, i, 7:8],
                                                           ALU.mult, ALU.mult), "vs")
                                v.wait_ge(sem["vs"], cnt["vs"])
                                if i >= 2:
                                    v.wait_ge(sem["ts"], t0 + 8 * (i - 1))
                                inc(v.tensor_scalar(LNB[i % 2][:, :], RES[:, i, :],
                                                    STAT[:, i, 7:8], STAT[:, i, 3:4],
                                                    ALU.mult, ALU.add), "vs")
                                v.wait_ge(sem["ts"], t0 + 8 * (i + 1))
                                inc(v.tensor_copy(LN2FM[:, :, i * 128:(i + 1) * 128],
                                                  TPB[i % 2][:, :, :]), "vs")

                        @blk.tensor
                        def _(t):
                            for i in range(NT):
                                t.wait_ge(sem["vs"], v0 + 1 + 10 * i + 9)
                                if i >= 2:
                                    t.wait_ge(sem["vs"], v0 + 1 + 10 * (i - 2) + 10)
                                for k in range(8):
                                    inc(t.transpose(TPB[i % 2][:, k, :],
                                                    LNB[i % 2][:, k * 128:(k + 1) * 128],
                                                    CN["ident_bf"][:, :]), "ts")
                fence()
                if _au < 2:
                    return

                t0v, v0v, dw0v = cnt["ts"], cnt["vs"], cnt["dwa"]
                with ExitStack() as ph:
                    WVP = ph.enter_context(nc.sbuf_tensor(U("WVP"), [128, 8, 512], BF16))
                    VP = [ph.enter_context(nc.psum_tensor(U(f"VP{i}"), [128, 512], F32))
                          for i in range(2)]
                    with nc.Block() as blk:
                        @blk.gpsimd
                        def _(g):
                            for grp in range(2):
                                if grp >= 1:
                                    g.wait_ge(sem["ts"], t0v + NT * 8)
                                dinc(g.dma_start(out=WVP[:, :, :],
                                                 in_=wv_v[grp, :, :, :]), "dwa")

                        @blk.tensor
                        def _(t):
                            for grp in range(2):
                                t.wait_ge(sem["dwa"], dw0v + 16 * (grp + 1))
                                for i in range(NT):
                                    gi = grp * NT + i
                                    if gi >= 2:
                                        t.wait_ge(sem["vs"], v0v + gi - 1)
                                    for k in range(8):
                                        inc(t.matmul(VP[gi % 2][:, :],
                                                     LN2FM[:, k, i * 128:(i + 1) * 128],
                                                     WVP[:, k, :],
                                                     start=(k == 0), stop=(k == 7)),
                                            "ts")

                        @blk.vector
                        def _(v):
                            for grp in range(2):
                                for i in range(NT):
                                    gi = grp * NT + i
                                    v.wait_ge(sem["ts"], t0v + (gi + 1) * 8)
                                    out = VT[:, i, :].rearrange("p (h z) -> p h z", z=65)
                                    inc(v.tensor_copy(
                                        out[:, 8 * grp:8 * grp + 8, 0:64],
                                        VP[gi % 2][:, :]
                                        .rearrange("p (h d) -> p h d", d=64)), "vs")
                fence()
                if _au < 3:
                    return

                with ExitStack() as ph:
                    QF = ph.enter_context(nc.sbuf_tensor(U("QF"), [128, TOK], BF16))
                    KF = ph.enter_context(nc.sbuf_tensor(U("KF"), [128, TOK], BF16))
                    PT = [ph.enter_context(nc.sbuf_tensor(U(f"PT{i}"), [128, 512], BF16))
                          for i in range(2)]
                    RCP = ph.enter_context(nc.sbuf_tensor(U("RCP"), [128, 512], F32))
                    RCPB = ph.enter_context(nc.sbuf_tensor(U("RCPB"), [128, 512], F32))
                    RA = ph.enter_context(nc.sbuf_tensor(U("RA"), [1, 512], F32))
                    RB = ph.enter_context(nc.sbuf_tensor(U("RB"), [1, 512], F32))
                    WQP = [ph.enter_context(nc.sbuf_tensor(U(f"WQP{i}"), [128, 8, 128],
                                                           BF16)) for i in range(2)]
                    WKP = [ph.enter_context(nc.sbuf_tensor(U(f"WKP{i}"), [128, 8, 128],
                                                           BF16)) for i in range(2)]
                    ONB = [ph.enter_context(nc.sbuf_tensor(U(f"ONB{i}"), [128, 512],
                                                           BF16)) for i in range(2)]
                    QP = [ph.enter_context(nc.psum_tensor(U(f"QPp{i}"), [128, 512], F32))
                          for i in range(2)]
                    SP = [ph.enter_context(nc.psum_tensor(U(f"SPp{i}"), [128, 512], F32))
                          for i in range(2)]
                    OP1 = ph.enter_context(nc.psum_tensor(U("OP1"), [128, 512], F32))
                    OP2 = ph.enter_context(nc.psum_tensor(U("OP2"), [128, 512], F32))
                    LP = ph.enter_context(nc.psum_tensor(U("LP"), [128, 512], F32))
                    ta0, sa0, va0, ga0 = (cnt["ts"], cnt["ss"], cnt["vs"],
                                          cnt["gs"])
                    wqs_base = [cnt["wqs0"] // 16, cnt["wqs1"] // 16]
                    ons_base = [cnt["ons0"] // 16, cnt["ons1"] // 16]
                    TPP, SPP, VPP, GPP = 264, 84, 52, 8

                    def t_pair(pr):
                        return ta0 + pr * TPP

                    def s_pair(pr):
                        return sa0 + pr * SPP

                    def v_pair(pr):
                        return va0 + pr * VPP

                    def g_pair(pr):
                        return ga0 + pr * GPP

                    npairs = 8 if _au >= 4 else min(_au - 2, 8)
                    for pr in range(npairs):
                        with nc.Block() as blk:
                            @blk.gpsimd
                            def _(g, pr=pr):
                                if pr >= 2:
                                    g.wait_ge(sem["ts"], t_pair(pr - 1))
                                dinc(g.dma_start(out=WQP[pr % 2][:, :, :],
                                                 in_=wq_v[pr, :, :, :]), f"wqs{pr % 2}")
                                dinc(g.dma_start(out=WKP[pr % 2][:, :, :],
                                                 in_=wk_v[pr, :, :, :]), f"wqs{pr % 2}")

                            @blk.tensor
                            def _(t, pr=pr):
                                t.wait_ge(sem[f"wqs{pr % 2}"],
                                          16 * (2 * (pr // 2 + 1) + wqs_base[pr % 2]))
                                for tc in range(4):
                                    if tc >= 2:
                                        t.wait_ge(sem["ss"], s_pair(pr) + (tc - 2) + 1)
                                    for k in range(8):
                                        inc(t.matmul(
                                            QP[tc % 2][:, :], WQP[pr % 2][:, k, :],
                                            LN2FM[:, k, tc * 512:(tc + 1) * 512],
                                            start=(k == 0), stop=(k == 7)), "ts")
                                for tc in range(4):
                                    if tc < 2:
                                        t.wait_ge(sem["ss"], s_pair(pr) + tc + 3)
                                    else:
                                        t.wait_ge(sem["vs"], v_pair(pr) + (tc - 2) + 1)
                                    for k in range(8):
                                        inc(t.matmul(
                                            QP[tc % 2][:, :], WKP[pr % 2][:, k, :],
                                            LN2FM[:, k, tc * 512:(tc + 1) * 512],
                                            start=(k == 0), stop=(k == 7)), "ts")

                            @blk.scalar
                            def _(s, pr=pr):
                                for tc in range(4):
                                    s.wait_ge(sem["ts"], t_pair(pr) + (tc + 1) * 8)
                                    inc(s.activation(QF[:, tc * 512:(tc + 1) * 512],
                                                     QP[tc % 2][:, :], AF.Copy,
                                                     scale=0.125), "ss")

                            @blk.vector
                            def _(v, pr=pr):
                                for tc in range(4):
                                    v.wait_ge(sem["ts"], t_pair(pr) + 32 + (tc + 1) * 8)
                                    inc(v.tensor_copy(KF[:, tc * 512:(tc + 1) * 512],
                                                      QP[tc % 2][:, :]), "vs")

                        h1, h2 = 2 * pr, 2 * pr + 1
                        with nc.Block() as blk:
                            @blk.tensor
                            def _(t, pr=pr, h1=h1, h2=h2):
                                tb = t_pair(pr) + 64
                                sB = s_pair(pr) + 4
                                vB = v_pair(pr) + 4
                                t.wait_ge(sem["ss"], sB)
                                t.wait_ge(sem["vs"], vB)
                                m = 0
                                for qc in range(4):
                                    d = 0
                                    for kt in range(NKT[qc]):
                                        if m >= 1:
                                            t.wait_ge(sem["ss"], sB + 2 * m)
                                        inc(t.matmul(
                                            SP[0][:, :],
                                            KF[0:64, kt * 128:(kt + 1) * 128],
                                            QF[0:64, qc * 512:(qc + 1) * 512],
                                            start=True, stop=True), "ts")
                                        inc(t.matmul(
                                            SP[1][:, :],
                                            KF[64:128, kt * 128:(kt + 1) * 128],
                                            QF[64:128, qc * 512:(qc + 1) * 512],
                                            start=True, stop=True,
                                            tile_position=(64, 0)), "ts")
                                        t.wait_ge(sem["ss"], sB + 2 * (m + 1))
                                        if kt >= 4 * qc:
                                            d += 1
                                            t.wait_ge(sem["vs"], vB + qc * 12 + 2 * d)
                                        if kt == 0:
                                            t.wait_ge(sem["vs"], vB + qc * 12)
                                        inc(t.matmul(
                                            OP1[0:65, :],
                                            VT[:, kt, h1 * 65:h1 * 65 + 65],
                                            PT[0][:, :], start=(kt == 0),
                                            stop=(kt == NKT[qc] - 1)), "ts")
                                        inc(t.matmul(
                                            OP2[64:128, :],
                                            VT[:, kt, h2 * 65:h2 * 65 + 64],
                                            PT[1][:, :], start=(kt == 0),
                                            stop=(kt == NKT[qc] - 1),
                                            tile_position=(0, 64)), "ts")
                                        inc(t.matmul(
                                            LP[0:1, :],
                                            VT[:, kt, h1 * 65 + 64:h1 * 65 + 65],
                                            PT[1][:, :], start=(kt == 0),
                                            stop=(kt == NKT[qc] - 1)), "ts")
                                        m += 1

                            @blk.scalar
                            def _(s, pr=pr):
                                tb = t_pair(pr) + 64
                                m = 0
                                for qc in range(4):
                                    for kt in range(NKT[qc]):
                                        s.wait_ge(sem["ts"], tb + 5 * m + 2)
                                        inc(s.activation(PT[0][:, :], SP[0][:, :],
                                                         AF.Exp), "ss")
                                        inc(s.activation(PT[1][:, :], SP[1][:, :],
                                                         AF.Exp), "ss")
                                        m += 1

                            @blk.vector
                            def _(v, pr=pr):
                                tb = t_pair(pr) + 64
                                sB = s_pair(pr) + 4
                                gB = g_pair(pr)
                                m = 0
                                for qc in range(4):
                                    for kt in range(NKT[qc]):
                                        if kt >= 4 * qc:
                                            v.wait_ge(sem["ss"], sB + 2 * (m + 1))
                                            om = kt - 4 * qc
                                            inc(v.tensor_tensor(PT[0][:, :],
                                                                PT[0][:, :],
                                                                CN["cmask"][:, om, :],
                                                                ALU.mult), "vs")
                                            inc(v.tensor_tensor(PT[1][:, :],
                                                                PT[1][:, :],
                                                                CN["cmask"][:, om, :],
                                                                ALU.mult), "vs")
                                        m += 1
                                    v.wait_ge(sem["ts"], tb + 5 * CUMKT[qc])
                                    inc(v.reciprocal(RA[0:1, :], OP1[64:65, :]), "vs")
                                    inc(v.reciprocal(RB[0:1, :], LP[0:1, :]), "vs")
                                    v.wait_ge(sem["gs"], gB + 2 * (qc + 1))
                                    n_on = pr * 4 + qc
                                    j_on = n_on % 2
                                    if n_on >= 2:
                                        v.wait_ge(sem[f"ons{j_on}"],
                                                  16 * ((n_on - 2) // 2 + 1
                                                        + ons_base[j_on]))
                                    inc(v.tensor_tensor(
                                        ONB[j_on][0:64, :],
                                        OP1[0:64, :], RCP[0:64, :], ALU.mult), "vs")
                                    inc(v.tensor_tensor(
                                        ONB[j_on][64:128, :],
                                        OP2[64:128, :], RCPB[64:128, :], ALU.mult), "vs")

                            @blk.gpsimd
                            def _(g, pr=pr):
                                vB = v_pair(pr) + 4
                                for qc in range(4):
                                    g.wait_ge(sem["vs"], vB + qc * 12 + 10)
                                    inc(g.partition_broadcast(RCP[0:64, :],
                                                              RA[0:1, :]), "gs")
                                    inc(g.partition_broadcast(RCPB[:, :],
                                                              RB[0:1, :]), "gs")
                                    n_on = pr * 4 + qc
                                    g.wait_ge(sem["vs"], vB + qc * 12 + 12)
                                    dinc(g.dma_start(
                                        out=onorm_d[pr, :,
                                                    qc * 512:(qc + 1) * 512],
                                        in_=ONB[n_on % 2][:, :]),
                                        f"ons{n_on % 2}")
                fence()
                if _au < 4:
                    return

                tp0, vp0, dwp0 = cnt["ts"], cnt["vs"], cnt["dwa"]
                with ExitStack() as ph:
                    WOP = ph.enter_context(nc.sbuf_tensor(U("WOP"), [128, 8, E], BF16))
                    OIN = [ph.enter_context(nc.sbuf_tensor(U(f"OIN{i}"), [128, 8, 128],
                                                           BF16)) for i in range(2)]
                    APP = [ph.enter_context(nc.psum_tensor(U(f"APP{i}"), [128, 512], F32))
                           for i in range(2)]
                    oin_base = [cnt["oin0"] // 16, cnt["oin1"] // 16]
                    with nc.Block() as blk:
                        @blk.gpsimd
                        def _(g):
                            dinc(g.dma_start(out=WOP[:, :, :], in_=wo_v[:, :, :]), "dwa")
                            for qt in range(NT):
                                if qt >= 2:
                                    g.wait_ge(sem["ts"], tp0 + 16 * (qt - 1))
                                dinc(g.dma_start(
                                    out=OIN[qt % 2][:, :, :],
                                    in_=onorm_d[:, :, qt * 128:(qt + 1) * 128]
                                    .rearrange("c p t -> p c t")), f"oin{qt % 2}")

                        @blk.tensor
                        def _(t):
                            t.wait_ge(sem["dwa"], dwp0 + 16)
                            n = 0
                            for qt in range(NT):
                                t.wait_ge(sem[f"oin{qt % 2}"],
                                          16 * (qt // 2 + 1 + oin_base[qt % 2]))
                                for eh in range(2):
                                    if n >= 2:
                                        t.wait_ge(sem["vs"], vp0 + n - 1)
                                    for c in range(8):
                                        inc(t.matmul(
                                            APP[n % 2][:, :],
                                            OIN[qt % 2][:, c, :],
                                            WOP[:, c, eh * 512:(eh + 1) * 512],
                                            start=(c == 0), stop=(c == 7)), "ts")
                                    n += 1

                        @blk.vector
                        def _(v):
                            n = 0
                            for qt in range(NT):
                                for eh in range(2):
                                    v.wait_ge(sem["ts"], tp0 + (n + 1) * 8)
                                    inc(v.tensor_add(
                                        RES[:, qt, eh * 512:(eh + 1) * 512],
                                        APP[n % 2][:, :],
                                        RES[:, qt, eh * 512:(eh + 1) * 512]), "vs")
                                    n += 1
                fence()

        if upto >= 3:
            moe_sublayer()
        elif upto >= 2:
            emit_ln_gate()
            emit_ags()
            fence()
        if upto >= 6:
            emit_attention()
        if upto >= 7:
            moe_sublayer(zero_tbl=True)

        with nc.Block() as blk:
            @blk.sync
            def _(sy):
                for k in ("ts", "vs", "ss", "gs", "dsc", "dg", "dat", "cc"):
                    if cnt[k]:
                        sy.wait_ge(sem[k], cnt[k])
                sy.wait_ge(sem["dx"], dx_done)
                dinc(sy.dma_start(out=y_tm, in_=RES[:, :, :]), "dout")
                sy.wait_ge(sem["dout"], cnt["dout"])

        dump_outs = []
        for nm in dumps:
            ap, shape, dt = REG[nm]
            od = nc.dram_tensor("dump_" + nm, shape, dt, kind="ExternalOutput").ap()
            dump_outs.append((od, ap))
        if dump_outs:
            with nc.Block() as blk:
                @blk.sync
                def _(sy):
                    for od, ap in dump_outs:
                        dinc(sy.dma_start(out=od[...], in_=ap), "dout")
                    sy.wait_ge(sem["dout"], cnt["dout"])

    mybir.codegen_inst_isa_subclasses(nc)
    return nc


def kernel(**inputs):
    per_core = _host_pack(inputs)
    nc = build_program(upto=7)
    res = run_bass_kernel_spmd(nc, per_core, core_ids=list(range(NCORE)))
    out = np.stack([np.asarray(res.results[i]["y"]).reshape(S, E) for i in range(NCORE)])
    return out.astype(np.float32)


# version 04
# speedup vs baseline: 1.1139x; 1.0476x over previous
import numpy as np
import ml_dtypes
import concourse.bass as bass
import concourse.mybir as mybir
from concourse.bass_utils import run_bass_kernel_spmd
from concourse.library_config import mlp as _mlplib
from contextlib import ExitStack

F32 = mybir.dt.float32
BF16 = mybir.dt.bfloat16
I16 = mybir.dt.int16
U8 = mybir.dt.uint8
AF = mybir.ActivationFunctionType
ALU = mybir.AluOpType
AX = mybir.AxisListType
_DETECT_RACES = True

NCORE = 8
B, S, E = 8, 2048, 1024
H, DH = 16, 64
NE, FF, CAP = 8, 4096, 2560
T = B * S
TOK = S
NT = TOK // 128
NP = T // 128
TRASH = CAP
TBL_ROWS = CAP + 256
CW = CAP // 16
EPS = 1e-5
SCH = 512
NSC = CAP // SCH
W1PF = 256
NW1P = FF // W1PF
NKT = [4, 8, 12, 16]
CUMKT = [4, 12, 24, 40]

_TPERM = [4 * (j % 8) + j // 8 for j in range(32)]
TWALK = []
_seen = [False] * 32
for _s0 in range(32):
    _c = _s0
    while not _seen[_c]:
        _seen[_c] = True
        TWALK.append(_c)
        _c = _TPERM[_c]


def _host_pack(inputs):
    x = np.asarray(inputs["L0_attn_x"])
    wqkv = np.asarray(inputs["L0_attn_w_qkv"])
    wout = np.asarray(inputs["L0_attn_w_out"])
    gw = np.asarray(inputs["L0_ffn_gate_w"])
    gb = np.asarray(inputs["L0_ffn_gate_b"])
    w1 = np.asarray(inputs["L0_ffn_w1"])
    b1 = np.asarray(inputs["L0_ffn_b1"])
    w2 = np.asarray(inputs["L0_ffn_w2"])
    b2 = np.asarray(inputs["L0_ffn_b2"])

    wq = np.stack([np.concatenate([wqkv[:, 2 * p, 0:64], wqkv[:, 2 * p + 1, 0:64]], 1)
                   for p in range(8)]).astype(np.float32)
    wk = np.stack([np.concatenate([wqkv[:, 2 * p, 64:128], wqkv[:, 2 * p + 1, 64:128]], 1)
                   for p in range(8)]).astype(np.float32)
    wv = np.stack([np.concatenate([wqkv[:, 8 * g + h, 128:192] for h in range(8)], 1)
                   for g in range(2)]).astype(np.float32)
    wo = np.ascontiguousarray(wout.reshape(H * DH, E)).astype(np.float32)
    wq = np.ascontiguousarray(wq.reshape(8, 8, 128, 128).transpose(0, 2, 1, 3))
    wk = np.ascontiguousarray(wk.reshape(8, 8, 128, 128).transpose(0, 2, 1, 3))
    wv = np.ascontiguousarray(wv.reshape(2, 8, 128, 512).transpose(0, 2, 1, 3))
    wo = np.ascontiguousarray(wo.reshape(8, 128, 1024).transpose(1, 0, 2))

    gwT = np.ascontiguousarray(gw.T.reshape(8, 128, 8).transpose(1, 0, 2)).astype(np.float32)
    gb_dev = np.tile(np.asarray(gb)[None, :], (128, 1)).astype(np.float32)
    b1col = np.ascontiguousarray(b1.reshape(NE, 32, 128).transpose(0, 2, 1)).astype(np.float32)
    b2rep = np.tile(np.asarray(b2)[:, None, :], (1, 128, 1)).astype(np.float32)

    ident_bf = np.eye(128).astype(ml_dtypes.bfloat16)
    ident_f32 = np.eye(128).astype(np.float32)
    trils = np.triu(np.ones((128, 128)), 1).astype(ml_dtypes.bfloat16)
    iota8 = np.tile(np.arange(8, dtype=np.float32)[None, :], (128, 1))
    rowiota = (np.arange(16, dtype=np.float32)[:, None] * CW
               + np.arange(CW, dtype=np.float32)[None, :]).astype(np.float32)
    zeros16 = np.zeros((16, CW), np.float32)
    ones128 = np.ones((128, 1), dtype=ml_dtypes.bfloat16)
    th16 = np.tile((160.0 * np.arange(1, 17, dtype=np.float32))[None, :], (128, 1))
    gid128 = (np.arange(128, dtype=np.float32)[:, None]
              + 128.0 * np.arange(128, dtype=np.float32)[None, :])
    zcol = np.zeros((128, 128), np.float32)
    kk = np.arange(128)[:, None]
    qq = np.arange(512)[None, :]
    cmask = np.stack([(qq >= kk + 128 * o) for o in range(4)]).astype(ml_dtypes.bfloat16)

    per_core = []
    for ci in range(NCORE):
        esel = np.zeros((128, 16), np.float32)
        for m in range(16):
            esel[16 * ci + m, m] = 1.0
        per_core.append({
            "x_my": np.ascontiguousarray(x[ci]).astype(np.float32),
            "w1_my": np.ascontiguousarray(w1[ci]).astype(np.float32),
            "w2_my": np.ascontiguousarray(w2[ci]).astype(np.float32),
            "b1_my": b1col[ci], "b2_my": b2rep[ci],
            "gwT": gwT, "gb": gb_dev,
            "wq": wq, "wk": wk, "wv": wv, "wo": wo,
            "ident_bf": ident_bf, "ident_f32": ident_f32, "trils": trils,
            "iota8": iota8, "rowiota": rowiota, "zeros16": zeros16, "zcol": zcol,
            "cmask": np.ascontiguousarray(cmask),
            "gid128": gid128, "ones128": ones128, "th16": th16,
            "mye": np.full((128, 1), float(ci), np.float32),
            "myoh": np.eye(8, dtype=np.float32)[ci:ci + 1].copy(),
            "esel": esel,
        })
    return per_core


def build_program(upto=99, dumps=()):
    nc = bass.Bass("TRN2", target_bir_lowering=False, detect_race_conditions=_DETECT_RACES)
    X = {}

    def ein(name, shape, dt):
        X[name] = nc.dram_tensor(name, shape, dt, kind="ExternalInput").ap()

    for name, shape, dt in [
            ("x_my", [TOK, E], F32), ("w1_my", [E, FF], F32), ("w2_my", [FF, E], F32),
            ("b1_my", [128, 32], F32), ("b2_my", [128, E], F32),
            ("gwT", [128, 8, 8], F32), ("gb", [128, 8], F32),
            ("wq", [8, 128, 8, 128], F32), ("wk", [8, 128, 8, 128], F32),
            ("wv", [2, 128, 8, 512], F32), ("wo", [128, 8, E], F32),
            ("ident_bf", [128, 128], BF16), ("ident_f32", [128, 128], F32),
            ("trils", [128, 128], BF16), ("iota8", [128, 8], F32),
            ("rowiota", [16, CW], F32), ("zeros16", [16, CW], F32),
            ("zcol", [128, 128], F32), ("cmask", [4, 128, 512], BF16),
            ("mye", [128, 1], F32), ("myoh", [1, 8], F32), ("esel", [128, 16], F32),
            ("gid128", [128, 128], F32), ("ones128", [128, 1], BF16),
            ("th16", [128, 16], F32)]:
        ein(name, shape, dt)

    y = nc.dram_tensor("y", [TOK, E], F32, kind="ExternalOutput").ap()

    w1pk = nc.dram_tensor("w1pk", [8, 128, 8, 512], BF16).ap()
    w2pk = nc.dram_tensor("w2pk", [2, 8, 128, 4, 512], BF16).ap()
    ln_in = nc.dram_tensor("ln_in", [TOK, E], BF16).ap()
    ln_all = nc.dram_tensor("ln_all", [T, E], BF16).ap()
    lg_in = nc.dram_tensor("lg_in", [TOK, 8], F32).ap()
    lg_all = nc.dram_tensor("lg_all", [T, 8], F32).ap()
    eo_in = nc.dram_tensor("eo_in", [CAP, E], F32).ap()
    eo_all = nc.dram_tensor("eo_all", [NE * CAP, E], F32).ap()
    tbl = nc.dram_tensor("tbl", [TBL_ROWS, 64], F32).ap()
    dscat = nc.dram_tensor("dscat", [16, NP * 8], I16).ap()
    onorm_d = nc.dram_tensor("onorm_d", [8, 128, TOK], BF16).ap()
    down = nc.dram_tensor("down", [16, 128], I16).ap()

    es = ExitStack()
    sem, cnt = {}, {}

    def mksem(*names):
        for n in names:
            sem[n] = es.enter_context(nc.semaphore(n))
            cnt[n] = 0

    def inc(inst, name, by=1):
        cnt[name] += by
        return inst.then_inc(sem[name], by)

    def dinc(inst, name):
        return inc(inst, name, 16)

    REG = {}
    NREG = {}
    _uqc = [0]

    def U(nm):
        _uqc[0] += 1
        return f"{nm}_u{_uqc[0]}"

    with es:
        sb = nc.alloc_sbuf_tensor
        RES = sb("RES", [128, NT, E], F32)
        CN = {}
        for nm, shape, dt in [("ident_bf", [128, 128], BF16), ("ident_f32", [128, 128], F32),
                              ("trils", [128, 128], BF16), ("iota8", [128, 8], F32),
                              ("rowiota", [16, CW], F32), ("zeros16", [16, CW], F32),
                              ("zcol", [128, 128], F32), ("cmask", [128, 4, 512], BF16),
                              ("gid128", [128, 128], F32), ("ones128", [128, 1], BF16),
                              ("th16", [128, 16], F32),
                              ("mye", [128, 1], F32), ("myoh", [1, 8], F32),
                              ("esel", [128, 16], F32), ("gwT", [128, 8, 8], F32),
                              ("gb", [128, 8], F32), ("b1c", [128, 32], F32),
                              ("b2r", [128, E], F32)]:
            CN[nm] = sb("c_" + nm, shape, dt)
        TOTB = sb("TOTB", [128, 8], BF16)
        EXCL = sb("EXCL", [128, 8], F32)
        CNTS = sb("CNTS", [1, 8], F32)
        CNT1 = sb("CNT1", [16, 1], F32)
        CNTsc = sb("CNTsc", [1, 1], F32)
        coefT = sb("coefT", [128, 16], F32)
        ownsl = sb("ownsl", [16, 128], F32)
        idxown = sb("idxown", [128, 128], I16)
        idxg = sb("idxg", [128, CW], I16)
        idg16 = sb("idg16", [16, CW], I16)
        selg = sb("selg", [16, CW], F32)
        maskv = sb("maskv", [16, CW], U8)
        LGT = sb("LGT", [128, NT, 8], F32)
        SQT = [sb(f"SQT{i}", [128, E], F32) for i in range(2)]
        STAT = sb("STAT", [128, NT, 8], F32)
        LNT = [sb(f"LNT{i}", [128, E], F32) for i in range(2)]
        LNB = [sb(f"LNB{i}", [128, E], BF16) for i in range(2)]
        idxsc = sb("idxsc", [128, NP * 8], I16)
        REG.update(lgt=(LGT[:, :, :], [128, NT, 8], F32),
                   cnts=(CNTS[:, :], [1, 8], F32), coeft=(coefT[:, :], [128, 16], F32),
                   ownsl=(ownsl[:, :], [16, 128], F32),
                   idg16=(idg16[:, :], [16, CW], I16),
                   idxown=(idxown[:, :], [128, 128], I16),
                   res0=(RES[:, 0:2, :], [128, 2, E], F32))
        mksem("dx", "dw", "dwa", "dsc", "dg", "cc", "ts", "vs", "ss", "gs",
              "dat", "dout", "lnr0", "lnr1", "w1s0", "w1s1", "w2s0", "w2s1",
              "eots0", "eots1", "eots2", "eots3", "wqs0", "wqs1",
              "ons0", "ons1", "oin0", "oin1", "cv0", "cv1")

        x_tm = X["x_my"].rearrange("(a p) e -> p a e", p=128)
        ln_in_tm = ln_in.rearrange("(a p) e -> p a e", p=128)
        lg_in_tm = lg_in.rearrange("(a p) e -> p a e", p=128)
        y_tm = y.rearrange("(a p) e -> p a e", p=128)
        wq_v = X["wq"]
        wk_v = X["wk"]
        wv_v = X["wv"]
        wo_v = X["wo"]

        with ExitStack() as p0s:
            WCV = p0s.enter_context(nc.sbuf_tensor(U("WCV"), [128, FF], BF16))
            blk = p0s.enter_context(nc.Block())
            _ = blk

            @blk.sync
            def _(sy):
                dinc(sy.dma_start(out=RES[:, :, :], in_=x_tm), "dx")
                for nm in ("ident_bf", "ident_f32", "trils", "iota8", "rowiota",
                           "zeros16", "zcol", "gid128", "ones128", "th16", "mye", "myoh", "esel",
                           "gwT", "gb"):
                    dinc(sy.dma_start(out=CN[nm][...], in_=X[nm][...]), "dx")
                dinc(sy.dma_start(out=CN["cmask"][:, :, :],
                                  in_=X["cmask"].rearrange("o p q -> p o q")), "dx")
                dinc(sy.dma_start(out=CN["b1c"][:, :], in_=X["b1_my"][:, :]), "dx")
                dinc(sy.dma_start(out=CN["b2r"][:, :], in_=X["b2_my"][:, :]), "dx")

            @blk.gpsimd
            def _(g):
                g.load_library(_mlplib)
                for v in (896, 256, 1024, 512):
                    NREG[v] = g.alloc_register(f"nreg{v}")
                    g.reg_mov(NREG[v], v)
                g.wait_ge(sem["dx"], cnt["dx"])
                for zk in range(TBL_ROWS // 256):
                    dinc(g.dma_start(out=tbl[zk * 256:(zk + 1) * 256, :],
                                     in_=CN["zcol"][:, :]), "dw")
                for piece in range(8):
                    dinc(g.dma_start(out=WCV[:, :],
                                     in_=X["w1_my"][piece * 128:(piece + 1) * 128, :]), "dw")
                    g.wait_ge(sem["dw"], cnt["dw"])
                    dinc(g.dma_start(
                        out=w1pk[:, :, piece, :].rearrange("b p f -> p b f"),
                        in_=WCV[:, :].rearrange("p (b f) -> p b f", b=8)), "dw")
                    g.wait_ge(sem["dw"], cnt["dw"])
                for piece in range(32):
                    dinc(g.dma_start(out=WCV[:, 0:E],
                                     in_=X["w2_my"][piece * 128:(piece + 1) * 128, :]), "dw")
                    g.wait_ge(sem["dw"], cnt["dw"])
                    dinc(g.dma_start(
                        out=w2pk[:, piece // 4, :, piece % 4, :]
                        .rearrange("d p f -> p d f"),
                        in_=WCV[:, 0:E].rearrange("p (d f) -> p d f", d=2)), "dw")
                    g.wait_ge(sem["dw"], cnt["dw"])
        wconv_done = cnt["dw"]
        dx_done = cnt["dx"]

        def fence():
            snap = {k: v for k, v in cnt.items() if v and k != "dout"}
            with nc.Block() as blk:
                for eng_name in ("sync", "gpsimd", "tensor", "scalar", "vector"):
                    def body(eng, _snap=snap):
                        for k, v in _snap.items():
                            eng.wait_ge(sem[k], v)
                    getattr(blk, eng_name)(body)

        def emit_ln_gate():
            t0, v0, s0, d0 = cnt["ts"], cnt["vs"], cnt["ss"], cnt["dsc"]
            lnr_base = [cnt["lnr0"] // 16, cnt["lnr1"] // 16]
            with ExitStack() as ph:
                TPS = [ph.enter_context(nc.psum_tensor(U(f"TPS{i}"), [128, 8, 128], F32))
                       for i in range(2)]
                LNFM = [ph.enter_context(nc.sbuf_tensor(U(f"LNFM{i}"),
                                                        [128, 8, 128], F32))
                        for i in range(2)]
                LGP = [ph.enter_context(nc.psum_tensor(U(f"LGP{i}"), [128, 8], F32))
                       for i in range(2)]
                with nc.Block() as blk:
                    @blk.scalar
                    def _(s_):
                        s_.wait_ge(sem["dx"], dx_done)
                        for i in range(NT):
                            if i >= 2:
                                s_.wait_ge(sem["vs"], v0 + 11 * (i - 2) + 2)
                            inc(s_.activation(SQT[i % 2][:, :], RES[:, i, :],
                                              AF.Square), "ss")
                            s_.wait_ge(sem["vs"], v0 + 11 * i + 6)
                            inc(s_.activation(STAT[:, i, 6:7], STAT[:, i, 5:6],
                                              AF.Sqrt), "ss")

                    @blk.vector
                    def _(v):
                        v.wait_ge(sem["dx"], dx_done)
                        for i in range(NT):
                            inc(v.reduce_sum(STAT[:, i, 0:1], RES[:, i, :],
                                             axis=AX.X), "vs")
                            v.wait_ge(sem["ss"], s0 + 2 * i + 1)
                            inc(v.reduce_sum(STAT[:, i, 1:2], SQT[i % 2][:, :],
                                             axis=AX.X), "vs")
                            v.wait_ge(sem["vs"], cnt["vs"])
                            inc(v.tensor_scalar_mul(STAT[:, i, 2:3], STAT[:, i, 0:1],
                                                    1.0 / E), "vs")
                            v.wait_ge(sem["vs"], cnt["vs"])
                            inc(v.tensor_mul(STAT[:, i, 3:4], STAT[:, i, 2:3],
                                             STAT[:, i, 2:3]), "vs")
                            v.wait_ge(sem["vs"], cnt["vs"])
                            inc(v.scalar_tensor_tensor(STAT[:, i, 4:5], STAT[:, i, 1:2],
                                                       1.0 / E, STAT[:, i, 3:4],
                                                       ALU.mult, ALU.subtract), "vs")
                            v.wait_ge(sem["vs"], cnt["vs"])
                            inc(v.tensor_scalar_add(STAT[:, i, 5:6], STAT[:, i, 4:5],
                                                    EPS), "vs")
                            v.wait_ge(sem["ss"], s0 + 2 * i + 2)
                            inc(v.reciprocal(STAT[:, i, 7:8], STAT[:, i, 6:7]), "vs")
                            v.wait_ge(sem["vs"], cnt["vs"])
                            inc(v.scalar_tensor_tensor(STAT[:, i, 3:4], STAT[:, i, 2:3],
                                                       -1.0, STAT[:, i, 7:8],
                                                       ALU.mult, ALU.mult), "vs")
                            v.wait_ge(sem["vs"], cnt["vs"])
                            if i >= 2:
                                v.wait_ge(sem[f"lnr{i % 2}"], 16 * ((i - 2) // 2 + 1 + lnr_base[i % 2]))
                                v.wait_ge(sem["ts"], t0 + 16 * (i - 2) + 8)
                            inc(v.tensor_scalar(LNT[i % 2][:, :], RES[:, i, :],
                                                STAT[:, i, 7:8], STAT[:, i, 3:4],
                                                ALU.mult, ALU.add), "vs")
                            v.wait_ge(sem["ts"], t0 + 16 * i + 8)
                            inc(v.tensor_copy(LNFM[i % 2][:, :, :],
                                              TPS[i % 2][:, :, :]), "vs")
                            v.wait_ge(sem["ts"], t0 + 16 * i + 16)
                            inc(v.tensor_add(LGT[:, i, :], LGP[i % 2][:, :],
                                             CN["gb"][:, :]), "vs")

                    @blk.gpsimd
                    def _(g):
                        for i in range(NT):
                            g.wait_ge(sem["vs"], v0 + 11 * i + 9)
                            dinc(g.dma_start(out=ln_in_tm[:, i, :],
                                             in_=LNT[i % 2][:, :]), f"lnr{i % 2}")

                    @blk.tensor
                    def _(t):
                        for i in range(NT):
                            t.wait_ge(sem["vs"], v0 + 11 * i + 9)
                            if i >= 2:
                                t.wait_ge(sem["vs"], v0 + 11 * (i - 2) + 10)
                            for k in range(8):
                                inc(t.transpose(TPS[i % 2][:, k, :],
                                                LNT[i % 2][:, k * 128:(k + 1) * 128],
                                                CN["ident_f32"][:, :]), "ts")
                            t.wait_ge(sem["vs"], v0 + 11 * i + 10)
                            if i >= 2:
                                t.wait_ge(sem["vs"], v0 + 11 * (i - 2) + 11)
                            for k in range(8):
                                inc(t.matmul(LGP[i % 2][:, :], LNFM[i % 2][:, k, :],
                                             CN["gwT"][:, k, :],
                                             start=(k == 0), stop=(k == 7)), "ts")
            with nc.Block() as blk:
                @blk.gpsimd
                def _(g):
                    g.wait_ge(sem["vs"], cnt["vs"])
                    dinc(g.dma_start(out=lg_in_tm[:, :, :], in_=LGT[:, :, :]), "dsc")

        def emit_ags():
            with nc.Block() as blk:
                @blk.gpsimd
                def _(g):
                    g.wait_ge(sem["dsc"], cnt["dsc"])
                    g.wait_ge(sem["lnr0"], cnt["lnr0"])
                    g.wait_ge(sem["lnr1"], cnt["lnr1"])
                    inc(g.collective_compute("AllGather", ALU.bypass,
                                             replica_groups=[list(range(NCORE))],
                                             ins=[lg_in[:, :].opt()],
                                             outs=[lg_all[:, :].opt()]), "cc")
                    inc(g.collective_compute("AllGather", ALU.bypass,
                                             replica_groups=[list(range(NCORE))],
                                             ins=[ln_in[:, :].opt()],
                                             outs=[ln_all[:, :].opt()]), "cc")

        def emit_routing_scatter():
            with ExitStack() as ph:
                LG = ph.enter_context(nc.sbuf_tensor(U("LG"), [128, NP, 8], F32))
                EXV = ph.enter_context(nc.sbuf_tensor(U("EXV"), [128, NP, 8], F32))
                OHV = ph.enter_context(nc.sbuf_tensor(U("OHV"), [128, NP, 8], F32))
                INCL = ph.enter_context(nc.sbuf_tensor(U("INCL"), [128, NP, 8], F32))
                SCR8 = ph.enter_context(nc.sbuf_tensor(U("SCR8"), [128, NP, 8], F32))
                RM = ph.enter_context(nc.sbuf_tensor(U("RM"), [128, NP], F32))
                RS = ph.enter_context(nc.sbuf_tensor(U("RS"), [128, NP], F32))
                TP = ph.enter_context(nc.sbuf_tensor(U("TP"), [128, NP], F32))
                ET = ph.enter_context(nc.sbuf_tensor(U("ET"), [128, NP], F32))
                POS = ph.enter_context(nc.sbuf_tensor(U("POS"), [128, NP], F32))
                KEEP = ph.enter_context(nc.sbuf_tensor(U("KEEP"), [128, NP], F32))
                COEF = ph.enter_context(nc.sbuf_tensor(U("COEF"), [128, NP], F32))
                SA = ph.enter_context(nc.sbuf_tensor(U("SA"), [128, NP], F32))
                SB_ = ph.enter_context(nc.sbuf_tensor(U("SB_"), [128, NP], F32))
                gidrep = ph.enter_context(nc.sbuf_tensor(U("gidrep"), [128, NP, 64], F32))
                SB2 = ph.enter_context(nc.sbuf_tensor(U("SB2"), [128, NP], F32))
                OW2 = ph.enter_context(nc.sbuf_tensor(U("OW2"), [16, 128], F32))
                TMP16 = ph.enter_context(nc.sbuf_tensor(U("TMP16"), [128, NP, 16], F32))
                REG.update(et=(ET[:, :], [128, NP], F32), pos=(POS[:, :], [128, NP], F32),
                           keep=(KEEP[:, :], [128, NP], F32),
                           coef=(COEF[:, :], [128, NP], F32),
                           tp=(TP[:, :], [128, NP], F32))
                iota_bc = CN["iota8"][:, :].unsqueeze(1).broadcast_to([128, NP, 8])
                with ExitStack() as php:
                    EXP_ = php.enter_context(nc.psum_tensor(U("EXCP"), [128, 8], F32))
                    CNP = php.enter_context(nc.psum_tensor(U("CNP"), [1, 8], F32))
                    OSP = php.enter_context(nc.psum_tensor(U("OSP"), [16, 128], F32))
                    CTP = php.enter_context(nc.psum_tensor(U("CTP"), [128, 16], F32))
                    with nc.Block() as blk:
                        @blk.sync
                        def _(sy):
                            sy.wait_ge(sem["cc"], cnt["cc"])
                            dinc(sy.dma_start(
                                out=LG[:, :, :],
                                in_=lg_all.rearrange("(p a) e -> p a e", p=128)), "dat")

                        @blk.scalar
                        def _(s):
                            s.wait_ge(sem["dat"], cnt["dat"])
                            inc(s.activation(EXV[:, :, :], LG[:, :, :], AF.Exp), "ss")

                        @blk.vector
                        def _(v):
                            v.wait_ge(sem["ss"], cnt["ss"])
                            inc(v.reduce_max(RM[:, :], EXV[:, :, :], axis=AX.X), "vs")
                            inc(v.reduce_sum(RS[:, :], EXV[:, :, :], axis=AX.X), "vs")
                            v.wait_ge(sem["vs"], cnt["vs"])
                            inc(v.reciprocal(TP[:, :], RS[:, :]), "vs")
                            v.wait_ge(sem["vs"], cnt["vs"])
                            inc(v.tensor_tensor(TP[:, :], TP[:, :], RM[:, :],
                                                ALU.mult), "vs")
                            rm_bc = RM[:, :].unsqueeze(2).broadcast_to([128, NP, 8])
                            v.wait_ge(sem["vs"], cnt["vs"])
                            inc(v.tensor_tensor(SCR8[:, :, :], EXV[:, :, :], rm_bc,
                                                ALU.is_lt), "vs")
                            v.wait_ge(sem["vs"], cnt["vs"])
                            inc(v.scalar_tensor_tensor(SCR8[:, :, :], SCR8[:, :, :], 9.0,
                                                       iota_bc, ALU.mult, ALU.add), "vs")
                            v.wait_ge(sem["vs"], cnt["vs"])
                            inc(v.tensor_reduce(ET[:, :], SCR8[:, :, :], axis=AX.X,
                                                op=ALU.min), "vs")
                            et_bc = ET[:, :].unsqueeze(2).broadcast_to([128, NP, 8])
                            v.wait_ge(sem["vs"], cnt["vs"])
                            inc(v.tensor_tensor(OHV[:, :, :], iota_bc, et_bc,
                                                ALU.is_equal), "vs")
                            v.wait_ge(sem["vs"], cnt["vs"])
                            for e in range(8):
                                inc(v.tensor_tensor_scan(INCL[:, :, e], OHV[:, :, e],
                                                         CN["zcol"][:, :], 0.0,
                                                         ALU.add, ALU.add), "vs")
                            v.wait_ge(sem["vs"], cnt["vs"])
                            inc(v.tensor_copy(TOTB[:, :], INCL[:, NP - 1, :]), "vs")

                        @blk.tensor
                        def _(t):
                            t.wait_ge(sem["vs"], cnt["vs"])
                            inc(t.matmul(EXP_[:, :], CN["trils"][:, :], TOTB[:, :],
                                         start=True, stop=True), "ts")
                            inc(t.matmul(CNP[:, :], CN["ones128"][:, :], TOTB[:, :],
                                         start=True, stop=True), "ts")

                        @blk.vector
                        def _(v):
                            v.wait_ge(sem["ts"], cnt["ts"])
                            v.wait_ge(sem["vs"], cnt["vs"])
                            inc(v.tensor_copy(EXCL[:, :], EXP_[:, :]), "vs")
                            v.wait_ge(sem["vs"], cnt["vs"])
                            inc(v.tensor_tensor(SCR8[:, :, :], OHV[:, :, :],
                                                INCL[:, :, :], ALU.mult), "vs")
                            v.wait_ge(sem["vs"], cnt["vs"])
                            inc(v.reduce_sum(POS[:, :], SCR8[:, :, :], axis=AX.X), "vs")
                            ex_bc = EXCL[:, :].unsqueeze(1).broadcast_to([128, NP, 8])
                            v.wait_ge(sem["vs"], cnt["vs"])
                            inc(v.tensor_tensor(SCR8[:, :, :], OHV[:, :, :], ex_bc,
                                                ALU.mult), "vs")
                            v.wait_ge(sem["vs"], cnt["vs"])
                            inc(v.reduce_sum(SB_[:, :], SCR8[:, :, :], axis=AX.X), "vs")
                            v.wait_ge(sem["vs"], cnt["vs"])
                            inc(v.scalar_tensor_tensor(POS[:, :], POS[:, :], -1.0,
                                                       SB_[:, :], ALU.add, ALU.add), "vs")
                            v.wait_ge(sem["vs"], cnt["vs"])
                            inc(v.tensor_single_scalar(KEEP[:, :], POS[:, :],
                                                       float(CAP), ALU.is_lt), "vs")
                            v.wait_ge(sem["vs"], cnt["vs"])
                            inc(v.scalar_tensor_tensor(COEF[:, :], TP[:, :], 0.5,
                                                       KEEP[:, :], ALU.mult,
                                                       ALU.mult), "vs")
                            v.wait_ge(sem["vs"], cnt["vs"])
                            inc(v.tensor_scalar(SA[:, :], ET[:, :], CN["mye"][:, :],
                                                None, ALU.is_equal, ALU.bypass), "vs")
                            v.wait_ge(sem["vs"], cnt["vs"])
                            inc(v.tensor_tensor(SA[:, :], SA[:, :], KEEP[:, :],
                                                ALU.mult), "vs")
                            v.wait_ge(sem["vs"], cnt["vs"])
                            inc(v.tensor_scalar_add(SB_[:, :], POS[:, :],
                                                    -float(TRASH)), "vs")
                            v.wait_ge(sem["vs"], cnt["vs"])
                            inc(v.tensor_tensor(SB_[:, :], SB_[:, :], SA[:, :],
                                                ALU.mult), "vs")
                            v.wait_ge(sem["vs"], cnt["vs"])
                            inc(v.tensor_scalar_add(SB_[:, :], SB_[:, :],
                                                    float(TRASH)), "vs")
                            v.wait_ge(sem["vs"], cnt["vs"])
                            inc(v.tensor_tensor(SA[:, :], POS[:, :], KEEP[:, :],
                                                ALU.mult), "vs")
                            v.wait_ge(sem["vs"], cnt["vs"])
                            inc(v.tensor_tensor(
                                TMP16[:, :, :],
                                SA[:, :].unsqueeze(2).broadcast_to([128, NP, 16]),
                                CN["th16"][:, :].unsqueeze(1)
                                .broadcast_to([128, NP, 16]), ALU.is_ge), "vs")
                            v.wait_ge(sem["vs"], cnt["vs"])
                            inc(v.reduce_sum(RS[:, :], TMP16[:, :, :], axis=AX.X), "vs")
                            v.wait_ge(sem["vs"], cnt["vs"])
                            inc(v.scalar_tensor_tensor(RM[:, :], RS[:, :], -float(CW),
                                                       SA[:, :], ALU.mult, ALU.add), "vs")
                            v.wait_ge(sem["vs"], cnt["vs"])
                            inc(v.scalar_tensor_tensor(SA[:, :], ET[:, :], float(CAP),
                                                       RS[:, :], ALU.mult, ALU.add), "vs")
                            v.wait_ge(sem["vs"], cnt["vs"])
                            inc(v.scalar_tensor_tensor(SA[:, :], RM[:, :], 16.0,
                                                       SA[:, :], ALU.mult, ALU.add), "vs")

                        @blk.tensor
                        def _(t):
                            t.wait_ge(sem["vs"], cnt["vs"])
                            inc(t.matmul(OSP[:, :], CN["esel"][:, :], SA[:, :],
                                         start=True, stop=True), "ts")
                            inc(t.matmul(CTP[:, :], COEF[:, :], CN["esel"][:, :],
                                         start=True, stop=True), "ts")

                        @blk.vector
                        def _(v):
                            v.wait_ge(sem["ts"], cnt["ts"])
                            inc(v.tensor_copy(ownsl[:, :], OSP[:, :]), "vs")
                            inc(v.tensor_copy(coefT[:, :], CTP[:, :]), "vs")
                            inc(v.tensor_copy(CNTS[:, :], CNP[:, :]), "vs")
                            v.wait_ge(sem["vs"], cnt["vs"])
                            inc(v.tensor_tensor(CNTS[:, :], CNTS[:, :],
                                                CN["myoh"][:, :], ALU.mult), "vs")
                            v.wait_ge(sem["vs"], cnt["vs"])
                            inc(v.reduce_sum(CNTsc[:, :], CNTS[:, :], axis=AX.X), "vs")
                            inc(v.tensor_copy(
                                gidrep[:, :, :],
                                CN["gid128"][:, :].unsqueeze(2)
                                .broadcast_to([128, NP, 64])), "vs")
                            v.wait_ge(sem["vs"], cnt["vs"])
                            inc(v.tensor_copy(
                                SB2[:, :].rearrange("p (q d) -> p q d", q=16),
                                SB_[:, :].rearrange("p (d q) -> p q d", q=16)), "vs")
                            inc(v.tensor_copy(
                                OW2[:, :].rearrange("p (q d) -> p q d", q=16),
                                ownsl[:, :].rearrange("p (d q) -> p q d", q=16)), "vs")

                        @blk.gpsimd
                        def _(g):
                            g.wait_ge(sem["vs"], cnt["vs"])
                            inc(g.partition_broadcast(CNT1[:, :], CNTsc[:, :]), "gs")
                            dinc(g.dma_start(
                                out=dscat.rearrange("q (p d) -> p q d", p=128),
                                in_=SB2[:, :].rearrange("p (q d) -> p q d", q=16)), "dsc")
                            dinc(g.dma_start(
                                out=down.rearrange("q (p d) -> p q d", p=16),
                                in_=OW2[:, :].rearrange("p (q d) -> p q d", q=16)), "dsc")
                            g.wait_ge(sem["dsc"], cnt["dsc"])
                            for grp in range(8):
                                dinc(g.dma_start(out=idxsc[16 * grp:16 * grp + 16, :],
                                                 in_=dscat[:, :]), "dsc")
                                dinc(g.dma_start(out=idxown[16 * grp:16 * grp + 16, :],
                                                 in_=down[:, :]), "dsc")
                            g.wait_ge(sem["dsc"], cnt["dsc"])
                            g.wait_ge(sem["vs"], cnt["vs"])
                            sc_chunks = [(ch * 896, 896) for ch in range(18)]
                            sc_chunks.append((18 * 896, T - 18 * 896))
                            for base, ncnt in sc_chunks:
                                dinc(g.dma_scatter_add(
                                    tbl[:, :],
                                    gidrep[:, base // 128:(base + ncnt) // 128, :],
                                    idxsc[:, base // 16:(base + ncnt) // 16],
                                    ncnt, NREG[ncnt], 64), "dg")
                                g.wait_ge(sem["dg"], cnt["dg"])

        def emit_table_read():
            with ExitStack() as ph:
                TRD = ph.enter_context(nc.sbuf_tensor(U("TRD"), [16, CW, 64], F32))
                with nc.Block() as blk:
                    @blk.gpsimd
                    def _(g):
                        dinc(g.dma_start(
                            out=TRD[:, :, :],
                            in_=tbl[0:CAP, :].rearrange("(q f) z -> q f z", q=16)), "dg")
                        g.wait_ge(sem["dg"], cnt["dg"])

                    @blk.vector
                    def _(v):
                        v.wait_ge(sem["dg"], cnt["dg"])
                        v.wait_ge(sem["gs"], cnt["gs"])
                        inc(v.tensor_scalar(maskv[:, :], CN["rowiota"][:, :],
                                            CNT1[:, :], None, ALU.is_lt,
                                            ALU.bypass), "vs")
                        v.wait_ge(sem["vs"], cnt["vs"])
                        inc(v.select(selg[:, :], maskv[:, :], TRD[:, :, 0],
                                     CN["zeros16"][:, :], add_drain=True), "vs")
                        v.wait_ge(sem["vs"], cnt["vs"])
                        inc(v.tensor_copy(idg16[:, :], selg[:, :]), "vs")

                    @blk.gpsimd
                    def _(g):
                        g.wait_ge(sem["vs"], cnt["vs"])
                        for grp in range(8):
                            dinc(g.dma_start(out=idxg[16 * grp:16 * grp + 16, :],
                                             in_=idg16[:, :]), "dsc")
                        g.wait_ge(sem["dsc"], cnt["dsc"])

        def emit_ffn():
            with ExitStack() as ph:
                BUF = ph.enter_context(nc.sbuf_tensor(U("BUF"), [128, CAP // 128, E], BF16))
                HFM = ph.enter_context(nc.sbuf_tensor(U("HFM"), [128, 32, SCH], BF16))
                W1P = [ph.enter_context(nc.sbuf_tensor(U(f"W1P{i}"), [128, 8, 512], BF16))
                       for i in range(2)]
                W2P = [ph.enter_context(nc.sbuf_tensor(U(f"W2P{i}"), [128, 4, 512], BF16))
                       for i in range(2)]
                EOT = [ph.enter_context(nc.sbuf_tensor(U(f"EOT{i}"), [128, 512], F32))
                       for i in range(2)]
                REG.update(buf0=(BUF[:, 0:2, :], [128, 2, E], BF16))

                with nc.Block() as blk:
                    @blk.gpsimd
                    def _(g):
                        for base, ncnt in ((0, 1024), (1024, 1024), (2048, 512)):
                            dinc(g.dma_gather(
                                BUF[:, base // 128:(base + ncnt) // 128, :],
                                ln_all[:, :], idxg[:, base // 16:(base + ncnt) // 16],
                                ncnt, NREG[ncnt], E, transpose=False), "dg")
                            g.wait_ge(sem["dg"], cnt["dg"])

                ts0, ss0, vs0 = cnt["ts"], cnt["ss"], cnt["vs"]
                import os
                if os.environ.get("KDBG"):
                    print("FFN bases: ts0", ts0, "ss0", ss0, "vs0", vs0,
                          "w1s", [cnt["w1s0"], cnt["w1s1"]],
                          "w2s", [cnt["w2s0"], cnt["w2s1"]],
                          "dg", cnt["dg"], "dw", cnt["dw"])
                w1s_base = [cnt["w1s0"] // 16, cnt["w1s1"] // 16]
                w2s_base = [cnt["w2s0"] // 16, cnt["w2s1"] // 16]
                eot_base = [cnt[f"eots{i}"] // 16 for i in range(2)]

                def ts_h(sc_, pc):
                    return ts0 + sc_ * 544 + 32 + (pc + 1) * 16

                def ts_h_half(sc_, fc):
                    return ts0 + sc_ * 544 + 32 + fc * 8 + 8

                def ts_eo(sc_, dmh, fc):
                    return ts0 + sc_ * 544 + 288 + dmh * 128 + (fc + 1) * 4

                def ss_relu(sc_, fc):
                    return ss0 + sc_ * 32 + fc + 1

                def vs_ev(sc_, k):
                    return vs0 + sc_ * 40 + 32 + k + 1

                with ExitStack() as php:
                    HP = [php.enter_context(nc.psum_tensor(U(f"HP{i}"), [128, SCH], F32))
                          for i in range(2)]
                    EP = [php.enter_context(nc.psum_tensor(U(f"EPp{i}"), [128, 512], F32))
                          for i in range(4)]
                    TPB2 = [php.enter_context(nc.psum_tensor(U(f"TPB2{i}"), [128, 8, 128],
                                                             BF16))
                            for i in range(2)]
                    with nc.Block() as blk:
                        @blk.sync
                        def _(sy):
                            sy.wait_ge(sem["dw"], wconv_done)
                            for sc_ in range(NSC):
                                for b in range(8):
                                    gb = sc_ * 8 + b
                                    if gb >= 2:
                                        psc, pb = divmod(gb - 2, 8)
                                        sy.wait_ge(sem["ts"], ts_h(psc, 2 * pb + 1))
                                    dinc(sy.dma_start(
                                        out=W1P[gb % 2][:, :, :],
                                        in_=w1pk[b, :, :, :]),
                                        f"w1s{gb % 2}")
                                for dmh in range(2):
                                    for bk in range(8):
                                        gj = sc_ * 16 + dmh * 8 + bk
                                        if gj >= 2:
                                            psc, r = divmod(gj - 2, 16)
                                            pdmh, pbk = divmod(r, 8)
                                            sy.wait_ge(sem["ts"],
                                                       ts_eo(psc, pdmh,
                                                             pbk * 4 + 3))
                                        dinc(sy.dma_start(
                                            out=W2P[gj % 2][:, :, :],
                                            in_=w2pk[dmh, bk, :, :, :]),
                                            f"w2s{gj % 2}")

                        @blk.tensor
                        def _(t):
                            t.wait_ge(sem["dg"], cnt["dg"])
                            for sc_ in range(NSC):
                                for i_, tj in enumerate(TWALK):
                                    st, dmc = divmod(tj, 8)
                                    if i_ >= 2:
                                        t.wait_ge(sem["vs"],
                                                  vs0 + sc_ * 40 + i_ - 1)
                                    inc(t.transpose(
                                        TPB2[i_ % 2][:, 0, :],
                                        BUF[:, 4 * sc_ + st,
                                            dmc * 128:(dmc + 1) * 128],
                                        CN["ident_bf"][:, :]), "ts")
                                t.wait_ge(sem["vs"], vs0 + sc_ * 40 + 32)
                                for pc in range(NW1P):
                                    gb = sc_ * 8 + pc // 2
                                    t.wait_ge(sem[f"w1s{gb % 2}"],
                                              16 * (gb // 2 + 1 + w1s_base[gb % 2]))
                                    for f1 in range(2):
                                        fc = pc * 2 + f1
                                        gfc = sc_ * 32 + fc
                                        if gfc >= 2:
                                            psc, pfc = divmod(gfc - 2, 32)
                                            t.wait_ge(sem["ss"], ss_relu(psc, pfc))
                                        off = (pc % 2) * 256 + f1 * 128
                                        for k in range(8):
                                            inc(t.matmul(
                                                HP[fc % 2][:, :],
                                                W1P[gb % 2][:, k, off:off + 128],
                                                BUF[:, 4 * sc_ + k // 2,
                                                    (k % 2) * 512:
                                                    (k % 2) * 512 + 512],
                                                start=(k == 0), stop=(k == 7)), "ts")
                                for dmh in range(2):
                                    for fc in range(32):
                                        gj = sc_ * 16 + dmh * 8 + fc // 4
                                        if fc % 4 == 0:
                                            t.wait_ge(sem[f"w2s{gj % 2}"],
                                                      16 * (gj // 2 + 1
                                                            + w2s_base[gj % 2]))
                                        t.wait_ge(sem["ss"], ss_relu(sc_, fc))
                                        if fc == 0:
                                            if dmh == 1:
                                                t.wait_ge(sem["vs"], vs_ev(sc_, 3))
                                            elif sc_ >= 1:
                                                t.wait_ge(sem["vs"], vs_ev(sc_ - 1, 7))
                                        for st in range(4):
                                            inc(t.matmul(
                                                EP[st][:, :],
                                                HFM[:, fc, st * 128:(st + 1) * 128],
                                                W2P[gj % 2][:, fc % 4, :],
                                                start=(fc == 0), stop=(fc == 31)), "ts")

                        @blk.scalar
                        def _(s):
                            for sc_ in range(NSC):
                                for fc in range(32):
                                    s.wait_ge(sem["ts"], ts_h_half(sc_, fc))
                                    if sc_ >= 1:
                                        s.wait_ge(sem["ts"], ts_eo(sc_ - 1, 1, 31))
                                    inc(s.activation(HFM[:, fc, :], HP[fc % 2][:, :],
                                                     AF.Relu,
                                                     bias=CN["b1c"][:, fc:fc + 1]), "ss")

                        @blk.vector
                        def _(v):
                            for sc_ in range(NSC):
                                for i_, tj in enumerate(TWALK):
                                    tgt = 4 * (tj % 8) + tj // 8
                                    v.wait_ge(sem["ts"],
                                              ts0 + sc_ * 544 + min(i_ + 2, 32))
                                    inc(v.tensor_copy(
                                        BUF[:, 4 * sc_ + tgt // 8,
                                            (tgt % 8) * 128:(tgt % 8) * 128 + 128],
                                        TPB2[i_ % 2][:, 0, :]), "vs")
                                for k in range(8):
                                    dmh, st = divmod(k, 4)
                                    v.wait_ge(sem["ts"], ts_eo(sc_, dmh, 31))
                                    gk = sc_ * 8 + k
                                    if gk >= 2:
                                        v.wait_ge(sem[f"eots{gk % 2}"],
                                                  16 * ((gk - 2) // 2 + 1
                                                        + eot_base[gk % 2]))
                                    inc(v.tensor_add(
                                        EOT[gk % 2][:, :], EP[st][:, :],
                                        CN["b2r"][:, dmh * 512:(dmh + 1) * 512]), "vs")

                        @blk.gpsimd
                        def _(g):
                            for sc_ in range(NSC):
                                for k in range(8):
                                    dmh, st = divmod(k, 4)
                                    gk = sc_ * 8 + k
                                    g.wait_ge(sem["vs"], vs_ev(sc_, k))
                                    dinc(g.dma_start(
                                        out=eo_in[sc_ * SCH + st * 128:
                                                  sc_ * SCH + (st + 1) * 128,
                                                  dmh * 512:(dmh + 1) * 512],
                                        in_=EOT[gk % 2][:, :]), f"eots{gk % 2}")
                with nc.Block() as blk:
                    @blk.gpsimd
                    def _(g):
                        for i4 in range(2):
                            g.wait_ge(sem[f"eots{i4}"], cnt[f"eots{i4}"])
                        inc(g.collective_compute("AllGather", ALU.bypass,
                                                 replica_groups=[list(range(NCORE))],
                                                 ins=[eo_in[:, :].opt()],
                                                 outs=[eo_all[:, :].opt()]), "cc")

        def emit_gather_back():
            with ExitStack() as ph:
                GATH = ph.enter_context(nc.sbuf_tensor(U("GATH"), [128, NT, E], F32))
                REG.update(gath0=(GATH[:, 0:2, :], [128, 2, E], F32))
                with nc.Block() as blk:
                    @blk.gpsimd
                    def _(g):
                        g.wait_ge(sem["cc"], cnt["cc"])
                        for ch in range(2):
                            dinc(g.dma_gather(GATH[:, 8 * ch:8 * ch + 8, :],
                                              eo_all[:, :],
                                              idxown[:, 64 * ch:64 * ch + 64],
                                              1024, NREG[1024], E, transpose=False), "dg")
                            g.wait_ge(sem["dg"], cnt["dg"])

                    @blk.vector
                    def _(v):
                        v.wait_ge(sem["dg"], cnt["dg"])
                        v.wait_ge(sem["vs"], cnt["vs"])
                        for i in range(NT):
                            inc(v.scalar_tensor_tensor(RES[:, i, :], GATH[:, i, :],
                                                       coefT[:, i:i + 1], RES[:, i, :],
                                                       ALU.mult, ALU.add), "vs")

        def moe_sublayer(zero_tbl=False):
            if zero_tbl:
                with nc.Block() as blk:
                    @blk.gpsimd
                    def _(g):
                        for zk in range(TBL_ROWS // 256):
                            dinc(g.dma_start(out=tbl[zk * 256:(zk + 1) * 256, :],
                                             in_=CN["zcol"][:, :]), "dw")
                        g.wait_ge(sem["dw"], cnt["dw"])
            emit_ln_gate()
            emit_ags()
            emit_routing_scatter()
            if upto >= 4:
                emit_table_read()
            if upto >= 5:
                emit_ffn()
                emit_gather_back()
            fence()

        def emit_attention():
            import os
            _au = int(os.environ.get("ATTN_UPTO", "9"))
            with ExitStack() as att:
                LN2FM = att.enter_context(nc.sbuf_tensor(U("LN2FM"), [128, 8, TOK], BF16))
                VT = att.enter_context(nc.sbuf_tensor(U("VT"), [128, NT, 1040], BF16))
                REG.update(ln2fm0=(LN2FM[:, :, 0:256], [128, 8, 256], BF16),
                           vt0=(VT[:, 0, :], [128, 1040], BF16))

                t0, v0, s0 = cnt["ts"], cnt["vs"], cnt["ss"]
                with ExitStack() as ph:
                    TPB = [ph.enter_context(nc.psum_tensor(U(f"TPB{i}"), [128, 8, 128],
                                                           BF16))
                           for i in range(2)]
                    with nc.Block() as blk:
                        @blk.scalar
                        def _(s_):
                            for i in range(NT):
                                if i >= 2:
                                    s_.wait_ge(sem["vs"], v0 + 1 + 10 * (i - 2) + 2)
                                inc(s_.activation(SQT[i % 2][:, :], RES[:, i, :],
                                                  AF.Square), "ss")
                                s_.wait_ge(sem["vs"], v0 + 1 + 10 * i + 6)
                                inc(s_.activation(STAT[:, i, 6:7], STAT[:, i, 5:6],
                                                  AF.Sqrt), "ss")

                        @blk.vector
                        def _(v):
                            inc(v.memset(VT[:, :, :]
                                         .rearrange("p a (h z) -> p a h z", z=65)
                                         [:, :, :, 64:65], 1.0), "vs")
                            for i in range(NT):
                                inc(v.reduce_sum(STAT[:, i, 0:1], RES[:, i, :],
                                                 axis=AX.X), "vs")
                                v.wait_ge(sem["ss"], s0 + 2 * i + 1)
                                inc(v.reduce_sum(STAT[:, i, 1:2], SQT[i % 2][:, :],
                                                 axis=AX.X), "vs")
                                v.wait_ge(sem["vs"], cnt["vs"])
                                inc(v.tensor_scalar_mul(STAT[:, i, 2:3],
                                                        STAT[:, i, 0:1], 1.0 / E), "vs")
                                v.wait_ge(sem["vs"], cnt["vs"])
                                inc(v.tensor_mul(STAT[:, i, 3:4], STAT[:, i, 2:3],
                                                 STAT[:, i, 2:3]), "vs")
                                v.wait_ge(sem["vs"], cnt["vs"])
                                inc(v.scalar_tensor_tensor(STAT[:, i, 4:5],
                                                           STAT[:, i, 1:2], 1.0 / E,
                                                           STAT[:, i, 3:4],
                                                           ALU.mult, ALU.subtract), "vs")
                                v.wait_ge(sem["vs"], cnt["vs"])
                                inc(v.tensor_scalar_add(STAT[:, i, 5:6],
                                                        STAT[:, i, 4:5], EPS), "vs")
                                v.wait_ge(sem["ss"], s0 + 2 * i + 2)
                                inc(v.reciprocal(STAT[:, i, 7:8], STAT[:, i, 6:7]), "vs")
                                v.wait_ge(sem["vs"], cnt["vs"])
                                inc(v.scalar_tensor_tensor(STAT[:, i, 3:4],
                                                           STAT[:, i, 2:3], -1.0,
                                                           STAT[:, i, 7:8],
                                                           ALU.mult, ALU.mult), "vs")
                                v.wait_ge(sem["vs"], cnt["vs"])
                                if i >= 2:
                                    v.wait_ge(sem["ts"], t0 + 8 * (i - 1))
                                inc(v.tensor_scalar(LNB[i % 2][:, :], RES[:, i, :],
                                                    STAT[:, i, 7:8], STAT[:, i, 3:4],
                                                    ALU.mult, ALU.add), "vs")
                                v.wait_ge(sem["ts"], t0 + 8 * (i + 1))
                                inc(v.tensor_copy(LN2FM[:, :, i * 128:(i + 1) * 128],
                                                  TPB[i % 2][:, :, :]), "vs")

                        @blk.tensor
                        def _(t):
                            for i in range(NT):
                                t.wait_ge(sem["vs"], v0 + 1 + 10 * i + 9)
                                if i >= 2:
                                    t.wait_ge(sem["vs"], v0 + 1 + 10 * (i - 2) + 10)
                                for k in range(8):
                                    inc(t.transpose(TPB[i % 2][:, k, :],
                                                    LNB[i % 2][:, k * 128:(k + 1) * 128],
                                                    CN["ident_bf"][:, :]), "ts")
                fence()
                if _au < 2:
                    return

                t0v, v0v, dw0v = cnt["ts"], cnt["vs"], cnt["dwa"]
                with ExitStack() as ph:
                    WVP = ph.enter_context(nc.sbuf_tensor(U("WVP"), [128, 8, 512], BF16))
                    VP = [ph.enter_context(nc.psum_tensor(U(f"VP{i}"), [128, 512], F32))
                          for i in range(2)]
                    with nc.Block() as blk:
                        @blk.gpsimd
                        def _(g):
                            for grp in range(2):
                                if grp >= 1:
                                    g.wait_ge(sem["ts"], t0v + NT * 8)
                                dinc(g.dma_start(out=WVP[:, :, :],
                                                 in_=wv_v[grp, :, :, :]), "dwa")

                        @blk.tensor
                        def _(t):
                            for grp in range(2):
                                t.wait_ge(sem["dwa"], dw0v + 16 * (grp + 1))
                                for i in range(NT):
                                    gi = grp * NT + i
                                    if gi >= 2:
                                        t.wait_ge(sem["vs"], v0v + gi - 1)
                                    for k in range(8):
                                        inc(t.matmul(VP[gi % 2][:, :],
                                                     LN2FM[:, k, i * 128:(i + 1) * 128],
                                                     WVP[:, k, :],
                                                     start=(k == 0), stop=(k == 7)),
                                            "ts")

                        @blk.vector
                        def _(v):
                            for grp in range(2):
                                for i in range(NT):
                                    gi = grp * NT + i
                                    v.wait_ge(sem["ts"], t0v + (gi + 1) * 8)
                                    out = VT[:, i, :].rearrange("p (h z) -> p h z", z=65)
                                    inc(v.tensor_copy(
                                        out[:, 8 * grp:8 * grp + 8, 0:64],
                                        VP[gi % 2][:, :]
                                        .rearrange("p (h d) -> p h d", d=64)), "vs")
                fence()
                if _au < 3:
                    return

                with ExitStack() as ph:
                    QF = ph.enter_context(nc.sbuf_tensor(U("QF"), [128, TOK], BF16))
                    KF = ph.enter_context(nc.sbuf_tensor(U("KF"), [128, TOK], BF16))
                    PT = [ph.enter_context(nc.sbuf_tensor(U(f"PT{i}"), [128, 512], BF16))
                          for i in range(2)]
                    RCP = ph.enter_context(nc.sbuf_tensor(U("RCP"), [128, 512], F32))
                    RCPB = ph.enter_context(nc.sbuf_tensor(U("RCPB"), [128, 512], F32))
                    RA = ph.enter_context(nc.sbuf_tensor(U("RA"), [1, 512], F32))
                    RB = ph.enter_context(nc.sbuf_tensor(U("RB"), [1, 512], F32))
                    WQP = [ph.enter_context(nc.sbuf_tensor(U(f"WQP{i}"), [128, 8, 128],
                                                           BF16)) for i in range(2)]
                    WKP = [ph.enter_context(nc.sbuf_tensor(U(f"WKP{i}"), [128, 8, 128],
                                                           BF16)) for i in range(2)]
                    ONB = [ph.enter_context(nc.sbuf_tensor(U(f"ONB{i}"), [128, 512],
                                                           BF16)) for i in range(2)]
                    QP = [ph.enter_context(nc.psum_tensor(U(f"QPp{i}"), [128, 512], F32))
                          for i in range(2)]
                    SP = [ph.enter_context(nc.psum_tensor(U(f"SPp{i}"), [128, 512], F32))
                          for i in range(2)]
                    OP1 = ph.enter_context(nc.psum_tensor(U("OP1"), [128, 512], F32))
                    OP2 = ph.enter_context(nc.psum_tensor(U("OP2"), [128, 512], F32))
                    LP = ph.enter_context(nc.psum_tensor(U("LP"), [128, 512], F32))
                    ta0, sa0, va0, ga0 = (cnt["ts"], cnt["ss"], cnt["vs"],
                                          cnt["gs"])
                    wqs_base = [cnt["wqs0"] // 16, cnt["wqs1"] // 16]
                    ons_base = [cnt["ons0"] // 16, cnt["ons1"] // 16]
                    TPP, SPP, VPP, GPP = 264, 84, 52, 8

                    def t_pair(pr):
                        return ta0 + pr * TPP

                    def s_pair(pr):
                        return sa0 + pr * SPP

                    def v_pair(pr):
                        return va0 + pr * VPP

                    def g_pair(pr):
                        return ga0 + pr * GPP

                    npairs = 8 if _au >= 4 else min(_au - 2, 8)
                    for pr in range(npairs):
                        with nc.Block() as blk:
                            @blk.gpsimd
                            def _(g, pr=pr):
                                if pr >= 2:
                                    g.wait_ge(sem["ts"], t_pair(pr - 1))
                                dinc(g.dma_start(out=WQP[pr % 2][:, :, :],
                                                 in_=wq_v[pr, :, :, :]), f"wqs{pr % 2}")
                                dinc(g.dma_start(out=WKP[pr % 2][:, :, :],
                                                 in_=wk_v[pr, :, :, :]), f"wqs{pr % 2}")

                            @blk.tensor
                            def _(t, pr=pr):
                                t.wait_ge(sem[f"wqs{pr % 2}"],
                                          16 * (2 * (pr // 2 + 1) + wqs_base[pr % 2]))
                                for tc in range(4):
                                    if tc >= 2:
                                        t.wait_ge(sem["ss"], s_pair(pr) + (tc - 2) + 1)
                                    for k in range(8):
                                        inc(t.matmul(
                                            QP[tc % 2][:, :], WQP[pr % 2][:, k, :],
                                            LN2FM[:, k, tc * 512:(tc + 1) * 512],
                                            start=(k == 0), stop=(k == 7)), "ts")
                                for tc in range(4):
                                    if tc < 2:
                                        t.wait_ge(sem["ss"], s_pair(pr) + tc + 3)
                                    else:
                                        t.wait_ge(sem["vs"], v_pair(pr) + (tc - 2) + 1)
                                    for k in range(8):
                                        inc(t.matmul(
                                            QP[tc % 2][:, :], WKP[pr % 2][:, k, :],
                                            LN2FM[:, k, tc * 512:(tc + 1) * 512],
                                            start=(k == 0), stop=(k == 7)), "ts")

                            @blk.scalar
                            def _(s, pr=pr):
                                for tc in range(4):
                                    s.wait_ge(sem["ts"], t_pair(pr) + (tc + 1) * 8)
                                    inc(s.activation(QF[:, tc * 512:(tc + 1) * 512],
                                                     QP[tc % 2][:, :], AF.Copy,
                                                     scale=0.125), "ss")

                            @blk.vector
                            def _(v, pr=pr):
                                for tc in range(4):
                                    v.wait_ge(sem["ts"], t_pair(pr) + 32 + (tc + 1) * 8)
                                    inc(v.tensor_copy(KF[:, tc * 512:(tc + 1) * 512],
                                                      QP[tc % 2][:, :]), "vs")

                        h1, h2 = 2 * pr, 2 * pr + 1
                        with nc.Block() as blk:
                            @blk.tensor
                            def _(t, pr=pr, h1=h1, h2=h2):
                                tb = t_pair(pr) + 64
                                sB = s_pair(pr) + 4
                                vB = v_pair(pr) + 4
                                t.wait_ge(sem["ss"], sB)
                                t.wait_ge(sem["vs"], vB)
                                m = 0
                                for qc in range(4):
                                    d = 0
                                    for kt in range(NKT[qc]):
                                        if m >= 1:
                                            t.wait_ge(sem["ss"], sB + 2 * m)
                                        inc(t.matmul(
                                            SP[0][:, :],
                                            KF[0:64, kt * 128:(kt + 1) * 128],
                                            QF[0:64, qc * 512:(qc + 1) * 512],
                                            start=True, stop=True), "ts")
                                        inc(t.matmul(
                                            SP[1][:, :],
                                            KF[64:128, kt * 128:(kt + 1) * 128],
                                            QF[64:128, qc * 512:(qc + 1) * 512],
                                            start=True, stop=True,
                                            tile_position=(64, 0)), "ts")
                                        t.wait_ge(sem["ss"], sB + 2 * (m + 1))
                                        if kt >= 4 * qc:
                                            d += 1
                                            t.wait_ge(sem["vs"], vB + qc * 12 + 2 * d)
                                        if kt == 0:
                                            t.wait_ge(sem["vs"], vB + qc * 12)
                                        inc(t.matmul(
                                            OP1[0:65, :],
                                            VT[:, kt, h1 * 65:h1 * 65 + 65],
                                            PT[0][:, :], start=(kt == 0),
                                            stop=(kt == NKT[qc] - 1)), "ts")
                                        inc(t.matmul(
                                            OP2[64:128, :],
                                            VT[:, kt, h2 * 65:h2 * 65 + 64],
                                            PT[1][:, :], start=(kt == 0),
                                            stop=(kt == NKT[qc] - 1),
                                            tile_position=(0, 64)), "ts")
                                        inc(t.matmul(
                                            LP[0:1, :],
                                            VT[:, kt, h1 * 65 + 64:h1 * 65 + 65],
                                            PT[1][:, :], start=(kt == 0),
                                            stop=(kt == NKT[qc] - 1)), "ts")
                                        m += 1

                            @blk.scalar
                            def _(s, pr=pr):
                                tb = t_pair(pr) + 64
                                m = 0
                                for qc in range(4):
                                    for kt in range(NKT[qc]):
                                        s.wait_ge(sem["ts"], tb + 5 * m + 2)
                                        inc(s.activation(PT[0][:, :], SP[0][:, :],
                                                         AF.Exp), "ss")
                                        inc(s.activation(PT[1][:, :], SP[1][:, :],
                                                         AF.Exp), "ss")
                                        m += 1

                            @blk.vector
                            def _(v, pr=pr):
                                tb = t_pair(pr) + 64
                                sB = s_pair(pr) + 4
                                gB = g_pair(pr)
                                m = 0
                                for qc in range(4):
                                    for kt in range(NKT[qc]):
                                        if kt >= 4 * qc:
                                            v.wait_ge(sem["ss"], sB + 2 * (m + 1))
                                            om = kt - 4 * qc
                                            inc(v.tensor_tensor(PT[0][:, :],
                                                                PT[0][:, :],
                                                                CN["cmask"][:, om, :],
                                                                ALU.mult), "vs")
                                            inc(v.tensor_tensor(PT[1][:, :],
                                                                PT[1][:, :],
                                                                CN["cmask"][:, om, :],
                                                                ALU.mult), "vs")
                                        m += 1
                                    v.wait_ge(sem["ts"], tb + 5 * CUMKT[qc])
                                    inc(v.reciprocal(RA[0:1, :], OP1[64:65, :]), "vs")
                                    inc(v.reciprocal(RB[0:1, :], LP[0:1, :]), "vs")
                                    v.wait_ge(sem["gs"], gB + 2 * (qc + 1))
                                    n_on = pr * 4 + qc
                                    j_on = n_on % 2
                                    if n_on >= 2:
                                        v.wait_ge(sem[f"ons{j_on}"],
                                                  16 * ((n_on - 2) // 2 + 1
                                                        + ons_base[j_on]))
                                    inc(v.tensor_tensor(
                                        ONB[j_on][0:64, :],
                                        OP1[0:64, :], RCP[0:64, :], ALU.mult), "vs")
                                    inc(v.tensor_tensor(
                                        ONB[j_on][64:128, :],
                                        OP2[64:128, :], RCPB[64:128, :], ALU.mult), "vs")

                            @blk.gpsimd
                            def _(g, pr=pr):
                                vB = v_pair(pr) + 4
                                for qc in range(4):
                                    g.wait_ge(sem["vs"], vB + qc * 12 + 10)
                                    inc(g.partition_broadcast(RCP[0:64, :],
                                                              RA[0:1, :]), "gs")
                                    inc(g.partition_broadcast(RCPB[:, :],
                                                              RB[0:1, :]), "gs")
                                    n_on = pr * 4 + qc
                                    g.wait_ge(sem["vs"], vB + qc * 12 + 12)
                                    dinc(g.dma_start(
                                        out=onorm_d[pr, :,
                                                    qc * 512:(qc + 1) * 512],
                                        in_=ONB[n_on % 2][:, :]),
                                        f"ons{n_on % 2}")
                fence()
                if _au < 4:
                    return

                tp0, vp0, dwp0 = cnt["ts"], cnt["vs"], cnt["dwa"]
                with ExitStack() as ph:
                    WOP = ph.enter_context(nc.sbuf_tensor(U("WOP"), [128, 8, E], BF16))
                    OIN = [ph.enter_context(nc.sbuf_tensor(U(f"OIN{i}"), [128, 8, 128],
                                                           BF16)) for i in range(2)]
                    APP = [ph.enter_context(nc.psum_tensor(U(f"APP{i}"), [128, 512], F32))
                           for i in range(2)]
                    oin_base = [cnt["oin0"] // 16, cnt["oin1"] // 16]
                    with nc.Block() as blk:
                        @blk.gpsimd
                        def _(g):
                            dinc(g.dma_start(out=WOP[:, :, :], in_=wo_v[:, :, :]), "dwa")
                            for qt in range(NT):
                                if qt >= 2:
                                    g.wait_ge(sem["ts"], tp0 + 16 * (qt - 1))
                                dinc(g.dma_start(
                                    out=OIN[qt % 2][:, :, :],
                                    in_=onorm_d[:, :, qt * 128:(qt + 1) * 128]
                                    .rearrange("c p t -> p c t")), f"oin{qt % 2}")

                        @blk.tensor
                        def _(t):
                            t.wait_ge(sem["dwa"], dwp0 + 16)
                            n = 0
                            for qt in range(NT):
                                t.wait_ge(sem[f"oin{qt % 2}"],
                                          16 * (qt // 2 + 1 + oin_base[qt % 2]))
                                for eh in range(2):
                                    if n >= 2:
                                        t.wait_ge(sem["vs"], vp0 + n - 1)
                                    for c in range(8):
                                        inc(t.matmul(
                                            APP[n % 2][:, :],
                                            OIN[qt % 2][:, c, :],
                                            WOP[:, c, eh * 512:(eh + 1) * 512],
                                            start=(c == 0), stop=(c == 7)), "ts")
                                    n += 1

                        @blk.vector
                        def _(v):
                            n = 0
                            for qt in range(NT):
                                for eh in range(2):
                                    v.wait_ge(sem["ts"], tp0 + (n + 1) * 8)
                                    inc(v.tensor_add(
                                        RES[:, qt, eh * 512:(eh + 1) * 512],
                                        APP[n % 2][:, :],
                                        RES[:, qt, eh * 512:(eh + 1) * 512]), "vs")
                                    n += 1
                fence()

        if upto >= 3:
            moe_sublayer()
        elif upto >= 2:
            emit_ln_gate()
            emit_ags()
            fence()
        if upto >= 6:
            emit_attention()
        if upto >= 7:
            moe_sublayer(zero_tbl=True)

        with nc.Block() as blk:
            @blk.sync
            def _(sy):
                for k in ("ts", "vs", "ss", "gs", "dsc", "dg", "dat", "cc"):
                    if cnt[k]:
                        sy.wait_ge(sem[k], cnt[k])
                sy.wait_ge(sem["dx"], dx_done)
                dinc(sy.dma_start(out=y_tm, in_=RES[:, :, :]), "dout")
                sy.wait_ge(sem["dout"], cnt["dout"])

        dump_outs = []
        for nm in dumps:
            ap, shape, dt = REG[nm]
            od = nc.dram_tensor("dump_" + nm, shape, dt, kind="ExternalOutput").ap()
            dump_outs.append((od, ap))
        if dump_outs:
            with nc.Block() as blk:
                @blk.sync
                def _(sy):
                    for od, ap in dump_outs:
                        dinc(sy.dma_start(out=od[...], in_=ap), "dout")
                    sy.wait_ge(sem["dout"], cnt["dout"])

    mybir.codegen_inst_isa_subclasses(nc)
    return nc


def kernel(**inputs):
    per_core = _host_pack(inputs)
    nc = build_program(upto=7)
    res = run_bass_kernel_spmd(nc, per_core, core_ids=list(range(NCORE)))
    out = np.stack([np.asarray(res.results[i]["y"]).reshape(S, E) for i in range(NCORE)])
    return out.astype(np.float32)
